# version 09
# baseline (speedup 1.0000x reference)
import numpy as np
from contextlib import ExitStack

import concourse.bass as bass
import concourse.mybir as mybir
from concourse.bass_types import AP
from concourse.bass_utils import run_bass_kernel_spmd

F32 = mybir.dt.float32
I32 = mybir.dt.int32
BF16 = mybir.dt.bfloat16
AF = mybir.ActivationFunctionType
ALU = mybir.AluOpType
AX = mybir.AxisListType

B, S, D = 4, 8192, 1024
H, DK, DV, LSEG = 16, 64, 64, 512
TC = 4096
TT = TC + 2
NSEG = TC // LSEG
NCH = D // 128
DFF0, NCH0 = 2816, 22
DFF1, NCH1 = 4096, 32
SC0 = 1.0 / 8.0
NEG = -1.0e30

_DEBUG_TAPS = ()
_PHASES = 99
_P2N = 128
_P2MODE = 'full'
_P3N = 512
_P3MODE = 'full'
_P3TAIL = True
_P1M = 8
_P4MODE = 'full'
_DUMPS = ()
_GHOSTS = True


def _ttiles(n, step, start=0):
    out = []
    t = start
    while t < n:
        out.append((t, min(step, n - t)))
        t += step
    return out


def build_nc(taps=()):
    nc = bass.Bass("TRN2", target_bir_lowering=False)
    T, V, A, G, Sy = nc.tensor, nc.vector, nc.scalar, nc.gpsimd, nc.sync

    xf = nc.dram_tensor("xf", [128, NCH, TT], F32, kind="ExternalInput").ap()
    xpre = nc.dram_tensor("xpre", [128, NCH, TC], F32, kind="ExternalInput").ap()
    flags = nc.dram_tensor("flags", [1, 4], F32, kind="ExternalInput").ap()
    wi = {}
    for nm, shp in [
        ("L0_attn_Wq", [D, D]), ("L0_attn_Wk", [D, D]), ("L0_attn_Wv", [D, D]),
        ("L0_attn_Wo", [D, D]), ("L0_attn_betas", [H, DV]),
        ("L0_attn_memory0", [H, DK, DV]), ("L0_attn_z0", [H, DK]),
        ("L0_ffn_w_gate", [DFF0, D]), ("L0_ffn_w_up", [DFF0, D]),
        ("L0_ffn_w_down", [D, DFF0]),
        ("L1_attn_Wq", [D, D]), ("L1_attn_bq", [1, D]),
        ("L1_attn_Wk", [D, D]), ("L1_attn_bk", [1, D]),
        ("L1_attn_Wv", [D, D]), ("L1_attn_bv", [1, D]),
        ("L1_attn_weights", [H, DK, DK]),
        ("L1_ffn_w1", [DFF1, D]), ("L1_ffn_conv_w", [DFF1, 3]),
        ("L1_ffn_conv_b", [1, DFF1]), ("L1_ffn_w2", [D, DFF1]),
    ]:
        wi[nm] = nc.dram_tensor(nm, shp, F32, kind="ExternalInput").ap()
    y = nc.dram_tensor("y", [128, NCH, TC], F32, kind="ExternalOutput").ap()
    dbg = {}
    for t_ in taps:
        dbg[t_] = nc.dram_tensor("dbg_" + t_, [128, NCH, TT], F32,
                                 kind="ExternalOutput").ap()
    dmp = {}
    for t_ in _DUMPS:
        dmp[t_] = nc.dram_tensor("dmp_" + t_, [128, 16256], F32,
                                 kind="ExternalOutput").ap()

    qdr = nc.dram_tensor("qdr", [128, NCH, TT], F32).ap()
    kdr = nc.dram_tensor("kdr", [128, NCH, TT], F32).ap()
    vdr = nc.dram_tensor("vdr", [128, NCH, TT], F32).ap()
    kpre = nc.dram_tensor("kpre", [128, NCH, TC], F32).ap()
    vpre = nc.dram_tensor("vpre", [128, NCH, TC], F32).ap()
    wgT = nc.dram_tensor("wgT", [128, NCH, DFF0], F32).ap()
    wuT = nc.dram_tensor("wuT", [128, NCH, DFF0], F32).ap()
    wdT = nc.dram_tensor("wdT", [128, NCH0, D], F32).ap()
    w1T = nc.dram_tensor("w1T", [128, NCH, DFF1], F32).ap()
    w2T = nc.dram_tensor("w2T", [128, NCH1, D], F32).ap()
    wqT1 = nc.dram_tensor("wqT1", [128, NCH, D], F32).ap()
    wkT1 = nc.dram_tensor("wkT1", [128, NCH, D], F32).ap()
    wvT1 = nc.dram_tensor("wvT1", [128, NCH, D], F32).ap()

    with ExitStack() as st:
        def sbt(name, shape, dt=F32):
            return st.enter_context(nc.sbuf_tensor(name, shape, dt))

        xs = sbt("xs", [128, NCH, TT])
        AR = 16256
        ar = sbt("arena", [128, AR])
        wb = sbt("wb", [128, NCH * 128])
        wnat = sbt("wnat", [128, 512])
        ident = sbt("ident", [128, 128])
        ones = sbt("ones", [128, 128])
        biasq = sbt("biasq", [128, NCH])
        biask = sbt("biask", [128, NCH])
        biasv = sbt("biasv", [128, NCH])
        convw = sbt("convw", [128, NCH1, 3])
        convb = sbt("convb", [128, NCH1])
        gates = sbt("gates", [128, H])
        phiw = sbt("phiw", [128, NCH * DK])
        stats = sbt("stats", [128, 64])
        MS = 68
        mst = sbt("mst", [128, H * MS])
        pb = [st.enter_context(nc.psum_tensor("pb%d" % i, [128, 512], F32))
              for i in range(8)]

        xsf = xs[:, :, :].rearrange("p a b -> p (a b)")
        iicol = wnat

        CH = st.enter_context(nc.semaphore("ch"))
        SD = st.enter_context(nc.semaphore("sd"))
        C = {"ch": 0, "sd": 0}

        def dma(out, in_):
            Sy.dma_start(out=out, in_=in_).then_inc(SD, 16)
            C["sd"] += 16

        def dmas(out, in_):
            Sy.dma_start(out=out, in_=in_,
                         allow_slow_non_contiguous=True).then_inc(SD, 16)
            C["sd"] += 16

        def step(instr):
            instr.then_inc(CH, 1)
            C["ch"] += 1

        def begin(eng):
            eng.wait_ge(CH, C["ch"])

        def sy_step():
            Sy.wait_ge(SD, C["sd"])
            step(Sy.nop())

        class DStream:
            def __init__(self, name):
                self.name = name
                self.sems = [st.enter_context(nc.semaphore(name + "0")),
                             st.enter_context(nc.semaphore(name + "1"))]
                self.cnt = [0, 0]
                C.setdefault("streams", {})[name] = self

            def issue(self, b, pairs, slow=False):
                for (o, i_) in pairs:
                    Sy.dma_start(out=o, in_=i_,
                                 allow_slow_non_contiguous=slow).then_inc(
                        self.sems[b], 16)
                    self.cnt[b] += 16

            def wait(self, b):
                Sy.wait_ge(self.sems[b], self.cnt[b])

        dsW = DStream("dsw")
        dsW2 = DStream("dsw2")
        dsH = DStream("dsh")
        dsX = DStream("dsx")

        def arv(off, shape):
            n = int(np.prod(shape))
            assert off + n <= AR, (off, shape)
            a = ar[:, off:off + n]
            if len(shape) == 1:
                return a
            if len(shape) == 2:
                return a.rearrange("p (a b) -> p a b", b=shape[1])
            return a.rearrange("p (a b c) -> p a b c", b=shape[1], c=shape[2])

        qsg = arv(0, [256])
        ksg = arv(256, [256])
        kmsg = arv(512, [256])
        vsg = arv(768, [256])
        ktm = arv(1024, [256])
        VS = 68
        vtm = arv(1280, [4 * 68])
        sktm = arv(1552, [256])
        sq_ = arv(1808, [256])
        psb = arv(2064, [4, 512])
        ptsb = arv(4112, [512])
        attd = arv(4624, [512])
        nums = arv(5136, [512])
        stg = arv(5648, [256])
        denr = arv(5904, [512])
        ghatt = arv(6416, [NCH, 2])
        attfm = arv(6432, [NCH, 512])
        mask = arv(10528, [4, 512])
        wb2 = arv(14848, [NCH * 128])
        wb3 = arv(12576, [NCH * 128])
        wb4 = arv(13600, [NCH * 128])
        sgB = arv(12576, [1024])
        qsg2 = sgB[:, 0:256]
        ksg2 = sgB[:, 256:512]
        kmsg2 = sgB[:, 512:768]
        vsg2 = sgB[:, 768:1024]
        pstg = arv(0, [512])
        xpre_sb = arv(512, [NCH, 512])
        xpre_sb2 = arv(4608, [NCH, 512])
        m0tmp = arv(0, [H, 68])
        attfm_f = attfm.rearrange("p a b -> p (a b)")
        lnrows = arv(4112, [1536])
        sqb = psb.rearrange("p a b -> p (a b)")

        begin(Sy)
        dma(xs[:, :, :], xf)
        dmas(gates[0:64, 0:H], AP(wi["L0_attn_betas"].tensor, 0, [[1, DV], [DV, H]]))
        dmas(biasq[:, 0:NCH], AP(wi["L1_attn_bq"].tensor, 0, [[1, 128], [128, NCH]]))
        dmas(biask[:, 0:NCH], AP(wi["L1_attn_bk"].tensor, 0, [[1, 128], [128, NCH]]))
        dmas(biasv[:, 0:NCH], AP(wi["L1_attn_bv"].tensor, 0, [[1, 128], [128, NCH]]))
        dmas(convw[:, :, :], AP(wi["L1_ffn_conv_w"].tensor, 0,
                                [[3, 128], [3 * 128, NCH1], [1, 3]]))
        dmas(convb[:, 0:NCH1], AP(wi["L1_ffn_conv_b"].tensor, 0,
                                  [[1, 128], [128, NCH1]]))
        dma(stats[0:1, 0:4], flags)
        sy_step()
        begin(V)
        V.memset(stats[:, 13:14], 1e-5)
        step(V.drain())
        begin(A)
        step(A.activation(gates[0:64, 0:H], gates[0:64, 0:H], AF.Sigmoid))

        begin(G)
        G.iota(iicol[:, 0:128], pattern=[[1, 128]], base=0, channel_multiplier=-1,
               allow_small_or_imprecise_dtypes=True)
        step(G.drain())
        begin(V)
        V.tensor_single_scalar(ident[:, :], iicol[:, 0:128], 0.0, ALU.is_equal)
        V.memset(ones[:, :], 1.0)
        step(V.drain())
        begin(T)
        step(T.matmul(pb[0][:, 0:4], ones[0:1, 0:128], stats[0:1, 0:4],
                      start=True, stop=True))
        begin(V)
        V.tensor_copy(stats[:, 4:8], pb[0][:, 0:4])
        step(V.drain())

        begin(G)
        G.iota(iicol[:, 0:128].rearrange("p (a b) -> p a b", b=32),
               pattern=[[2, 4], [16, 32]], base=0, channel_multiplier=0,
               allow_small_or_imprecise_dtypes=True)
        step(G.drain())
        begin(T)
        step(T.transpose(pb[0][:, 0:128], iicol[:, 0:128], ident[:, :]))
        begin(V)
        V.tensor_copy(iicol[:, 0:1], pb[0][:, 0:1])
        step(V.drain())
        begin(G)
        G.iota(denr[:, :].rearrange("p (a b c) -> p a b c", b=8, c=32),
               pattern=[[1, 2], [2, 8], [16, 32]], base=0, channel_multiplier=0,
               allow_small_or_imprecise_dtypes=True)
        step(G.drain())
        begin(V)
        for blk in range(4):
            clo_i, ihalf = blk // 2, blk % 2
            off = float(8 * ihalf + clo_i)
            V.tensor_scalar(stats[:, 8 + blk:9 + blk], iicol[:, 0:1], off, None,
                            ALU.add)
        V.drain()
        for blk in range(4):
            V.scalar_tensor_tensor(mask[:, blk, :], denr[:, :],
                                   stats[:, 8 + blk:9 + blk], denr[:, :],
                                   ALU.subtract, ALU.bypass)
        V.drain()
        for blk in range(4):
            V.tensor_scalar(mask[:, blk, :], mask[:, blk, :], 0.0, NEG,
                            ALU.is_gt, ALU.mult)
        step(V.drain())

        C["mark_P0"] = (C["ch"], C["sd"])
        main_tiles = _ttiles(TT, 512)
        pre_tiles = _ttiles(TC, 512)
        xpre_f = xpre_sb.rearrange("p a b -> p (a b)")

        def w_issue(b, whbm, m, nk=NCH, stream=None):
            s = stream if stream is not None else dsW
            buf = ([wb, wb2] if s is dsW else [wb3, wb4])[b]
            s.issue(b, [(buf[:, k * 128:(k + 1) * 128],
                         whbm[k * 128:(k + 1) * 128, m * 128:(m + 1) * 128])
                        for k in range(nk)])
            return buf

        def proj_pass(whbm, dst, src_is_stream, tiles):
            xpre_bufs = [xpre_sb, xpre_sb2]
            xpre_fs = [xpre_f, xpre_sb2.rearrange("p a b -> p (a b)")]
            begin(Sy)
            w_issue(0, whbm, 0)
            if not src_is_stream:
                dsX.issue(0, [(xpre_bufs[0][:, :, 0:tiles[0][1]],
                               xpre[:, :, tiles[0][0]:tiles[0][0] + tiles[0][1]])])
            step(Sy.nop())
            nt = len(tiles)
            for m in range(NCH):
                mb = m % 2
                begin(Sy)
                dsW.wait(mb)
                if m + 1 < NCH:
                    w_issue((m + 1) % 2, whbm, m + 1)
                step(Sy.nop())
                wcur = [wb, wb2][mb]
                for ti, (t0, tn) in enumerate(tiles):
                    gi = m * nt + ti
                    if not src_is_stream:
                        begin(Sy)
                        dsX.wait(gi % 2)
                        if gi + 1 < NCH * nt:
                            t0n, tnn = tiles[(ti + 1) % nt]
                            dsX.issue((gi + 1) % 2,
                                      [(xpre_bufs[(gi + 1) % 2][:, :, 0:tnn],
                                        xpre[:, :, t0n:t0n + tnn])])
                        step(Sy.nop())
                    begin(T)
                    for k in range(NCH):
                        rhs = (xsf[:, k * TT + t0: k * TT + t0 + tn]
                               if src_is_stream
                               else xpre_fs[gi % 2][:, k * 512: k * 512 + tn])
                        T.matmul(pb[0][:, 0:tn], wcur[:, k * 128:(k + 1) * 128],
                                 rhs, start=(k == 0), stop=(k == NCH - 1))
                    step(T.drain())
                    begin(V)
                    V.tensor_copy(pstg[:, 0:tn], pb[0][:, 0:tn])
                    step(V.drain())
                    begin(Sy)
                    dma(dst[:, m, t0:t0 + tn], pstg[:, 0:tn])
                    sy_step()

        if _PHASES >= 1:
            proj_pass(wi["L0_attn_Wq"], qdr, True, main_tiles)
            proj_pass(wi["L0_attn_Wk"], kdr, True, main_tiles)
            proj_pass(wi["L0_attn_Wv"], vdr, True, main_tiles)
            proj_pass(wi["L0_attn_Wk"], kpre, False, pre_tiles)
            proj_pass(wi["L0_attn_Wv"], vpre, False, pre_tiles)

        C["mark_P1"] = (C["ch"], C["sd"])
        begin(V)
        V.memset(mst[:, :], 0.0)
        step(V.drain())

        def elu1(dst, src, tmp):
            begin(V)
            V.tensor_scalar_min(tmp, src, 0.0)
            step(V.drain())
            begin(A)
            A.activation(tmp, tmp, AF.Exp)
            step(A.drain())
            begin(V)
            V.scalar_tensor_tensor(dst, src, 0.0, tmp, ALU.max, ALU.add)
            step(V.drain())

        SGS = [(qsg, ksg, kmsg, vsg), (qsg2, ksg2, kmsg2, vsg2)]

        def head_kv_issue(b, ksrc, vsrc, qsrc, col0, with_q):
            qs, ks, km, vs = SGS[b]
            pairs = [(ks[:, :].rearrange("p (a b) -> p a b", b=32),
                      ksrc[:, :, col0:col0 + 32]),
                     (vs[:, :].rearrange("p (a b) -> p a b", b=32),
                      vsrc[:, :, col0:col0 + 32])]
            if with_q:
                pairs.append((qs[:, :].rearrange("p (a b) -> p a b", b=32),
                              qsrc[:, :, col0:col0 + 32]))
            dsH.issue(b, pairs)

        def head_kv_mirror(b):
            qs, ks, km, vs = SGS[b]
            dsH.issue(b, [(km[0:64, :], ks[64:128, :]),
                          (km[64:128, :], ks[0:64, :])])

        def head_tm(sg, with_sk=True, parts='all', pre_T=None):
            for rnd, (src, dstt, stridew) in enumerate(
                    [(sg[1], ktm, 64), (sg[3], vtm, VS)]):
                begin(T)
                if rnd == 0 and pre_T is not None:
                    pre_T()
                for L in range(4):
                    clo, g = L // 2, L % 2
                    idn = ident[64 * clo:64 * clo + 64, 64 * clo:64 * clo + 64]
                    T.transpose(pb[4 + L][:, 0:64],
                                src[64 * clo:64 * clo + 64,
                                    g * 128:(g + 1) * 128], idn)
                step(T.drain())
                begin(V)
                for L in range(4):
                    V.tensor_copy(dstt[:, L * stridew:L * stridew + 64],
                                  pb[4 + L][:, 0:64])
                if rnd == 1 and parts != 'tp':
                    V.memset(vtm.rearrange("p (a b) -> p a b", b=VS)[:, :, 64:65],
                             1.0)
                    if parts != 'ones' and with_sk:
                        V.tensor_scalar_min(denr[:, 0:256], ktm[:, :], 0.0)
                step(V.drain())
            if parts in ('tp', 'ones'):
                return
            if with_sk:
                begin(A)
                A.activation(denr[:, 0:256], denr[:, 0:256], AF.Exp)
                step(A.drain())
                begin(V)
                V.scalar_tensor_tensor(sktm[:, :], ktm[:, :], 0.0,
                                       denr[:, 0:256], ALU.max, ALU.add)
                step(V.drain())

        def dm_update(h, mfac=None):
            begin(T)
            for base in range(2):
                for L in range(4):
                    T.matmul(pb[7][64 * base:64 * base + 64, 0:DV + 1],
                             sktm[:, L * 64:(L + 1) * 64],
                             vtm[:, L * VS:L * VS + 65],
                             start=(L == 0), stop=(L == 3))
            step(T.drain())
            begin(V)
            for base in range(2):
                sl = slice(64 * base, 64 * base + 64)
                if mfac is None:
                    V.tensor_add(mst[sl, h * MS:h * MS + 65],
                                 mst[sl, h * MS:h * MS + 65],
                                 pb[7][sl, 0:DV + 1])
                else:
                    V.scalar_tensor_tensor(mst[sl, h * MS:h * MS + 65],
                                           pb[7][sl, 0:DV + 1], mfac[sl, :],
                                           mst[sl, h * MS:h * MS + 65],
                                           ALU.mult, ALU.add)
            step(V.drain())

        fM = stats[:, 4:5]
        begin(Sy)
        m0 = wi["L0_attn_memory0"]
        z0 = wi["L0_attn_z0"]
        dmas(m0tmp[0:64, :, 0:DK], AP(m0.tensor, 0, [[DV, 64], [DK * DV, H], [1, DV]]))
        dmas(m0tmp[64:128, :, 0:DK], AP(m0.tensor, 0, [[DV, 64], [DK * DV, H], [1, DV]]))
        dmas(m0tmp[0:64, :, DK:DK + 1], AP(z0.tensor, 0, [[1, 64], [DK, H]]))
        dmas(m0tmp[64:128, :, DK:DK + 1], AP(z0.tensor, 0, [[1, 64], [DK, H]]))
        sy_step()
        begin(V)
        V.tensor_add(mst[:, :], mst[:, :],
                     m0tmp[:, :, :].rearrange("p a b -> p (a b)"))
        step(V.drain())

        gq = arv(2064, [8])
        gk = arv(2072, [8])
        gv = arv(2080, [8])
        gkm = arv(2088, [8])
        gsq = arv(2096, [8])
        gnum = arv(2112, [16])
        gattd = arv(2128, [16])
        gden = arv(2144, [16])
        gstg = arv(2160, [64])
        gvt = arv(2224, [68])
        gptm = arv(2292, [64])
        gsc = arv(2356, [1024])
        gmk = arv(3380, [1024])
        ghatt_f = ghatt.rearrange("p a b -> p (a b)")

        def ghost_softmax(ncols):
            v2 = gsc[0:8, 0:2 * ncols].rearrange("p (a b) -> p a b", b=ncols)
            st2 = stats[0:8, 56:58]
            begin(V)
            V.tensor_reduce(st2[:, :, None], v2, axis=AX.X, op=ALU.max,
                            negate=True)
            step(V.drain())
            bca = AP(st2.tensor, st2.offset, [[64, 8], [1, 2], [0, ncols]])
            begin(V)
            V.tensor_tensor(v2, v2, bca, ALU.add)
            step(V.drain())
            begin(A)
            A.activation(gsc[0:8, 0:2 * ncols], gsc[0:8, 0:2 * ncols], AF.Exp,
                         scale=SC0)
            step(A.drain())
            begin(V)
            V.tensor_reduce(st2[:, :, None], v2, axis=AX.X, op=ALU.add)
            V.drain()
            V.reciprocal(st2, st2)
            step(V.drain())
            begin(V)
            V.tensor_tensor(v2, v2, bca, ALU.mult)
            step(V.drain())

        def ghost_mem_combine(h):
            begin(T)
            T.matmul(pb[7][0:DK + 1, 0:8], mst[0:64, h * MS:h * MS + 65],
                     gsq[0:64, 0:8], start=True, stop=True)
            T.drain()
            T.matmul(pb[7][0:DK + 1, 8:16], mst[64:128, h * MS:h * MS + 65],
                     gsq[64:128, 0:8], start=True, stop=True)
            step(T.drain())
            begin(V)
            V.reciprocal(gden[64:65, 0:16], pb[7][64:65, 0:16])
            step(V.drain())
            begin(T)
            step(T.matmul(pb[2][0:64, 0:16], ones[64:65, 0:64],
                          gden[64:65, 0:16], start=True, stop=True))
            begin(V)
            V.tensor_copy(gnum[0:64, 0:16], pb[7][0:64, 0:16])
            V.drain()
            V.tensor_mul(gnum[0:64, 0:16], gnum[0:64, 0:16], pb[2][0:64, 0:16])
            V.drain()
            V.tensor_sub(gnum[0:64, 0:16], gnum[0:64, 0:16], gattd[0:64, 0:16])
            V.drain()
            V.scalar_tensor_tensor(gnum[0:64, 0:16], gnum[0:64, 0:16],
                                   gates[0:64, h:h + 1], gattd[0:64, 0:16],
                                   ALU.mult, ALU.add)
            step(V.drain())

        def ghost_store(colidx):
            begin(V)
            V.tensor_copy(ghatt[0:64, :, colidx:colidx + 1],
                          gnum[0:64, 0:8][:, :, None])
            V.tensor_copy(gstg[0:64, 0:8], gnum[0:64, 8:16])
            step(V.drain())
            begin(Sy)
            dmas(ghatt[64:128, :, colidx:colidx + 1], gstg[0:64, 0:8][:, :, None])
            sy_step()

        def ghost_L(sg):
            begin(Sy)
            dmas(gq[:, 0:8], qdr[:, :, 0:1])
            sy_step()
            begin(T)
            for ci in range(2):
                lhsT = gq[64 * ci:64 * ci + 64, 0:8]
                for cj in range(2):
                    srct = sg[1] if cj == ci else sg[2]
                    T.matmul(pb[ci][0:8, cj * 256:cj * 256 + 256], lhsT,
                             srct[64 * ci:64 * ci + 64, 0:256],
                             start=(cj == 0), stop=(cj == 1))
            step(T.drain())
            begin(G)
            G.iota(gmk[:, :].rearrange("p (a b c d) -> p a b c d", b=2, c=8, d=32),
                   pattern=[[-1, 2], [1, 2], [2, 8], [16, 32]], base=-496,
                   channel_multiplier=-2, allow_small_or_imprecise_dtypes=True)
            step(G.drain())
            begin(V)
            V.tensor_scalar(gmk[:, :], gmk[:, :], 0.0, NEG, ALU.is_gt, ALU.mult)
            V.drain()
            for ci in range(2):
                V.tensor_add(gsc[0:8, ci * 512:(ci + 1) * 512],
                             pb[ci][0:8, 0:512], gmk[0:8, ci * 512:(ci + 1) * 512])
            step(V.drain())
            ghost_softmax(512)
            for ci in range(2):
                begin(T)
                for Jc in range(4):
                    T.transpose(pb[2 + Jc][0:128, 0:8],
                                gsc[0:8, ci * 512 + Jc * 128: ci * 512 + (Jc + 1) * 128],
                                ident[0:8, 0:8])
                step(T.drain())
                begin(V)
                for Jc in range(4):
                    V.tensor_copy(gptm[:, Jc * 16 + ci * 8: Jc * 16 + ci * 8 + 8],
                                  pb[2 + Jc][0:128, 0:8])
                step(V.drain())
            begin(T)
            for Jc in range(4):
                T.matmul(pb[6][0:DV, 0:16], vtm[:, Jc * VS:Jc * VS + 64],
                         gptm[:, Jc * 16:(Jc + 1) * 16],
                         start=(Jc == 0), stop=(Jc == 3))
            step(T.drain())
            begin(V)
            V.tensor_copy(gattd[0:64, 0:16], pb[6][0:DV, 0:16])
            step(V.drain())
            for clo in range(2):
                sl = slice(64 * clo, 64 * clo + 64)
                elu1(gsq[sl, 0:8], gq[sl, 0:8], denr[sl, 0:8])
            ghost_mem_combine(15)
            ghost_store(0)

        def ghost_R():
            begin(Sy)
            dmas(gq[:, 0:8], qdr[:, :, TT - 1:TT])
            dmas(gk[:, 0:8], kdr[:, :, TT - 1:TT])
            dmas(gv[:, 0:8], vdr[:, :, TT - 1:TT])
            sy_step()
            begin(Sy)
            dma(gkm[0:64, 0:8], gk[64:128, 0:8])
            dma(gkm[64:128, 0:8], gk[0:64, 0:8])
            sy_step()
            begin(T)
            for ci in range(2):
                lhsT = gq[64 * ci:64 * ci + 64, 0:8]
                for cj in range(2):
                    srct = gk if cj == ci else gkm
                    T.matmul(pb[ci][0:8, cj * 8:cj * 8 + 8], lhsT,
                             srct[64 * ci:64 * ci + 64, 0:8],
                             start=(cj == 0), stop=(cj == 1))
            step(T.drain())
            begin(G)
            G.iota(gmk[:, 0:32].rearrange("p (a b c) -> p a b c", b=2, c=8),
                   pattern=[[-1, 2], [1, 2], [2, 8]], base=0,
                   channel_multiplier=-2, allow_small_or_imprecise_dtypes=True)
            step(G.drain())
            begin(V)
            V.tensor_scalar(gmk[:, 0:32], gmk[:, 0:32], 0.0, NEG,
                            ALU.is_gt, ALU.mult)
            V.drain()
            for ci in range(2):
                V.tensor_add(gsc[0:8, ci * 16:(ci + 1) * 16],
                             pb[ci][0:8, 0:16], gmk[0:8, ci * 16:(ci + 1) * 16])
            step(V.drain())
            ghost_softmax(16)
            begin(T)
            T.transpose(pb[2][0:8, 0:64], gv[0:64, 0:8], ident[0:64, 0:64])
            T.drain()
            T.transpose(pb[3][0:8, 0:64], gv[64:128, 0:8], ident[64:128, 64:128])
            step(T.drain())
            begin(V)
            V.tensor_copy(gvt[0:8, 0:64], pb[2][0:8, 0:64])
            V.tensor_copy(gstg[0:8, 0:64], pb[3][0:8, 0:64])
            step(V.drain())
            begin(Sy)
            dma(gvt[8:16, 0:64], gstg[0:8, 0:64])
            sy_step()
            begin(T)
            T.transpose(pb[4][0:16, 0:8], gsc[0:8, 0:16], ident[0:8, 0:8])
            T.drain()
            T.transpose(pb[5][0:16, 0:8], gsc[0:8, 16:32], ident[0:8, 0:8])
            step(T.drain())
            begin(V)
            V.tensor_copy(gptm[0:16, 0:8], pb[4][0:16, 0:8])
            V.tensor_copy(gptm[0:16, 8:16], pb[5][0:16, 0:8])
            step(V.drain())
            begin(T)
            step(T.matmul(pb[6][0:DV, 0:16], gvt[0:16, 0:64],
                          gptm[0:16, 0:16], start=True, stop=True))
            begin(V)
            V.tensor_copy(gattd[0:64, 0:16], pb[6][0:DV, 0:16])
            step(V.drain())
            for clo in range(2):
                sl = slice(64 * clo, 64 * clo + 64)
                elu1(gsq[sl, 0:8], gq[sl, 0:8], denr[sl, 0:8])
            ghost_mem_combine(0)
            ghost_store(1)

        if _PHASES >= 2:
            heads2 = [(seg, h) for seg in range(NSEG) for h in range(H)]
            begin(Sy)
            head_kv_issue(0, kpre, vpre, None, heads2[0][0] * LSEG, False)
            step(Sy.nop())
            begin(Sy)
            dsH.wait(0)
            head_kv_mirror(0)
            step(Sy.nop())
            for i, (seg, h) in enumerate(heads2):
                b = i % 2
                begin(Sy)
                dsH.wait(b)
                if i + 1 < len(heads2):
                    s2, h2 = heads2[i + 1]
                    head_kv_issue(1 - b, kpre, vpre, None,
                                  s2 * LSEG + h2 * 32, False)
                step(Sy.nop())
                head_tm(SGS[b], with_sk=True)
                if _GHOSTS and seg == 7 and h == 15:
                    ghost_L(SGS[b])
                dm_update(h, mfac=fM)
                if i + 1 < len(heads2):
                    begin(Sy)
                    dsH.wait(1 - b)
                    head_kv_mirror(1 - b)
                    step(Sy.nop())

        C["mark_P2"] = (C["ch"], C["sd"])
        def softmax_block(blk):
            bank = pb[blk]
            begin(V)
            V.tensor_add(bank[:, :], bank[:, :], mask[:, blk, :])
            V.drain()
            V.reduce_max(stats[:, 16:17], bank[:, :], axis=AX.X, negate=True)
            V.drain()
            step(V.tensor_scalar_mul(stats[:, 17:18], stats[:, 16:17], SC0))
            begin(A)
            A.activation(psb[:, blk, :], bank[:, :], AF.Exp, scale=SC0,
                         bias=stats[:, 17:18], accum_out=stats[:, 18:19])
            A.drain()
            A.activation(stats[:, 19:20], stats[:, 18:19], AF.Ln)
            step(A.drain())
            begin(V)
            step(V.tensor_sub(stats[:, 20:21], stats[:, 17:18], stats[:, 19:20]))
            begin(A)
            A.activation(psb[:, blk, :], bank[:, :], AF.Exp, scale=SC0,
                         bias=stats[:, 20:21])
            step(A.drain())

        def softmax_all():
            begin(V)
            for blk in range(4):
                V.tensor_add(pb[blk][:, :], pb[blk][:, :], mask[:, blk, :])
            V.drain()
            for blk in range(4):
                V.reduce_max(stats[:, 16 + blk:17 + blk], pb[blk][:, :],
                             axis=AX.X, negate=True)
            V.drain()
            for blk in range(4):
                V.tensor_scalar_mul(stats[:, 20 + blk:21 + blk],
                                    stats[:, 16 + blk:17 + blk], SC0)
            step(V.drain())
            begin(A)
            for blk in range(4):
                A.activation(psb[:, blk, :], pb[blk][:, :], AF.Exp, scale=SC0,
                             bias=stats[:, 20 + blk:21 + blk],
                             accum_out=stats[:, 24 + blk:25 + blk])
                A.drain()
            for blk in range(4):
                A.activation(stats[:, 28 + blk:29 + blk],
                             stats[:, 24 + blk:25 + blk], AF.Ln)
            step(A.drain())
            begin(V)
            for blk in range(4):
                V.tensor_sub(stats[:, 32 + blk:33 + blk],
                             stats[:, 20 + blk:21 + blk],
                             stats[:, 28 + blk:29 + blk])
            step(V.drain())
            begin(A)
            for blk in range(4):
                A.activation(psb[:, blk, :], pb[blk][:, :], AF.Exp, scale=SC0,
                             bias=stats[:, 32 + blk:33 + blk])
            step(A.drain())

        def attn_head(sg, qtr, hq, h):
            qsg_, ksg_, kmsg_, vsg_ = sg
            mode = _P3MODE

            def scores_T():
                for blk in range(4):
                    clo_i, ihalf = blk // 2, blk % 2
                    lhsT = qsg_[64 * clo_i:64 * clo_i + 64,
                                ihalf * 128:(ihalf + 1) * 128]
                    for clo_j in range(2):
                        srct = ksg_ if clo_j == clo_i else kmsg_
                        rhs = srct[64 * clo_i:64 * clo_i + 64, 0:256]
                        T.matmul(pb[blk][:, clo_j * 256:clo_j * 256 + 256],
                                 lhsT, rhs, start=(clo_j == 0),
                                 stop=(clo_j == 1))

            head_tm(sg, with_sk=True, pre_T=scores_T)
            if mode == 'tm':
                return
            softmax_all()
            if mode == 'softmax':
                return
            begin(T)
            for Ib in range(4):
                T.transpose(pb[Ib][:, 0:128],
                            psb[:, Ib, 0:128], ident[:, :])
            step(T.drain())
            for Jc in range(4):
                begin(V)
                for Ib in range(4):
                    V.tensor_copy(ptsb[:, Ib * 128:(Ib + 1) * 128],
                                  pb[Ib][:, 0:128])
                step(V.drain())
                begin(T)
                T.matmul(pb[4][0:DV, 0:512], vtm[:, Jc * VS:Jc * VS + 64],
                         ptsb[:, 0:512], start=(Jc == 0), stop=(Jc == 3))
                if Jc < 3:
                    for Ib in range(4):
                        T.transpose(pb[Ib][:, 0:128],
                                    psb[:, Ib, (Jc + 1) * 128:(Jc + 2) * 128],
                                    ident[:, :])
                step(T.drain())
            begin(V)
            V.tensor_copy(attd[0:64, 0:512], pb[4][0:DV, 0:512])
            for clo in range(2):
                sl = slice(64 * clo, 64 * clo + 64)
                V.tensor_scalar_min(denr[sl, 0:256], qsg_[sl, :], 0.0)
            step(V.drain())
            if mode == 'pt':
                return
            begin(A)
            for clo in range(2):
                sl = slice(64 * clo, 64 * clo + 64)
                A.activation(denr[sl, 0:256], denr[sl, 0:256], AF.Exp)
            step(A.drain())
            begin(V)
            for clo in range(2):
                sl = slice(64 * clo, 64 * clo + 64)
                V.scalar_tensor_tensor(sq_[sl, :], qsg_[sl, :], 0.0,
                                       denr[sl, 0:256], ALU.max, ALU.add)
            step(V.drain())
            if mode == 'elu':
                return
            begin(T)
            T.matmul(pb[5][0:DK + 1, 0:256], mst[0:64, h * MS:h * MS + 65],
                     sq_[0:64, :], start=True, stop=True)
            T.drain()
            T.matmul(pb[5][0:DK + 1, 256:512], mst[64:128, h * MS:h * MS + 65],
                     sq_[64:128, :], start=True, stop=True)
            step(T.drain())
            begin(V)
            V.reciprocal(denr[64:65, 0:512], pb[5][64:65, 0:512])
            V.tensor_copy(nums[0:64, 0:512], pb[5][0:64, 0:512])
            step(V.drain())
            begin(T)
            step(T.matmul(pb[6][0:64, 0:512], ones[64:65, 0:64],
                          denr[64:65, 0:512], start=True, stop=True))
            if mode == 'mem':
                return
            begin(V)
            V.tensor_mul(nums[0:64, 0:512], nums[0:64, 0:512], pb[6][0:64, 0:512])
            V.drain()
            V.tensor_sub(nums[0:64, 0:512], nums[0:64, 0:512], attd[0:64, 0:512])
            V.drain()
            c0 = qtr * 128 + hq * 32
            V.scalar_tensor_tensor(
                attfm[0:64, :, c0:c0 + 32],
                nums[0:64, 0:256].rearrange("p (a b) -> p a b", b=32),
                gates[0:64, h:h + 1],
                attd[0:64, 0:256].rearrange("p (a b) -> p a b", b=32),
                ALU.mult, ALU.add)
            V.scalar_tensor_tensor(
                stg[0:64, 0:256], nums[0:64, 256:512], gates[0:64, h:h + 1],
                attd[0:64, 256:512], ALU.mult, ALU.add)
            step(V.drain())
            begin(Sy)
            dma(attfm[64:128, :, c0:c0 + 32],
                stg[0:64, 0:256].rearrange("p (a b) -> p a b", b=32))
            sy_step()
            dm_update(h)

        def emit_ln(t0, tn):
            for half in range(2):
                ca = 4 * half
                begin(V)
                for c in range(ca, ca + 4):
                    V.tensor_mul(sqb[:, (c - ca) * 512:(c - ca) * 512 + tn],
                                 xs[:, c, t0:t0 + tn], xs[:, c, t0:t0 + tn])
                step(V.drain())
                begin(T)
                for c in range(ca, ca + 4):
                    T.matmul(pb[6][0:1, 0:tn], ones[:, 0:1],
                             xsf[:, c * TT + t0: c * TT + t0 + tn],
                             start=(c == 0), stop=(c == 7))
                for c in range(ca, ca + 4):
                    T.matmul(pb[7][0:1, 0:tn], ones[:, 0:1],
                             sqb[:, (c - ca) * 512:(c - ca) * 512 + tn],
                             start=(c == 0), stop=(c == 7))
                step(T.drain())
            murow = lnrows[0:1, 0:512]
            varrow = lnrows[0:1, 512:1024]
            mrsrow = lnrows[0:1, 1024:1536]
            begin(V)
            V.tensor_scalar_mul(murow[:, 0:tn], pb[6][0:1, 0:tn], 1.0 / D)
            V.drain()
            V.tensor_mul(varrow[:, 0:tn], murow[:, 0:tn], murow[:, 0:tn])
            V.drain()
            V.scalar_tensor_tensor(varrow[:, 0:tn], pb[7][0:1, 0:tn], 1.0 / D,
                                   varrow[:, 0:tn], ALU.mult, ALU.subtract)
            step(V.drain())
            begin(A)
            A.activation(varrow[:, 0:tn], varrow[:, 0:tn], AF.Ln,
                         bias=stats[0:1, 13:14])
            A.drain()
            A.activation(varrow[:, 0:tn], varrow[:, 0:tn], AF.Exp, scale=-0.5)
            step(A.drain())
            begin(V)
            V.tensor_mul(mrsrow[:, 0:tn], murow[:, 0:tn], varrow[:, 0:tn])
            step(V.drain())
            begin(T)
            T.matmul(pb[4][:, 0:tn], ones[0:1, 0:128], varrow[:, 0:tn],
                     start=True, stop=True)
            T.matmul(pb[5][:, 0:tn], ones[0:1, 0:128], mrsrow[:, 0:tn],
                     start=True, stop=True)
            step(T.drain())
            begin(V)
            for c in range(NCH):
                V.tensor_mul(xs[:, c, t0:t0 + tn], xs[:, c, t0:t0 + tn],
                             pb[4][:, 0:tn])
            V.drain()
            for c in range(NCH):
                V.tensor_sub(xs[:, c, t0:t0 + tn], xs[:, c, t0:t0 + tn],
                             pb[5][:, 0:tn])
            step(V.drain())

        heads3 = ([(seg, h) for seg in range(NSEG) for h in range(H)]
                  if _PHASES >= 3 else [])
        if heads3:
            begin(Sy)
            head_kv_issue(0, kdr, vdr, qdr, 1 + heads3[0][0] * LSEG, True)
            step(Sy.nop())
            begin(Sy)
            dsH.wait(0)
            head_kv_mirror(0)
            step(Sy.nop())
        for i, (seg, h) in enumerate(heads3):
            qtr, hq = h // 4, h % 4
            b = i % 2
            begin(Sy)
            dsH.wait(b)
            if i + 1 < len(heads3):
                s2, h2 = heads3[i + 1]
                head_kv_issue(1 - b, kdr, vdr, qdr,
                              1 + s2 * LSEG + h2 * 32, True)
            step(Sy.nop())
            attn_head(SGS[b], qtr, hq, h)
            if i + 1 < len(heads3):
                begin(Sy)
                dsH.wait(1 - b)
                head_kv_mirror(1 - b)
                step(Sy.nop())
            if h != H - 1:
                continue
            segc = 1 + seg * LSEG
            begin(Sy)
            w_issue(0, wi["L0_attn_Wo"], 0)
            step(Sy.nop())
            for m in range(NCH):
                begin(Sy)
                dsW.wait(m % 2)
                if m + 1 < NCH:
                    w_issue((m + 1) % 2, wi["L0_attn_Wo"], m + 1)
                step(Sy.nop())
                wcur = [wb, wb2][m % 2]
                begin(T)
                for k in range(NCH):
                    T.matmul(pb[0][:, 0:512], wcur[:, k * 128:(k + 1) * 128],
                             attfm_f[:, k * 512:(k + 1) * 512],
                             start=(k == 0), stop=(k == NCH - 1))
                step(T.drain())
                begin(V)
                step(V.tensor_add(xs[:, m, segc:segc + 512],
                                  xs[:, m, segc:segc + 512], pb[0][:, 0:512]))
            emit_ln(segc, 512)

        if _GHOSTS and _PHASES >= 3:
            ghost_R()
            for m in range(NCH):
                begin(Sy)
                for k in range(NCH):
                    dma(wb[:, k * 128:(k + 1) * 128],
                        wi["L0_attn_Wo"][k * 128:(k + 1) * 128,
                                         m * 128:(m + 1) * 128])
                sy_step()
                begin(T)
                for k in range(NCH):
                    T.matmul(pb[0][:, 0:2], wb[:, k * 128:(k + 1) * 128],
                             ghatt_f[:, k * 2:(k + 1) * 2],
                             start=(k == 0), stop=(k == NCH - 1))
                step(T.drain())
                begin(V)
                V.scalar_tensor_tensor(xs[:, m, 0:1], pb[0][:, 0:1],
                                       stats[:, 5:6], xs[:, m, 0:1],
                                       ALU.mult, ALU.add)
                V.drain()
                V.scalar_tensor_tensor(xs[:, m, TT - 1:TT], pb[0][:, 1:2],
                                       stats[:, 6:7], xs[:, m, TT - 1:TT],
                                       ALU.mult, ALU.add)
                step(V.drain())
            emit_ln(0, 1)
            emit_ln(TT - 1, 1)

        if "l0attn" in taps:
            begin(Sy)
            dma(dbg["l0attn"], xs[:, :, :])
            sy_step()

        C["mark_P3"] = (C["ch"], C["sd"])
        wpA = arv(0, [512])
        wpB = arv(512, [512])
        wpC = arv(1024, [512])
        wpD = arv(1536, [512])
        wpS = arv(2048, [512])
        wpM = arv(2560, [128])

        MID = [0.25, 0.75, 1.25, 1.75, 2.5, 3.5, 5.0]

        def fp4_stair(asrc, chain, chain2, nb16, wide):
            begin(V)
            V.tensor_scalar(chain, asrc, MID[0], None, ALU.is_gt)
            V.drain()
            for j in (1, 2, 3):
                V.scalar_tensor_tensor(chain, asrc, MID[j], chain,
                                       ALU.is_gt, ALU.add)
                V.drain()
            V.tensor_scalar(chain2, asrc, MID[4], None, ALU.is_gt)
            V.drain()
            V.scalar_tensor_tensor(chain2, asrc, MID[5], chain2,
                                   ALU.is_gt, ALU.add)
            V.drain()
            V.tensor_scalar(asrc, asrc, MID[6], 4.0, ALU.is_gt, ALU.mult)
            V.drain()
            V.scalar_tensor_tensor(chain2, chain2, 2.0, asrc, ALU.mult, ALU.add)
            V.drain()
            V.tensor_add(chain, chain, chain2)
            step(V.drain())

        def fp4_quant_nat(w_ap, nb16):
            n = nb16 * 16
            bm = wpM[:, 0:nb16]
            sh = wpM[:, 32:32 + nb16]
            rr = wpM[:, 64:64 + nb16]
            begin(V)
            V.tensor_reduce(bm[:, :, None], w_ap.rearrange("p (a b) -> p a b", b=16),
                            axis=AX.X, op=ALU.max, apply_absolute_value=True)
            V.drain()
            V.tensor_scalar(sh, bm, 1.0 / 6.0, 1e-12, ALU.mult, ALU.max)
            V.drain()
            V.reciprocal(rr, sh)
            V.drain()
            V.tensor_scalar_mul(sh, sh, 0.5)
            step(V.drain())
            rb = AP(rr.tensor, rr.offset, [[AR, 128], [1, nb16], [0, 16]])
            sb_ = AP(sh.tensor, sh.offset, [[AR, 128], [1, nb16], [0, 16]])
            begin(V)
            V.tensor_tensor(wpA[:, 0:n].rearrange("p (a b) -> p a b", b=16),
                            w_ap.rearrange("p (a b) -> p a b", b=16), rb, ALU.mult)
            step(V.drain())
            begin(A)
            A.activation(wpB[:, 0:n], wpA[:, 0:n], AF.Sign)
            A.drain()
            A.activation(wpA[:, 0:n], wpA[:, 0:n], AF.Abs)
            step(A.drain())
            fp4_stair(wpA[:, 0:n], wpC[:, 0:n], wpD[:, 0:n], nb16, n)
            begin(V)
            V.tensor_tensor(wpC[:, 0:n].rearrange("p (a b) -> p a b", b=16),
                            wpC[:, 0:n].rearrange("p (a b) -> p a b", b=16),
                            sb_, ALU.mult)
            V.drain()
            V.tensor_mul(wpC[:, 0:n], wpC[:, 0:n], wpB[:, 0:n])
            step(V.drain())

        def prep_store_T(src_ap, dstT, n0, kglob0, nk):
            for g0 in range(0, nk, 4):
                gn = min(4, nk - g0)
                begin(T)
                for i in range(gn):
                    T.transpose(pb[i][:, 0:128],
                                src_ap[:, (g0 + i) * 128:(g0 + i + 1) * 128],
                                ident[:, :])
                step(T.drain())
                begin(V)
                for i in range(gn):
                    V.tensor_copy(wpS[:, i * 128:(i + 1) * 128], pb[i][:, 0:128])
                step(V.drain())
                begin(Sy)
                dma(dstT[:, kglob0 + g0:kglob0 + g0 + gn, n0:n0 + 128],
                    wpS[:, 0:gn * 128].rearrange("p (a b) -> p a b", b=128))
                sy_step()

        WNB = None

        def prep_tiles(NW, KW):
            return [(n0, k0, min(512, KW - k0))
                    for n0 in range(0, NW, 128) for k0 in range(0, KW, 512)]

        def prep_loop(whbm, tiles, body):
            bufs = [wnat, wb3[:, 0:512]]
            begin(Sy)
            n0, k0, kn = tiles[0]
            dsX.issue(0, [(bufs[0][:, 0:kn], whbm[n0:n0 + 128, k0:k0 + kn])])
            step(Sy.nop())
            for i, (n0, k0, kn) in enumerate(tiles):
                b = i % 2
                begin(Sy)
                dsX.wait(b)
                if i + 1 < len(tiles):
                    n2, k2, kn2 = tiles[i + 1]
                    dsX.issue(1 - b, [(bufs[1 - b][:, 0:kn2],
                                       whbm[n2:n2 + 128, k2:k2 + kn2])])
                step(Sy.nop())
                body(bufs[b], n0, k0, kn)

        def prep_fp4_weight(whbm, dstT, NW, KW):
            def body(buf, n0, k0, kn):
                fp4_quant_nat(buf[:, 0:kn], kn // 16)
                prep_store_T(wpC, dstT, n0, k0 // 128, kn // 128)
            prep_loop(whbm, prep_tiles(NW, KW), body)

        def prep_plain_weight(whbm, dstT, NW, KW):
            def body(buf, n0, k0, kn):
                begin(V)
                V.tensor_copy(wpC[:, 0:kn], buf[:, 0:kn])
                step(V.drain())
                prep_store_T(wpC, dstT, n0, k0 // 128, kn // 128)
            prep_loop(whbm, prep_tiles(NW, KW), body)

        def prep_bit_weight(whbm, dstT, NW, KW, scol):
            acc = stats[:, 40:41]
            tmp = stats[:, 41:42]
            begin(V)
            V.memset(acc, 0.0)
            step(V.drain())
            def body1(buf, n0, k0, kn):
                begin(V)
                V.tensor_reduce(tmp, buf[:, 0:kn], axis=AX.X, op=ALU.add,
                                apply_absolute_value=True)
                V.drain()
                V.tensor_add(acc, acc, tmp)
                step(V.drain())
            prep_loop(whbm, prep_tiles(NW, KW), body1)
            begin(T)
            step(T.matmul(pb[0][0:1, 0:1], ones[:, 0:1], acc, start=True, stop=True))
            begin(V)
            V.tensor_scalar(stats[0:1, 43:44], pb[0][0:1, 0:1], 1.0 / (NW * KW),
                            1e-5, ALU.mult, ALU.max)
            V.drain()
            V.reciprocal(stats[0:1, 44:45], stats[0:1, 43:44])
            V.drain()
            V.tensor_scalar_mul(stats[0:1, 45:46], stats[0:1, 43:44], -1.0)
            step(V.drain())
            begin(T)
            T.matmul(pb[0][:, 0:1], ones[0:1, 0:128], stats[0:1, 44:45],
                     start=True, stop=True)
            T.matmul(pb[1][:, 0:1], ones[0:1, 0:128], stats[0:1, 45:46],
                     start=True, stop=True)
            step(T.drain())
            begin(V)
            V.tensor_copy(stats[:, scol:scol + 1], pb[0][:, 0:1])
            V.tensor_copy(stats[:, scol + 1:scol + 2], pb[1][:, 0:1])
            step(V.drain())
            def body2(buf, n0, k0, kn):
                begin(V)
                V.tensor_scalar(wpA[:, 0:kn], buf[:, 0:kn],
                                stats[:, scol:scol + 1], None, ALU.mult)
                V.drain()
                V.tensor_scalar(wpB[:, 0:kn], wpA[:, 0:kn], 0.5, None,
                                ALU.is_gt)
                V.drain()
                V.scalar_tensor_tensor(wpC[:, 0:kn], wpA[:, 0:kn], -0.5,
                                       wpB[:, 0:kn], ALU.is_lt, ALU.subtract)
                V.drain()
                V.tensor_scalar(wpC[:, 0:kn], wpC[:, 0:kn],
                                stats[:, scol + 1:scol + 2], None, ALU.mult)
                step(V.drain())
                prep_store_T(wpC, dstT, n0, k0 // 128, kn // 128)
            prep_loop(whbm, prep_tiles(NW, KW), body2)

        if _PHASES >= 4:
            if _P4MODE in ('fp4g', 'fp4', 'full'):
                prep_fp4_weight(wi["L0_ffn_w_gate"], wgT, DFF0, D)
            if _P4MODE in ('fp4', 'full'):
                prep_fp4_weight(wi["L0_ffn_w_up"], wuT, DFF0, D)
                prep_fp4_weight(wi["L0_ffn_w_down"], wdT, D, DFF0)
            if _P4MODE in ('plain', 'full'):
                prep_plain_weight(wi["L1_attn_Wq"], wqT1, D, D)
                prep_plain_weight(wi["L1_attn_Wk"], wkT1, D, D)
                prep_plain_weight(wi["L1_attn_Wv"], wvT1, D, D)
            if _P4MODE in ('bit', 'full'):
                prep_bit_weight(wi["L1_ffn_w1"], w1T, DFF1, D, 46)
                prep_bit_weight(wi["L1_ffn_w2"], w2T, D, DFF1, 49)
            for h in (range(H) if _P4MODE in ('phi', 'full') else []):
                begin(Sy)
                dma(wnat[0:64, 0:64],
                    AP(wi["L1_attn_weights"].tensor, h * DK * DK,
                       [[DK, 64], [1, DK]]))
                sy_step()
                begin(T)
                step(T.transpose(pb[0][0:64, 0:64],
                                 wnat[0:64, 0:64], ident[0:64, 0:64]))
                if h % 2 == 0:
                    begin(V)
                    V.tensor_copy(phiw[0:64, (h // 2) * 64:(h // 2) * 64 + 64],
                                  pb[0][0:64, 0:64])
                    step(V.drain())
                else:
                    begin(V)
                    V.tensor_copy(wpS[0:64, 0:64], pb[0][0:64, 0:64])
                    step(V.drain())
                    begin(Sy)
                    dma(phiw[64:128, (h // 2) * 64:(h // 2) * 64 + 64],
                        wpS[0:64, 0:64])
                    sy_step()

        C["mark_P4"] = (C["ch"], C["sd"])
        xqF = arv(0, [NCH, 384])
        XT = arv(3072, [512])
        FB1 = arv(3584, [512])
        FB2 = arv(4096, [512])
        FB3 = arv(4608, [512])
        FB4 = arv(5120, [512])
        hqb = arv(5648, [NCH0, 384])
        xqF_f = xqF.rearrange("p a b -> p (a b)")
        hqb_f = hqb.rearrange("p a b -> p (a b)")

        qm5 = arv(14720, [128])
        mst0 = mst[:, 0:1024]

        def quant_fp4_tm(buf, nsub, w, nb16_tot):
            n = nsub * 128
            bm = qm5[:, 0:nb16_tot]
            sh = qm5[:, 32:32 + nb16_tot]
            rr = qm5[:, 64:64 + nb16_tot]
            begin(V)
            V.tensor_reduce(bm[:, :, None],
                            buf[:, 0:n].rearrange("p (a b) -> p a b", b=16),
                            axis=AX.X, op=ALU.max, apply_absolute_value=True)
            V.drain()
            V.tensor_scalar(sh, bm, 1.0 / 6.0, 1e-12, ALU.mult, ALU.max)
            V.drain()
            V.reciprocal(rr, sh)
            V.drain()
            V.tensor_scalar_mul(sh, sh, 0.5)
            step(V.drain())
            rb = AP(rr.tensor, rr.offset, [[AR, 128], [1, nb16_tot], [0, 16]])
            sb_ = AP(sh.tensor, sh.offset, [[AR, 128], [1, nb16_tot], [0, 16]])
            begin(V)
            V.tensor_tensor(buf[:, 0:n].rearrange("p (a b) -> p a b", b=16),
                            buf[:, 0:n].rearrange("p (a b) -> p a b", b=16),
                            rb, ALU.mult)
            step(V.drain())
            begin(A)
            A.activation(FB2[:, 0:n], buf[:, 0:n], AF.Sign)
            A.drain()
            A.activation(buf[:, 0:n], buf[:, 0:n], AF.Abs)
            step(A.drain())
            fp4_stair(buf[:, 0:n], FB3[:, 0:n], FB4[:, 0:n], nb16_tot, n)
            begin(V)
            V.tensor_tensor(FB3[:, 0:n].rearrange("p (a b) -> p a b", b=16),
                            FB3[:, 0:n].rearrange("p (a b) -> p a b", b=16),
                            sb_, ALU.mult)
            V.drain()
            V.tensor_mul(buf[:, 0:n], FB3[:, 0:n], FB2[:, 0:n])
            step(V.drain())

        def evac_tp(dst128, bank, w):
            if w < 128:
                V.memset(dst128, 0.0)
                V.drain()
            V.tensor_copy(dst128[0:w, 0:128] if w < 128 else dst128,
                          pb[bank][0:w, 0:128])

        def xq_fp4(t0, tn, dst, dst_f):
            for s0 in range(0, tn, 128):
                w = min(128, tn - s0)
                for fh in range(2):
                    begin(T)
                    for i in range(4):
                        c = fh * 4 + i
                        T.transpose(pb[i][0:w, 0:128],
                                    xsf[:, c * TT + t0 + s0: c * TT + t0 + s0 + w],
                                    ident[:, :])
                    step(T.drain())
                    begin(V)
                    for i in range(4):
                        evac_tp(XT[:, i * 128:(i + 1) * 128], i, w)
                    step(V.drain())
                    quant_fp4_tm(XT, 4, w, 32)
                    begin(T)
                    for i in range(4):
                        c = fh * 4 + i
                        T.transpose(pb[i][:, 0:128],
                                    XT[:, i * 128:(i + 1) * 128], ident[:, :])
                    step(T.drain())
                    begin(V)
                    for i in range(4):
                        c = fh * 4 + i
                        V.tensor_copy(dst_f[:, c * 384 + s0: c * 384 + s0 + w],
                                      pb[i][0:128, 0:w])
                    step(V.drain())

        if _PHASES >= 5:
            for (t0, tn) in _ttiles(TT, 384):
                xq_fp4(t0, tn, xqF, xqF_f)
                if t0 == 0 and 'p5xq' in _DUMPS:
                    begin(Sy)
                    dma(dmp['p5xq'], ar[:, :])
                    sy_step()
                begin(Sy)
                dsW.issue(0, [(wb[:, :].rearrange("p (a b) -> p a b", b=128),
                               wgT[:, :, 0:128])])
                dsW2.issue(0, [(wb2[:, :].rearrange("p (a b) -> p a b", b=128),
                                wuT[:, :, 0:128])])
                step(Sy.nop())
                for nch in range(NCH0):
                    nb = nch % 2
                    gbuf = [wb, mst0][nb]
                    ubuf = [wb2, mst0][0]
                    begin(Sy)
                    dsW.wait(nb)
                    dsW2.wait(nch % 2)
                    if nch + 1 < NCH0:
                        nbuf = [wb, mst0][1 - nb]
                        dsW.issue(1 - nb,
                                  [(nbuf[:, :].rearrange("p (a b) -> p a b", b=128),
                                    wgT[:, :, (nch + 1) * 128:(nch + 2) * 128])])
                    step(Sy.nop())
                    begin(T)
                    for k in range(NCH):
                        T.matmul(pb[4][:, 0:tn], gbuf[:, k * 128:(k + 1) * 128],
                                 xqF_f[:, k * 384:k * 384 + tn],
                                 start=(k == 0), stop=(k == NCH - 1))
                    step(T.drain())
                    begin(T)
                    for k in range(NCH):
                        T.matmul(pb[5][:, 0:tn], ubuf[:, k * 128:(k + 1) * 128],
                                 xqF_f[:, k * 384:k * 384 + tn],
                                 start=(k == 0), stop=(k == NCH - 1))
                    step(T.drain())
                    begin(Sy)
                    if nch + 1 < NCH0:
                        dsW2.issue((nch + 1) % 2,
                                   [(wb2[:, :].rearrange("p (a b) -> p a b", b=128),
                                     wuT[:, :, (nch + 1) * 128:(nch + 2) * 128])])
                    step(Sy.nop())
                    begin(A)
                    A.activation(XT[:, 0:tn], pb[4][:, 0:tn], AF.Silu)
                    step(A.drain())
                    begin(V)
                    V.tensor_mul(XT[:, 0:tn], XT[:, 0:tn], pb[5][:, 0:tn])
                    step(V.drain())
                    nsub = (tn + 127) // 128
                    begin(T)
                    for s in range(nsub):
                        w = min(128, tn - s * 128)
                        T.transpose(pb[[6, 7, 3][s % 3]][0:w, 0:128],
                                    XT[:, s * 128:s * 128 + w], ident[:, :])
                    step(T.drain())
                    begin(V)
                    for s in range(nsub):
                        w = min(128, tn - s * 128)
                        evac_tp(FB1[:, s * 128:(s + 1) * 128], [6, 7, 3][s % 3], w)
                    step(V.drain())
                    if t0 == 0 and nch == 0 and 'p5h0' in _DUMPS:
                        begin(Sy)
                        dma(dmp['p5h0'], ar[:, :])
                        sy_step()
                    quant_fp4_tm(FB1, nsub, 128, nsub * 8)
                    if t0 == 0 and nch == 0 and 'p5hq0' in _DUMPS:
                        begin(Sy)
                        dma(dmp['p5hq0'], ar[:, :])
                        sy_step()
                    begin(T)
                    for s in range(nsub):
                        T.transpose(pb[[6, 7, 3][s % 3]][:, 0:128],
                                    FB1[:, s * 128:(s + 1) * 128], ident[:, :])
                    step(T.drain())
                    begin(V)
                    for s in range(nsub):
                        w = min(128, tn - s * 128)
                        V.tensor_copy(hqb_f[:, nch * 384 + s * 128:
                                            nch * 384 + s * 128 + w],
                                      pb[[6, 7, 3][s % 3]][0:128, 0:w])
                    step(V.drain())
                if t0 == 0 and 'p5hqb' in _DUMPS:
                    begin(Sy)
                    dma(dmp['p5hqb'], ar[:, :])
                    sy_step()
                dgroups = [(m, kg, min(8, NCH0 - kg))
                           for m in range(NCH) for kg in range(0, NCH0, 8)]
                begin(Sy)
                m0_, kg0, gn0 = dgroups[0]
                dsW.issue(0, [(wb[:, 0:gn0 * 128].rearrange("p (a b) -> p a b", b=128),
                               wdT[:, kg0:kg0 + gn0, m0_ * 128:(m0_ + 1) * 128])])
                step(Sy.nop())
                for gi, (m, kg, gn) in enumerate(dgroups):
                    gb = gi % 2
                    begin(Sy)
                    dsW.wait(gb)
                    if gi + 1 < len(dgroups):
                        m2, kg2, gn2 = dgroups[gi + 1]
                        nbuf = [wb, mst0][1 - gb]
                        dsW.issue(1 - gb,
                                  [(nbuf[:, 0:gn2 * 128].rearrange("p (a b) -> p a b", b=128),
                                    wdT[:, kg2:kg2 + gn2, m2 * 128:(m2 + 1) * 128])])
                    step(Sy.nop())
                    cbuf = [wb, mst0][gb]
                    begin(T)
                    for i in range(gn):
                        T.matmul(pb[m][:, 0:tn],
                                 cbuf[:, i * 128:(i + 1) * 128],
                                 hqb_f[:, (kg + i) * 384:(kg + i) * 384 + tn],
                                 start=(kg + i == 0),
                                 stop=(kg + i == NCH0 - 1))
                    step(T.drain())
                for m in range(NCH):
                    begin(V)
                    step(V.tensor_add(xs[:, m, t0:t0 + tn], xs[:, m, t0:t0 + tn],
                                      pb[m][:, 0:tn]))
                emit_ln(t0, tn)

        if "ffn0" in taps:
            begin(Sy)
            dma(dbg["ffn0"], xs[:, :, :])
            sy_step()

        C["mark_P5"] = (C["ch"], C["sd"])
        qf1 = arv(5648, [NCH, 256])
        kf1 = arv(7696, [NCH, 256])
        vf1 = arv(9744, [NCH, 256])
        phq = arv(11792, [256])
        phk = arv(12048, [256])
        prod = arv(12304, [256])
        srow = arv(12560, [256])
        rsrow = arv(12816, [256])
        tv1 = arv(13072, [256])
        dum6 = arv(13328, [256])
        phiwQ = arv(13584, [512])
        phiwK = arv(14096, [512])
        qf1_f = qf1.rearrange("p a b -> p (a b)")
        kf1_f = kf1.rearrange("p a b -> p (a b)")
        vf1_f = vf1.rearrange("p a b -> p (a b)")

        def l1_proj(dstf, wT, bias_sb, t0, tn):
            begin(Sy)
            dsW.issue(0, [(wb[:, :].rearrange("p (a b) -> p a b", b=128),
                           wT[:, :, 0:128])])
            step(Sy.nop())
            for m in range(NCH):
                mb = m % 2
                begin(Sy)
                dsW.wait(mb)
                if m + 1 < NCH:
                    nbuf = [wb, wb2][1 - mb]
                    dsW.issue(1 - mb,
                              [(nbuf[:, :].rearrange("p (a b) -> p a b", b=128),
                                wT[:, :, (m + 1) * 128:(m + 2) * 128])])
                step(Sy.nop())
                wcur = [wb, wb2][mb]
                begin(T)
                for k in range(NCH):
                    T.matmul(pb[0][:, 0:tn], wcur[:, k * 128:(k + 1) * 128],
                             xsf[:, k * TT + t0: k * TT + t0 + tn],
                             start=(k == 0), stop=(k == NCH - 1))
                step(T.drain())
                begin(A)
                A.activation(dstf[:, m * 256:m * 256 + tn], pb[0][:, 0:tn],
                             AF.Identity, bias=bias_sb[:, m:m + 1])
                step(A.drain())

        if _PHASES >= 6:
            begin(V)
            V.memset(stats[:, 33:34], 0.0)
            V.memset(stats[:, 34:35], 0.0)
            step(V.drain())
            for (wTl, acc_col, b_sb) in [(wqT1, 33, biasq), (wkT1, 34, biask)]:
                begin(Sy)
                dsW.issue(0, [(wb[:, :].rearrange("p (a b) -> p a b", b=128),
                               wTl[:, :, 0:128])])
                step(Sy.nop())
                for m in range(NCH):
                    mb = m % 2
                    begin(Sy)
                    dsW.wait(mb)
                    if m + 1 < NCH:
                        nbuf = [wb, wb2][1 - mb]
                        dsW.issue(1 - mb,
                                  [(nbuf[:, :].rearrange("p (a b) -> p a b", b=128),
                                    wTl[:, :, (m + 1) * 128:(m + 2) * 128])])
                    step(Sy.nop())
                    wpa_cur = [wb, wb2][mb]
                    for (t0, tn) in _ttiles(TC, 512, start=1):
                        tn = min(tn, 4097 - t0)
                        begin(T)
                        for k in range(NCH):
                            T.matmul(pb[0][:, 0:tn],
                                     wpa_cur[:, k * 128:(k + 1) * 128],
                                     xsf[:, k * TT + t0: k * TT + t0 + tn],
                                     start=(k == 0), stop=(k == NCH - 1))
                        step(T.drain())
                        begin(A)
                        A.activation(XT[:, 0:tn], pb[0][:, 0:tn], AF.Square,
                                     bias=b_sb[:, m:m + 1],
                                     accum_out=stats[:, 35:36])
                        step(A.drain())
                        begin(V)
                        V.tensor_add(stats[:, acc_col:acc_col + 1],
                                     stats[:, acc_col:acc_col + 1],
                                     stats[:, 35:36])
                        step(V.drain())
            LN8 = 2.0794415416798357
            for (acc_col, rq_col, ls_col) in [(33, 36, 37), (34, 38, 39)]:
                begin(T)
                step(T.matmul(pb[0][0:1, 0:1], ones[:, 0:1],
                              stats[:, acc_col:acc_col + 1], start=True, stop=True))
                begin(V)
                V.tensor_copy(stats[0:1, 60:61], pb[0][0:1, 0:1])
                step(V.drain())
                begin(A)
                A.activation(stats[0:1, 61:62], stats[0:1, 60:61], AF.Ln)
                A.drain()
                A.activation(stats[0:1, 61:62], stats[0:1, 61:62], AF.Exp,
                             scale=-0.5)
                step(A.drain())
                begin(V)
                V.tensor_mul(stats[0:1, 62:63], stats[0:1, 61:62],
                             stats[0:1, 61:62])
                V.drain()
                V.tensor_mul(stats[0:1, 62:63], stats[0:1, 62:63],
                             stats[0:1, 60:61])
                V.drain()
                V.tensor_scalar(stats[0:1, 62:63], stats[0:1, 62:63], -0.5,
                                -LN8, ALU.mult, ALU.add)
                step(V.drain())
                begin(T)
                T.matmul(pb[0][:, 0:1], ones[0:1, 0:128], stats[0:1, 61:62],
                         start=True, stop=True)
                T.matmul(pb[1][:, 0:1], ones[0:1, 0:128], stats[0:1, 62:63],
                         start=True, stop=True)
                step(T.drain())
                begin(V)
                V.tensor_copy(stats[:, rq_col:rq_col + 1], pb[0][:, 0:1])
                V.tensor_copy(stats[:, ls_col:ls_col + 1], pb[1][:, 0:1])
                step(V.drain())
            begin(V)
            V.tensor_scalar(phiwQ[:, :], phiw[:, :], stats[:, 36:37], None,
                            ALU.mult)
            V.tensor_scalar(phiwK[:, :], phiw[:, :], stats[:, 38:39], None,
                            ALU.mult)
            step(V.drain())
            edge_tiles = ([(0, 1, 5), (TT - 1, 1, 6)] if _GHOSTS else [])
            for (t0, tn, fcol) in [(1 + i * 256, 256, None) for i in range(16)] \
                    + edge_tiles:
                l1_proj(qf1_f, wqT1, biasq, t0, tn)
                l1_proj(kf1_f, wkT1, biask, t0, tn)
                l1_proj(vf1_f, wvT1, biasv, t0, tn)
                for h in range(H):
                    hb = 64 * (h % 2)
                    hsl = slice(hb, hb + 64)
                    hcol = (h // 2) * 256
                    begin(T)
                    T.matmul(pb[0][0:64, 0:tn],
                             phiwQ[hsl, (h // 2) * 64:(h // 2) * 64 + 64],
                             qf1_f[hsl, hcol:hcol + tn], start=True, stop=True)
                    T.drain()
                    T.matmul(pb[1][0:64, 0:tn],
                             phiwK[hsl, (h // 2) * 64:(h // 2) * 64 + 64],
                             kf1_f[hsl, hcol:hcol + tn], start=True, stop=True)
                    step(T.drain())
                    begin(A)
                    A.activation(phq[0:64, 0:tn], pb[0][0:64, 0:tn], AF.Exp,
                                 bias=stats[0:64, 37:38])
                    A.drain()
                    A.activation(phk[0:64, 0:tn], pb[1][0:64, 0:tn], AF.Exp,
                                 bias=stats[0:64, 39:40])
                    step(A.drain())
                    begin(V)
                    V.tensor_mul(prod[0:64, 0:tn], phq[0:64, 0:tn],
                                 phk[0:64, 0:tn])
                    step(V.drain())
                    begin(T)
                    step(T.matmul(pb[2][0:1, 0:tn], ones[0:64, 0:1],
                                  prod[0:64, 0:tn], start=True, stop=True))
                    begin(V)
                    V.tensor_copy(srow[0:1, 0:tn], pb[2][0:1, 0:tn])
                    V.drain()
                    V.reciprocal(rsrow[0:1, 0:tn], srow[0:1, 0:tn])
                    step(V.drain())
                    begin(T)
                    T.matmul(pb[3][hb:hb + 64, 0:tn], ones[0:1, 0:64],
                             srow[0:1, 0:tn], start=True, stop=True)
                    T.drain()
                    T.matmul(pb[4][hb:hb + 64, 0:tn], ones[0:1, 0:64],
                             rsrow[0:1, 0:tn], start=True, stop=True)
                    step(T.drain())
                    begin(V)
                    V.tensor_mul(tv1[hsl, 0:tn],
                                 vf1_f[hsl, hcol:hcol + tn],
                                 pb[3][hsl, 0:tn])
                    V.drain()
                    V.tensor_mul(tv1[hsl, 0:tn], tv1[hsl, 0:tn],
                                 pb[4][hsl, 0:tn])
                    V.drain()
                    if fcol is None:
                        V.tensor_add(xs[hsl, h // 2, t0:t0 + tn],
                                     xs[hsl, h // 2, t0:t0 + tn], tv1[hsl, 0:tn])
                    else:
                        V.scalar_tensor_tensor(xs[hsl, h // 2, t0:t0 + tn],
                                               tv1[hsl, 0:tn],
                                               stats[hsl, fcol:fcol + 1],
                                               xs[hsl, h // 2, t0:t0 + tn],
                                               ALU.mult, ALU.add)
                    step(V.drain())
                emit_ln(t0, tn)

        if "attn1" in taps:
            begin(Sy)
            dma(dbg["attn1"], xs[:, :, :])
            sy_step()

        C["mark_P6"] = (C["ch"], C["sd"])
        ari = sbt("ari", [128, 512], I32)
        hb7 = arv(5648, [NCH1, 322])
        xq7 = arv(0, [NCH, 322])
        hb7_f = hb7.rearrange("p a b -> p (a b)")
        xq7_f = xq7.rearrange("p a b -> p (a b)")

        def act_quant_tm(buf, n, maxcol, rscol):
            begin(V)
            V.tensor_scalar(FB2[:, 0:n], buf[:, 0:n],
                            stats[:, maxcol + 1:maxcol + 2], 0.5,
                            ALU.mult, ALU.add)
            V.drain()
            V.tensor_copy(ari[:, 0:n], FB2[:, 0:n])
            V.drain()
            V.tensor_copy(FB3[:, 0:n], ari[:, 0:n])
            V.drain()
            V.tensor_tensor(FB4[:, 0:n], FB3[:, 0:n], FB2[:, 0:n], ALU.is_gt)
            V.drain()
            V.tensor_sub(FB3[:, 0:n], FB3[:, 0:n], FB4[:, 0:n])
            V.drain()
            V.tensor_scalar(FB3[:, 0:n], FB3[:, 0:n], -128.0, 127.0,
                            ALU.max, ALU.min)
            V.drain()
            V.tensor_scalar(buf[:, 0:n], FB3[:, 0:n],
                            stats[:, rscol:rscol + 1], None, ALU.mult)
            step(V.drain())

        def xq_l1(t0c, tnc, dst_f, W_chunks, maxcol, rscol):
            for s0 in range(0, tnc, 128):
                w = min(128, tnc - s0)
                begin(V)
                V.memset(stats[:, maxcol:maxcol + 1], 0.0)
                step(V.drain())
                for fh in range(W_chunks // 4):
                    begin(T)
                    for i in range(4):
                        c = fh * 4 + i
                        T.transpose(pb[i][0:w, 0:128],
                                    xsf[:, c * TT + t0c + s0: c * TT + t0c + s0 + w],
                                    ident[:, :])
                    step(T.drain())
                    begin(V)
                    for i in range(4):
                        evac_tp(XT[:, i * 128:(i + 1) * 128], i, w)
                    step(V.drain())
                    begin(V)
                    V.tensor_reduce(stats[:, maxcol + 2:maxcol + 3],
                                    XT[:, 0:512], axis=AX.X, op=ALU.max,
                                    apply_absolute_value=True)
                    V.drain()
                    V.tensor_max(stats[:, maxcol:maxcol + 1],
                                 stats[:, maxcol:maxcol + 1],
                                 stats[:, maxcol + 2:maxcol + 3])
                    step(V.drain())
                begin(V)
                V.tensor_scalar(stats[:, maxcol:maxcol + 1],
                                stats[:, maxcol:maxcol + 1], 1e-5, None, ALU.max)
                V.drain()
                V.reciprocal(stats[:, maxcol + 1:maxcol + 2],
                             stats[:, maxcol:maxcol + 1])
                V.drain()
                V.tensor_scalar_mul(stats[:, maxcol + 1:maxcol + 2],
                                    stats[:, maxcol + 1:maxcol + 2], 127.0)
                V.drain()
                V.tensor_scalar_mul(stats[:, rscol:rscol + 1],
                                    stats[:, maxcol:maxcol + 1], 1.0 / 127.0)
                step(V.drain())
                for fh in range(W_chunks // 4):
                    begin(T)
                    for i in range(4):
                        c = fh * 4 + i
                        T.transpose(pb[i][0:w, 0:128],
                                    xsf[:, c * TT + t0c + s0: c * TT + t0c + s0 + w],
                                    ident[:, :])
                    step(T.drain())
                    begin(V)
                    for i in range(4):
                        evac_tp(XT[:, i * 128:(i + 1) * 128], i, w)
                    step(V.drain())
                    act_quant_tm(XT, 512, maxcol, rscol)
                    begin(T)
                    for i in range(4):
                        T.transpose(pb[i][:, 0:128],
                                    XT[:, i * 128:(i + 1) * 128], ident[:, :])
                    step(T.drain())
                    begin(V)
                    for i in range(4):
                        c = fh * 4 + i
                        V.tensor_copy(dst_f[:, c * 322 + s0: c * 322 + s0 + w],
                                      pb[i][0:128, 0:w])
                    step(V.drain())

        if _PHASES >= 7:
            SN7 = 320
            tiles7 = []
            t = 1
            while t < 1 + TC:
                n = min(SN7, 1 + TC - t)
                tiles7.append((t, n))
                t += n
            for (t0, tn) in tiles7:
                hn = tn + 2
                xq_l1(t0 - 1, hn, xq7_f, NCH, 50, 52)
                begin(Sy)
                dsW.issue(0, [(wb[:, :].rearrange("p (a b) -> p a b", b=128),
                               w1T[:, :, 0:128])])
                step(Sy.nop())
                for nch in range(NCH1):
                    nb = nch % 2
                    begin(Sy)
                    dsW.wait(nb)
                    if nch + 1 < NCH1:
                        nbuf = [wb, mst0][1 - nb]
                        dsW.issue(1 - nb,
                                  [(nbuf[:, :].rearrange("p (a b) -> p a b", b=128),
                                    w1T[:, :, (nch + 1) * 128:(nch + 2) * 128])])
                    step(Sy.nop())
                    wcur = [wb, mst0][nb]
                    begin(T)
                    for k in range(NCH):
                        T.matmul(pb[4][:, 0:hn], wcur[:, k * 128:(k + 1) * 128],
                                 xq7_f[:, k * 322:k * 322 + hn],
                                 start=(k == 0), stop=(k == NCH - 1))
                    step(T.drain())
                    begin(V)
                    V.tensor_scalar(FB2[:, 0:tn], pb[4][:, 0:tn],
                                    convw[:, nch, 0:1], None, ALU.mult)
                    V.drain()
                    V.scalar_tensor_tensor(FB2[:, 0:tn], pb[4][:, 1:1 + tn],
                                           convw[:, nch, 1:2], FB2[:, 0:tn],
                                           ALU.mult, ALU.add)
                    V.drain()
                    V.scalar_tensor_tensor(FB2[:, 0:tn], pb[4][:, 2:2 + tn],
                                           convw[:, nch, 2:3], FB2[:, 0:tn],
                                           ALU.mult, ALU.add)
                    step(V.drain())
                    begin(A)
                    A.activation(hb7_f[:, nch * 322:nch * 322 + tn],
                                 FB2[:, 0:tn], AF.Silu,
                                 bias=convb[:, nch:nch + 1])
                    step(A.drain())
                for s0 in range(0, tn, 128):
                    w = min(128, tn - s0)
                    begin(V)
                    V.memset(stats[:, 54:55], 0.0)
                    step(V.drain())
                    for fh in range(NCH1 // 4):
                        begin(T)
                        for i in range(4):
                            c = fh * 4 + i
                            T.transpose(pb[i][0:w, 0:128],
                                        hb7_f[:, c * 322 + s0: c * 322 + s0 + w],
                                        ident[:, :])
                        step(T.drain())
                        begin(V)
                        for i in range(4):
                            evac_tp(XT[:, i * 128:(i + 1) * 128], i, w)
                        step(V.drain())
                        begin(V)
                        V.tensor_reduce(stats[:, 56:57], XT[:, 0:512],
                                        axis=AX.X, op=ALU.max,
                                        apply_absolute_value=True)
                        V.drain()
                        V.tensor_max(stats[:, 54:55], stats[:, 54:55],
                                     stats[:, 56:57])
                        step(V.drain())
                    begin(V)
                    V.tensor_scalar(stats[:, 54:55], stats[:, 54:55], 1e-5,
                                    None, ALU.max)
                    V.drain()
                    V.reciprocal(stats[:, 55:56], stats[:, 54:55])
                    V.drain()
                    V.tensor_scalar_mul(stats[:, 55:56], stats[:, 55:56], 127.0)
                    V.drain()
                    V.tensor_scalar_mul(stats[:, 57:58], stats[:, 54:55],
                                        1.0 / 127.0)
                    step(V.drain())
                    for fh in range(NCH1 // 4):
                        begin(T)
                        for i in range(4):
                            c = fh * 4 + i
                            T.transpose(pb[i][0:w, 0:128],
                                        hb7_f[:, c * 322 + s0: c * 322 + s0 + w],
                                        ident[:, :])
                        step(T.drain())
                        begin(V)
                        for i in range(4):
                            evac_tp(XT[:, i * 128:(i + 1) * 128], i, w)
                        step(V.drain())
                        act_quant_tm(XT, 512, 54, 57)
                        begin(T)
                        for i in range(4):
                            T.transpose(pb[i][:, 0:128],
                                        XT[:, i * 128:(i + 1) * 128], ident[:, :])
                        step(T.drain())
                        begin(V)
                        for i in range(4):
                            c = fh * 4 + i
                            V.tensor_copy(hb7_f[:, c * 322 + s0: c * 322 + s0 + w],
                                          pb[i][0:128, 0:w])
                        step(V.drain())
                w2groups = [(m, kg) for m in range(NCH) for kg in range(0, NCH1, 8)]
                begin(Sy)
                m0_, kg0 = w2groups[0]
                dsW.issue(0, [(wb[:, :].rearrange("p (a b) -> p a b", b=128),
                               w2T[:, kg0:kg0 + 8, m0_ * 128:(m0_ + 1) * 128])])
                step(Sy.nop())
                for gi, (m, kg) in enumerate(w2groups):
                    gb = gi % 2
                    begin(Sy)
                    dsW.wait(gb)
                    if gi + 1 < len(w2groups):
                        m2, kg2 = w2groups[gi + 1]
                        nbuf = [wb, mst0][1 - gb]
                        dsW.issue(1 - gb,
                                  [(nbuf[:, :].rearrange("p (a b) -> p a b", b=128),
                                    w2T[:, kg2:kg2 + 8, m2 * 128:(m2 + 1) * 128])])
                    step(Sy.nop())
                    cbuf = [wb, mst0][gb]
                    begin(T)
                    for i in range(8):
                        T.matmul(pb[m][:, 0:tn],
                                 cbuf[:, i * 128:(i + 1) * 128],
                                 hb7_f[:, (kg + i) * 322:(kg + i) * 322 + tn],
                                 start=(kg + i == 0),
                                 stop=(kg + i == NCH1 - 1))
                    step(T.drain())
                for m in range(NCH):
                    begin(V)
                    step(V.tensor_add(xs[:, m, t0:t0 + tn],
                                      xs[:, m, t0:t0 + tn], pb[m][:, 0:tn]))
                emit_ln(t0, tn)

        begin(Sy)
        dma(y, xs[:, :, 1:1 + TC])
        sy_step()
        begin(Sy)
        step(Sy.nop())
        build_nc.counters = dict(C)
    return nc


def kernel(**inputs):
    cores = []
    x = np.asarray(inputs["L0_attn_x"])
    for c in range(8):
        b, half = c // 2, c % 2
        t0 = half * TC
        xs_ = np.zeros((D, TT), dtype=np.float32)
        xs_[:, 1:1 + TC] = x[b, t0:t0 + TC, :].T
        if half == 0:
            xs_[:, TT - 1] = x[b, t0 + TC, :]
        else:
            xs_[:, 0] = x[b, t0 - 1, :]
        xpre_ = np.ascontiguousarray(x[b, 0:TC, :].T)
        fM = 1.0 if half == 1 else 0.0
        fL = 1.0 if half == 1 else 0.0
        fR = 1.0 if half == 0 else 0.0
        im = {
            "xf": xs_.reshape(NCH, 128, TT).transpose(1, 0, 2).copy(),
            "xpre": xpre_.reshape(NCH, 128, TC).transpose(1, 0, 2).copy(),
            "flags": np.array([[fM, fL, fR, 0.0]], dtype=np.float32),
        }
        for nm in ["L0_attn_Wq", "L0_attn_Wk", "L0_attn_Wv", "L0_attn_Wo",
                   "L0_ffn_w_gate", "L0_ffn_w_up", "L0_ffn_w_down",
                   "L1_attn_Wq", "L1_attn_Wk", "L1_attn_Wv",
                   "L1_ffn_w1", "L1_ffn_w2"]:
            im[nm] = np.ascontiguousarray(np.asarray(inputs[nm], dtype=np.float32))
        im["L0_attn_betas"] = np.asarray(inputs["L0_attn_betas"], np.float32).reshape(H, DV)
        im["L0_attn_memory0"] = np.asarray(inputs["L0_attn_memory0"], np.float32).reshape(H, DK, DV)
        im["L0_attn_z0"] = np.asarray(inputs["L0_attn_z0"], np.float32).reshape(H, DK)
        im["L1_attn_bq"] = np.asarray(inputs["L1_attn_bq"], np.float32).reshape(1, D)
        im["L1_attn_bk"] = np.asarray(inputs["L1_attn_bk"], np.float32).reshape(1, D)
        im["L1_attn_bv"] = np.asarray(inputs["L1_attn_bv"], np.float32).reshape(1, D)
        im["L1_attn_weights"] = np.asarray(inputs["L1_attn_weights"], np.float32)
        im["L1_ffn_conv_w"] = np.asarray(inputs["L1_ffn_conv_w"], np.float32).reshape(DFF1, 3)
        im["L1_ffn_conv_b"] = np.asarray(inputs["L1_ffn_conv_b"], np.float32).reshape(1, DFF1)
        cores.append(im)
    nc = build_nc(taps=_DEBUG_TAPS)
    res = run_bass_kernel_spmd(nc, cores, core_ids=list(range(8)))
    kernel._last = res
    out = np.empty((B, S, D), dtype=np.float32)
    for c in range(8):
        b, half = c // 2, c % 2
        yc = np.asarray(res.results[c]["y"])
        out[b, half * TC:(half + 1) * TC, :] = \
            yc.transpose(1, 0, 2).reshape(D, TC).T
    return out


# version 10 (final)
# speedup vs baseline: 1.0351x; 1.0351x over previous
import numpy as np
from contextlib import ExitStack

import concourse.bass as bass
import concourse.mybir as mybir
from concourse.bass_types import AP
from concourse.bass_utils import run_bass_kernel_spmd

F32 = mybir.dt.float32
I32 = mybir.dt.int32
BF16 = mybir.dt.bfloat16
AF = mybir.ActivationFunctionType
ALU = mybir.AluOpType
AX = mybir.AxisListType

B, S, D = 4, 8192, 1024
H, DK, DV, LSEG = 16, 64, 64, 512
TC = 4096
TT = TC + 2
NSEG = TC // LSEG
NCH = D // 128
DFF0, NCH0 = 2816, 22
DFF1, NCH1 = 4096, 32
SC0 = 1.0 / 8.0
NEG = -1.0e30

_DEBUG_TAPS = ()
_PHASES = 99
_P2N = 128
_P2MODE = 'full'
_P3N = 512
_P3MODE = 'full'
_P3TAIL = True
_P1M = 8
_P4MODE = 'full'
_DUMPS = ()
_GHOSTS = True


def _ttiles(n, step, start=0):
    out = []
    t = start
    while t < n:
        out.append((t, min(step, n - t)))
        t += step
    return out


def build_nc(taps=()):
    nc = bass.Bass("TRN2", target_bir_lowering=False)
    T, V, A, G, Sy = nc.tensor, nc.vector, nc.scalar, nc.gpsimd, nc.sync

    xf = nc.dram_tensor("xf", [128, NCH, TT], F32, kind="ExternalInput").ap()
    xpre = nc.dram_tensor("xpre", [128, NCH, TC], F32, kind="ExternalInput").ap()
    flags = nc.dram_tensor("flags", [1, 4], F32, kind="ExternalInput").ap()
    wi = {}
    for nm, shp in [
        ("L0_attn_Wq", [D, D]), ("L0_attn_Wk", [D, D]), ("L0_attn_Wv", [D, D]),
        ("L0_attn_Wo", [D, D]), ("L0_attn_betas", [H, DV]),
        ("L0_attn_memory0", [H, DK, DV]), ("L0_attn_z0", [H, DK]),
        ("L0_ffn_w_gate", [DFF0, D]), ("L0_ffn_w_up", [DFF0, D]),
        ("L0_ffn_w_down", [D, DFF0]),
        ("L1_attn_Wq", [D, D]), ("L1_attn_bq", [1, D]),
        ("L1_attn_Wk", [D, D]), ("L1_attn_bk", [1, D]),
        ("L1_attn_Wv", [D, D]), ("L1_attn_bv", [1, D]),
        ("L1_attn_weights", [H, DK, DK]),
        ("L1_ffn_w1", [DFF1, D]), ("L1_ffn_conv_w", [DFF1, 3]),
        ("L1_ffn_conv_b", [1, DFF1]), ("L1_ffn_w2", [D, DFF1]),
    ]:
        wi[nm] = nc.dram_tensor(nm, shp, F32, kind="ExternalInput").ap()
    y = nc.dram_tensor("y", [128, NCH, TC], F32, kind="ExternalOutput").ap()
    dbg = {}
    for t_ in taps:
        dbg[t_] = nc.dram_tensor("dbg_" + t_, [128, NCH, TT], F32,
                                 kind="ExternalOutput").ap()
    dmp = {}
    for t_ in _DUMPS:
        dmp[t_] = nc.dram_tensor("dmp_" + t_, [128, 16256], F32,
                                 kind="ExternalOutput").ap()

    qdr = nc.dram_tensor("qdr", [128, NCH, TT], F32).ap()
    kdr = nc.dram_tensor("kdr", [128, NCH, TT], F32).ap()
    vdr = nc.dram_tensor("vdr", [128, NCH, TT], F32).ap()
    kpre = nc.dram_tensor("kpre", [128, NCH, TC], F32).ap()
    vpre = nc.dram_tensor("vpre", [128, NCH, TC], F32).ap()
    wgT = nc.dram_tensor("wgT", [128, NCH, DFF0], F32).ap()
    wuT = nc.dram_tensor("wuT", [128, NCH, DFF0], F32).ap()
    wdT = nc.dram_tensor("wdT", [128, NCH0, D], F32).ap()
    w1T = nc.dram_tensor("w1T", [128, NCH, DFF1], F32).ap()
    w2T = nc.dram_tensor("w2T", [128, NCH1, D], F32).ap()
    wqT1 = nc.dram_tensor("wqT1", [128, NCH, D], F32).ap()
    wkT1 = nc.dram_tensor("wkT1", [128, NCH, D], F32).ap()
    wvT1 = nc.dram_tensor("wvT1", [128, NCH, D], F32).ap()

    with ExitStack() as st:
        def sbt(name, shape, dt=F32):
            return st.enter_context(nc.sbuf_tensor(name, shape, dt))

        xs = sbt("xs", [128, NCH, TT])
        AR = 16256
        ar = sbt("arena", [128, AR])
        wb = sbt("wb", [128, NCH * 128])
        wnat = sbt("wnat", [128, 512])
        ident = sbt("ident", [128, 128])
        ones = sbt("ones", [128, 128])
        biasq = sbt("biasq", [128, NCH])
        biask = sbt("biask", [128, NCH])
        biasv = sbt("biasv", [128, NCH])
        convw = sbt("convw", [128, NCH1, 3])
        convb = sbt("convb", [128, NCH1])
        gates = sbt("gates", [128, H])
        phiw = sbt("phiw", [128, NCH * DK])
        stats = sbt("stats", [128, 64])
        MS = 68
        mst = sbt("mst", [128, H * MS])
        pb = [st.enter_context(nc.psum_tensor("pb%d" % i, [128, 512], F32))
              for i in range(8)]

        xsf = xs[:, :, :].rearrange("p a b -> p (a b)")
        iicol = wnat

        CH = st.enter_context(nc.semaphore("ch"))
        SD = st.enter_context(nc.semaphore("sd"))
        C = {"ch": 0, "sd": 0}

        def dma(out, in_):
            Sy.dma_start(out=out, in_=in_).then_inc(SD, 16)
            C["sd"] += 16

        def dmas(out, in_):
            Sy.dma_start(out=out, in_=in_,
                         allow_slow_non_contiguous=True).then_inc(SD, 16)
            C["sd"] += 16

        def step(instr):
            instr.then_inc(CH, 1)
            C["ch"] += 1

        def begin(eng):
            eng.wait_ge(CH, C["ch"])

        def sy_step():
            Sy.wait_ge(SD, C["sd"])
            step(Sy.nop())

        class DStream:
            def __init__(self, name):
                self.name = name
                self.sems = [st.enter_context(nc.semaphore(name + "0")),
                             st.enter_context(nc.semaphore(name + "1"))]
                self.cnt = [0, 0]
                C.setdefault("streams", {})[name] = self

            def issue(self, b, pairs, slow=False):
                for (o, i_) in pairs:
                    Sy.dma_start(out=o, in_=i_,
                                 allow_slow_non_contiguous=slow).then_inc(
                        self.sems[b], 16)
                    self.cnt[b] += 16

            def wait(self, b, eng=None):
                (eng or Sy).wait_ge(self.sems[b], self.cnt[b])

        dsW = DStream("dsw")
        dsW2 = DStream("dsw2")
        dsH = DStream("dsh")
        dsX = DStream("dsx")
        dsP = DStream("dsp")
        dsS = DStream("dss")

        def arv(off, shape):
            n = int(np.prod(shape))
            assert off + n <= AR, (off, shape)
            a = ar[:, off:off + n]
            if len(shape) == 1:
                return a
            if len(shape) == 2:
                return a.rearrange("p (a b) -> p a b", b=shape[1])
            return a.rearrange("p (a b c) -> p a b c", b=shape[1], c=shape[2])

        qsg = arv(0, [256])
        ksg = arv(256, [256])
        kmsg = arv(512, [256])
        vsg = arv(768, [256])
        ktm = arv(1024, [256])
        VS = 68
        vtm = arv(1280, [4 * 68])
        sktm = arv(1552, [256])
        sq_ = arv(1808, [256])
        psb = arv(2064, [4, 512])
        ptsb = arv(4112, [512])
        attd = arv(4624, [512])
        nums = arv(5136, [512])
        stg = arv(5648, [256])
        denr = arv(5904, [512])
        ghatt = arv(6416, [NCH, 2])
        attfm = arv(6432, [NCH, 512])
        mask = arv(10528, [4, 512])
        wb2 = arv(14848, [NCH * 128])
        wb3 = arv(12576, [NCH * 128])
        wb4 = arv(13600, [NCH * 128])
        sgB = arv(12576, [1024])
        qsg2 = sgB[:, 0:256]
        ksg2 = sgB[:, 256:512]
        kmsg2 = sgB[:, 512:768]
        vsg2 = sgB[:, 768:1024]
        pstg = arv(0, [512])
        pstg2 = arv(8704, [512])
        xpre_sb = arv(512, [NCH, 512])
        xpre_sb2 = arv(4608, [NCH, 512])
        m0tmp = arv(0, [H, 68])
        attfm_f = attfm.rearrange("p a b -> p (a b)")
        lnrows = arv(4112, [1536])
        sqb = psb.rearrange("p a b -> p (a b)")

        begin(Sy)
        dma(xs[:, :, :], xf)
        dmas(gates[0:64, 0:H], AP(wi["L0_attn_betas"].tensor, 0, [[1, DV], [DV, H]]))
        dmas(biasq[:, 0:NCH], AP(wi["L1_attn_bq"].tensor, 0, [[1, 128], [128, NCH]]))
        dmas(biask[:, 0:NCH], AP(wi["L1_attn_bk"].tensor, 0, [[1, 128], [128, NCH]]))
        dmas(biasv[:, 0:NCH], AP(wi["L1_attn_bv"].tensor, 0, [[1, 128], [128, NCH]]))
        dmas(convw[:, :, :], AP(wi["L1_ffn_conv_w"].tensor, 0,
                                [[3, 128], [3 * 128, NCH1], [1, 3]]))
        dmas(convb[:, 0:NCH1], AP(wi["L1_ffn_conv_b"].tensor, 0,
                                  [[1, 128], [128, NCH1]]))
        dma(stats[0:1, 0:4], flags)
        sy_step()
        begin(V)
        V.memset(stats[:, 13:14], 1e-5)
        step(V.drain())
        begin(A)
        step(A.activation(gates[0:64, 0:H], gates[0:64, 0:H], AF.Sigmoid))

        begin(G)
        G.iota(iicol[:, 0:128], pattern=[[1, 128]], base=0, channel_multiplier=-1,
               allow_small_or_imprecise_dtypes=True)
        step(G.drain())
        begin(V)
        V.tensor_single_scalar(ident[:, :], iicol[:, 0:128], 0.0, ALU.is_equal)
        V.memset(ones[:, :], 1.0)
        step(V.drain())
        begin(T)
        step(T.matmul(pb[0][:, 0:4], ones[0:1, 0:128], stats[0:1, 0:4],
                      start=True, stop=True))
        begin(V)
        V.tensor_copy(stats[:, 4:8], pb[0][:, 0:4])
        step(V.drain())

        begin(G)
        G.iota(iicol[:, 0:128].rearrange("p (a b) -> p a b", b=32),
               pattern=[[2, 4], [16, 32]], base=0, channel_multiplier=0,
               allow_small_or_imprecise_dtypes=True)
        step(G.drain())
        begin(T)
        step(T.transpose(pb[0][:, 0:128], iicol[:, 0:128], ident[:, :]))
        begin(V)
        V.tensor_copy(iicol[:, 0:1], pb[0][:, 0:1])
        step(V.drain())
        begin(G)
        G.iota(denr[:, :].rearrange("p (a b c) -> p a b c", b=8, c=32),
               pattern=[[1, 2], [2, 8], [16, 32]], base=0, channel_multiplier=0,
               allow_small_or_imprecise_dtypes=True)
        step(G.drain())
        begin(V)
        for blk in range(4):
            clo_i, ihalf = blk // 2, blk % 2
            off = float(8 * ihalf + clo_i)
            V.tensor_scalar(stats[:, 8 + blk:9 + blk], iicol[:, 0:1], off, None,
                            ALU.add)
        V.drain()
        for blk in range(4):
            V.scalar_tensor_tensor(mask[:, blk, :], denr[:, :],
                                   stats[:, 8 + blk:9 + blk], denr[:, :],
                                   ALU.subtract, ALU.bypass)
        V.drain()
        for blk in range(4):
            V.tensor_scalar(mask[:, blk, :], mask[:, blk, :], 0.0, NEG,
                            ALU.is_gt, ALU.mult)
        step(V.drain())

        C["mark_P0"] = (C["ch"], C["sd"])
        main_tiles = _ttiles(TT, 512)
        pre_tiles = _ttiles(TC, 512)
        xpre_f = xpre_sb.rearrange("p a b -> p (a b)")

        def w_issue(b, whbm, m, nk=NCH, stream=None):
            s = stream if stream is not None else dsW
            buf = ([wb, wb2] if s is dsW else [wb3, wb4])[b]
            s.issue(b, [(buf[:, k * 128:(k + 1) * 128],
                         whbm[k * 128:(k + 1) * 128, m * 128:(m + 1) * 128])
                        for k in range(nk)])
            return buf

        def proj_pass(whbm, dst, src_is_stream, tiles):
            xpre_bufs = [xpre_sb, xpre_sb2]
            xpre_fs = [xpre_f, xpre_sb2.rearrange("p a b -> p (a b)")]
            begin(Sy)
            w_issue(0, whbm, 0)
            if not src_is_stream:
                dsX.issue(0, [(xpre_bufs[0][:, :, 0:tiles[0][1]],
                               xpre[:, :, tiles[0][0]:tiles[0][0] + tiles[0][1]])])
            step(Sy.nop())
            nt = len(tiles)
            for m in range(NCH):
                mb = m % 2
                begin(Sy)
                dsW.wait(mb)
                if m + 1 < NCH:
                    w_issue((m + 1) % 2, whbm, m + 1)
                step(Sy.nop())
                wcur = [wb, wb2][mb]
                for ti, (t0, tn) in enumerate(tiles):
                    gi = m * nt + ti
                    if not src_is_stream:
                        begin(Sy)
                        dsX.wait(gi % 2)
                        if gi + 1 < NCH * nt:
                            t0n, tnn = tiles[(ti + 1) % nt]
                            dsX.issue((gi + 1) % 2,
                                      [(xpre_bufs[(gi + 1) % 2][:, :, 0:tnn],
                                        xpre[:, :, t0n:t0n + tnn])])
                        step(Sy.nop())
                    begin(T)
                    for k in range(NCH):
                        rhs = (xsf[:, k * TT + t0: k * TT + t0 + tn]
                               if src_is_stream
                               else xpre_fs[gi % 2][:, k * 512: k * 512 + tn])
                        T.matmul(pb[0][:, 0:tn], wcur[:, k * 128:(k + 1) * 128],
                                 rhs, start=(k == 0), stop=(k == NCH - 1))
                    step(T.drain())
                    begin(V)
                    dsP.wait(gi % 2, eng=V)
                    V.tensor_copy([pstg, pstg2][gi % 2][:, 0:tn],
                                  pb[0][:, 0:tn])
                    step(V.drain())
                    begin(Sy)
                    dsP.issue(gi % 2,
                              [(dst[:, m, t0:t0 + tn],
                                [pstg, pstg2][gi % 2][:, 0:tn])])
                    step(Sy.nop())

        if _PHASES >= 1:
            proj_pass(wi["L0_attn_Wq"], qdr, True, main_tiles)
            proj_pass(wi["L0_attn_Wk"], kdr, True, main_tiles)
            proj_pass(wi["L0_attn_Wv"], vdr, True, main_tiles)
            proj_pass(wi["L0_attn_Wk"], kpre, False, pre_tiles)
            proj_pass(wi["L0_attn_Wv"], vpre, False, pre_tiles)

        C["mark_P1"] = (C["ch"], C["sd"])
        begin(V)
        V.memset(mst[:, :], 0.0)
        step(V.drain())

        def elu1(dst, src, tmp):
            begin(V)
            V.tensor_scalar_min(tmp, src, 0.0)
            step(V.drain())
            begin(A)
            A.activation(tmp, tmp, AF.Exp)
            step(A.drain())
            begin(V)
            V.scalar_tensor_tensor(dst, src, 0.0, tmp, ALU.max, ALU.add)
            step(V.drain())

        SGS = [(qsg, ksg, kmsg, vsg), (qsg2, ksg2, kmsg2, vsg2)]

        def head_kv_issue(b, ksrc, vsrc, qsrc, col0, with_q):
            qs, ks, km, vs = SGS[b]
            pairs = [(ks[:, :].rearrange("p (a b) -> p a b", b=32),
                      ksrc[:, :, col0:col0 + 32]),
                     (vs[:, :].rearrange("p (a b) -> p a b", b=32),
                      vsrc[:, :, col0:col0 + 32])]
            if with_q:
                pairs.append((qs[:, :].rearrange("p (a b) -> p a b", b=32),
                              qsrc[:, :, col0:col0 + 32]))
            dsH.issue(b, pairs)

        def head_kv_mirror(b):
            qs, ks, km, vs = SGS[b]
            dsH.issue(b, [(km[0:64, :], ks[64:128, :]),
                          (km[64:128, :], ks[0:64, :])])

        def head_tm(sg, with_sk=True, parts='all', pre_T=None):
            for rnd, (src, dstt, stridew) in enumerate(
                    [(sg[1], ktm, 64), (sg[3], vtm, VS)]):
                begin(T)
                if rnd == 0 and pre_T is not None:
                    pre_T()
                for L in range(4):
                    clo, g = L // 2, L % 2
                    idn = ident[64 * clo:64 * clo + 64, 64 * clo:64 * clo + 64]
                    T.transpose(pb[4 + L][:, 0:64],
                                src[64 * clo:64 * clo + 64,
                                    g * 128:(g + 1) * 128], idn)
                step(T.drain())
                begin(V)
                for L in range(4):
                    V.tensor_copy(dstt[:, L * stridew:L * stridew + 64],
                                  pb[4 + L][:, 0:64])
                if rnd == 1 and parts != 'tp':
                    V.memset(vtm.rearrange("p (a b) -> p a b", b=VS)[:, :, 64:65],
                             1.0)
                    if parts != 'ones' and with_sk:
                        V.tensor_scalar_min(denr[:, 0:256], ktm[:, :], 0.0)
                step(V.drain())
            if parts in ('tp', 'ones'):
                return
            if with_sk:
                begin(A)
                A.activation(denr[:, 0:256], denr[:, 0:256], AF.Exp)
                step(A.drain())
                begin(V)
                V.scalar_tensor_tensor(sktm[:, :], ktm[:, :], 0.0,
                                       denr[:, 0:256], ALU.max, ALU.add)
                step(V.drain())

        def dm_update(h, mfac=None):
            begin(T)
            for base in range(2):
                for L in range(4):
                    T.matmul(pb[7][64 * base:64 * base + 64, 0:DV + 1],
                             sktm[:, L * 64:(L + 1) * 64],
                             vtm[:, L * VS:L * VS + 65],
                             start=(L == 0), stop=(L == 3))
            step(T.drain())
            begin(V)
            for base in range(2):
                sl = slice(64 * base, 64 * base + 64)
                if mfac is None:
                    V.tensor_add(mst[sl, h * MS:h * MS + 65],
                                 mst[sl, h * MS:h * MS + 65],
                                 pb[7][sl, 0:DV + 1])
                else:
                    V.scalar_tensor_tensor(mst[sl, h * MS:h * MS + 65],
                                           pb[7][sl, 0:DV + 1], mfac[sl, :],
                                           mst[sl, h * MS:h * MS + 65],
                                           ALU.mult, ALU.add)
            step(V.drain())

        fM = stats[:, 4:5]
        begin(Sy)
        dsP.wait(0)
        dsP.wait(1)
        m0 = wi["L0_attn_memory0"]
        z0 = wi["L0_attn_z0"]
        dmas(m0tmp[0:64, :, 0:DK], AP(m0.tensor, 0, [[DV, 64], [DK * DV, H], [1, DV]]))
        dmas(m0tmp[64:128, :, 0:DK], AP(m0.tensor, 0, [[DV, 64], [DK * DV, H], [1, DV]]))
        dmas(m0tmp[0:64, :, DK:DK + 1], AP(z0.tensor, 0, [[1, 64], [DK, H]]))
        dmas(m0tmp[64:128, :, DK:DK + 1], AP(z0.tensor, 0, [[1, 64], [DK, H]]))
        sy_step()
        begin(V)
        V.tensor_add(mst[:, :], mst[:, :],
                     m0tmp[:, :, :].rearrange("p a b -> p (a b)"))
        step(V.drain())

        gq = arv(2064, [8])
        gk = arv(2072, [8])
        gv = arv(2080, [8])
        gkm = arv(2088, [8])
        gsq = arv(2096, [8])
        gnum = arv(2112, [16])
        gattd = arv(2128, [16])
        gden = arv(2144, [16])
        gstg = arv(2160, [64])
        gvt = arv(2224, [68])
        gptm = arv(2292, [64])
        gsc = arv(2356, [1024])
        gmk = arv(3380, [1024])
        ghatt_f = ghatt.rearrange("p a b -> p (a b)")

        def ghost_softmax(ncols):
            v2 = gsc[0:8, 0:2 * ncols].rearrange("p (a b) -> p a b", b=ncols)
            st2 = stats[0:8, 56:58]
            begin(V)
            V.tensor_reduce(st2[:, :, None], v2, axis=AX.X, op=ALU.max,
                            negate=True)
            step(V.drain())
            bca = AP(st2.tensor, st2.offset, [[64, 8], [1, 2], [0, ncols]])
            begin(V)
            V.tensor_tensor(v2, v2, bca, ALU.add)
            step(V.drain())
            begin(A)
            A.activation(gsc[0:8, 0:2 * ncols], gsc[0:8, 0:2 * ncols], AF.Exp,
                         scale=SC0)
            step(A.drain())
            begin(V)
            V.tensor_reduce(st2[:, :, None], v2, axis=AX.X, op=ALU.add)
            V.drain()
            V.reciprocal(st2, st2)
            step(V.drain())
            begin(V)
            V.tensor_tensor(v2, v2, bca, ALU.mult)
            step(V.drain())

        def ghost_mem_combine(h):
            begin(T)
            T.matmul(pb[7][0:DK + 1, 0:8], mst[0:64, h * MS:h * MS + 65],
                     gsq[0:64, 0:8], start=True, stop=True)
            T.drain()
            T.matmul(pb[7][0:DK + 1, 8:16], mst[64:128, h * MS:h * MS + 65],
                     gsq[64:128, 0:8], start=True, stop=True)
            step(T.drain())
            begin(V)
            V.reciprocal(gden[64:65, 0:16], pb[7][64:65, 0:16])
            step(V.drain())
            begin(T)
            step(T.matmul(pb[2][0:64, 0:16], ones[64:65, 0:64],
                          gden[64:65, 0:16], start=True, stop=True))
            begin(V)
            V.tensor_copy(gnum[0:64, 0:16], pb[7][0:64, 0:16])
            V.drain()
            V.tensor_mul(gnum[0:64, 0:16], gnum[0:64, 0:16], pb[2][0:64, 0:16])
            V.drain()
            V.tensor_sub(gnum[0:64, 0:16], gnum[0:64, 0:16], gattd[0:64, 0:16])
            V.drain()
            V.scalar_tensor_tensor(gnum[0:64, 0:16], gnum[0:64, 0:16],
                                   gates[0:64, h:h + 1], gattd[0:64, 0:16],
                                   ALU.mult, ALU.add)
            step(V.drain())

        def ghost_store(colidx):
            begin(V)
            V.tensor_copy(ghatt[0:64, :, colidx:colidx + 1],
                          gnum[0:64, 0:8][:, :, None])
            V.tensor_copy(gstg[0:64, 0:8], gnum[0:64, 8:16])
            step(V.drain())
            begin(Sy)
            dmas(ghatt[64:128, :, colidx:colidx + 1], gstg[0:64, 0:8][:, :, None])
            sy_step()

        def ghost_L(sg):
            begin(Sy)
            dmas(gq[:, 0:8], qdr[:, :, 0:1])
            sy_step()
            begin(T)
            for ci in range(2):
                lhsT = gq[64 * ci:64 * ci + 64, 0:8]
                for cj in range(2):
                    srct = sg[1] if cj == ci else sg[2]
                    T.matmul(pb[ci][0:8, cj * 256:cj * 256 + 256], lhsT,
                             srct[64 * ci:64 * ci + 64, 0:256],
                             start=(cj == 0), stop=(cj == 1))
            step(T.drain())
            begin(G)
            G.iota(gmk[:, :].rearrange("p (a b c d) -> p a b c d", b=2, c=8, d=32),
                   pattern=[[-1, 2], [1, 2], [2, 8], [16, 32]], base=-496,
                   channel_multiplier=-2, allow_small_or_imprecise_dtypes=True)
            step(G.drain())
            begin(V)
            V.tensor_scalar(gmk[:, :], gmk[:, :], 0.0, NEG, ALU.is_gt, ALU.mult)
            V.drain()
            for ci in range(2):
                V.tensor_add(gsc[0:8, ci * 512:(ci + 1) * 512],
                             pb[ci][0:8, 0:512], gmk[0:8, ci * 512:(ci + 1) * 512])
            step(V.drain())
            ghost_softmax(512)
            for ci in range(2):
                begin(T)
                for Jc in range(4):
                    T.transpose(pb[2 + Jc][0:128, 0:8],
                                gsc[0:8, ci * 512 + Jc * 128: ci * 512 + (Jc + 1) * 128],
                                ident[0:8, 0:8])
                step(T.drain())
                begin(V)
                for Jc in range(4):
                    V.tensor_copy(gptm[:, Jc * 16 + ci * 8: Jc * 16 + ci * 8 + 8],
                                  pb[2 + Jc][0:128, 0:8])
                step(V.drain())
            begin(T)
            for Jc in range(4):
                T.matmul(pb[6][0:DV, 0:16], vtm[:, Jc * VS:Jc * VS + 64],
                         gptm[:, Jc * 16:(Jc + 1) * 16],
                         start=(Jc == 0), stop=(Jc == 3))
            step(T.drain())
            begin(V)
            V.tensor_copy(gattd[0:64, 0:16], pb[6][0:DV, 0:16])
            step(V.drain())
            for clo in range(2):
                sl = slice(64 * clo, 64 * clo + 64)
                elu1(gsq[sl, 0:8], gq[sl, 0:8], denr[sl, 0:8])
            ghost_mem_combine(15)
            ghost_store(0)

        def ghost_R():
            begin(Sy)
            dmas(gq[:, 0:8], qdr[:, :, TT - 1:TT])
            dmas(gk[:, 0:8], kdr[:, :, TT - 1:TT])
            dmas(gv[:, 0:8], vdr[:, :, TT - 1:TT])
            sy_step()
            begin(Sy)
            dma(gkm[0:64, 0:8], gk[64:128, 0:8])
            dma(gkm[64:128, 0:8], gk[0:64, 0:8])
            sy_step()
            begin(T)
            for ci in range(2):
                lhsT = gq[64 * ci:64 * ci + 64, 0:8]
                for cj in range(2):
                    srct = gk if cj == ci else gkm
                    T.matmul(pb[ci][0:8, cj * 8:cj * 8 + 8], lhsT,
                             srct[64 * ci:64 * ci + 64, 0:8],
                             start=(cj == 0), stop=(cj == 1))
            step(T.drain())
            begin(G)
            G.iota(gmk[:, 0:32].rearrange("p (a b c) -> p a b c", b=2, c=8),
                   pattern=[[-1, 2], [1, 2], [2, 8]], base=0,
                   channel_multiplier=-2, allow_small_or_imprecise_dtypes=True)
            step(G.drain())
            begin(V)
            V.tensor_scalar(gmk[:, 0:32], gmk[:, 0:32], 0.0, NEG,
                            ALU.is_gt, ALU.mult)
            V.drain()
            for ci in range(2):
                V.tensor_add(gsc[0:8, ci * 16:(ci + 1) * 16],
                             pb[ci][0:8, 0:16], gmk[0:8, ci * 16:(ci + 1) * 16])
            step(V.drain())
            ghost_softmax(16)
            begin(T)
            T.transpose(pb[2][0:8, 0:64], gv[0:64, 0:8], ident[0:64, 0:64])
            T.drain()
            T.transpose(pb[3][0:8, 0:64], gv[64:128, 0:8], ident[64:128, 64:128])
            step(T.drain())
            begin(V)
            V.tensor_copy(gvt[0:8, 0:64], pb[2][0:8, 0:64])
            V.tensor_copy(gstg[0:8, 0:64], pb[3][0:8, 0:64])
            step(V.drain())
            begin(Sy)
            dma(gvt[8:16, 0:64], gstg[0:8, 0:64])
            sy_step()
            begin(T)
            T.transpose(pb[4][0:16, 0:8], gsc[0:8, 0:16], ident[0:8, 0:8])
            T.drain()
            T.transpose(pb[5][0:16, 0:8], gsc[0:8, 16:32], ident[0:8, 0:8])
            step(T.drain())
            begin(V)
            V.tensor_copy(gptm[0:16, 0:8], pb[4][0:16, 0:8])
            V.tensor_copy(gptm[0:16, 8:16], pb[5][0:16, 0:8])
            step(V.drain())
            begin(T)
            step(T.matmul(pb[6][0:DV, 0:16], gvt[0:16, 0:64],
                          gptm[0:16, 0:16], start=True, stop=True))
            begin(V)
            V.tensor_copy(gattd[0:64, 0:16], pb[6][0:DV, 0:16])
            step(V.drain())
            for clo in range(2):
                sl = slice(64 * clo, 64 * clo + 64)
                elu1(gsq[sl, 0:8], gq[sl, 0:8], denr[sl, 0:8])
            ghost_mem_combine(0)
            ghost_store(1)

        if _PHASES >= 2:
            heads2 = [(seg, h) for seg in range(NSEG) for h in range(H)]
            begin(Sy)
            head_kv_issue(0, kpre, vpre, None, heads2[0][0] * LSEG, False)
            step(Sy.nop())
            begin(Sy)
            dsH.wait(0)
            head_kv_mirror(0)
            step(Sy.nop())
            for i, (seg, h) in enumerate(heads2):
                b = i % 2
                begin(Sy)
                dsH.wait(b)
                if i + 1 < len(heads2):
                    s2, h2 = heads2[i + 1]
                    head_kv_issue(1 - b, kpre, vpre, None,
                                  s2 * LSEG + h2 * 32, False)
                step(Sy.nop())
                head_tm(SGS[b], with_sk=True)
                if _GHOSTS and seg == 7 and h == 15:
                    ghost_L(SGS[b])
                dm_update(h, mfac=fM)
                if i + 1 < len(heads2):
                    begin(Sy)
                    dsH.wait(1 - b)
                    head_kv_mirror(1 - b)
                    step(Sy.nop())

        C["mark_P2"] = (C["ch"], C["sd"])
        def softmax_block(blk):
            bank = pb[blk]
            begin(V)
            V.tensor_add(bank[:, :], bank[:, :], mask[:, blk, :])
            V.drain()
            V.reduce_max(stats[:, 16:17], bank[:, :], axis=AX.X, negate=True)
            V.drain()
            step(V.tensor_scalar_mul(stats[:, 17:18], stats[:, 16:17], SC0))
            begin(A)
            A.activation(psb[:, blk, :], bank[:, :], AF.Exp, scale=SC0,
                         bias=stats[:, 17:18], accum_out=stats[:, 18:19])
            A.drain()
            A.activation(stats[:, 19:20], stats[:, 18:19], AF.Ln)
            step(A.drain())
            begin(V)
            step(V.tensor_sub(stats[:, 20:21], stats[:, 17:18], stats[:, 19:20]))
            begin(A)
            A.activation(psb[:, blk, :], bank[:, :], AF.Exp, scale=SC0,
                         bias=stats[:, 20:21])
            step(A.drain())

        def softmax_all():
            begin(V)
            for blk in range(4):
                V.tensor_add(pb[blk][:, :], pb[blk][:, :], mask[:, blk, :])
            V.drain()
            for blk in range(4):
                V.reduce_max(stats[:, 16 + blk:17 + blk], pb[blk][:, :],
                             axis=AX.X, negate=True)
            V.drain()
            for blk in range(4):
                V.tensor_scalar_mul(stats[:, 20 + blk:21 + blk],
                                    stats[:, 16 + blk:17 + blk], SC0)
            step(V.drain())
            begin(A)
            for blk in range(4):
                A.activation(psb[:, blk, :], pb[blk][:, :], AF.Exp, scale=SC0,
                             bias=stats[:, 20 + blk:21 + blk],
                             accum_out=stats[:, 24 + blk:25 + blk])
                A.drain()
            for blk in range(4):
                A.activation(stats[:, 28 + blk:29 + blk],
                             stats[:, 24 + blk:25 + blk], AF.Ln)
            step(A.drain())
            begin(V)
            for blk in range(4):
                V.tensor_sub(stats[:, 32 + blk:33 + blk],
                             stats[:, 20 + blk:21 + blk],
                             stats[:, 28 + blk:29 + blk])
            step(V.drain())
            begin(A)
            for blk in range(4):
                A.activation(psb[:, blk, :], pb[blk][:, :], AF.Exp, scale=SC0,
                             bias=stats[:, 32 + blk:33 + blk])
            step(A.drain())

        def attn_head(sg, qtr, hq, h):
            qsg_, ksg_, kmsg_, vsg_ = sg
            mode = _P3MODE

            def scores_T():
                for blk in range(4):
                    clo_i, ihalf = blk // 2, blk % 2
                    lhsT = qsg_[64 * clo_i:64 * clo_i + 64,
                                ihalf * 128:(ihalf + 1) * 128]
                    for clo_j in range(2):
                        srct = ksg_ if clo_j == clo_i else kmsg_
                        rhs = srct[64 * clo_i:64 * clo_i + 64, 0:256]
                        T.matmul(pb[blk][:, clo_j * 256:clo_j * 256 + 256],
                                 lhsT, rhs, start=(clo_j == 0),
                                 stop=(clo_j == 1))

            head_tm(sg, with_sk=True, pre_T=scores_T)
            if mode == 'tm':
                return
            softmax_all()
            if mode == 'softmax':
                return
            begin(T)
            for Ib in range(4):
                T.transpose(pb[Ib][:, 0:128],
                            psb[:, Ib, 0:128], ident[:, :])
            step(T.drain())
            for Jc in range(4):
                begin(V)
                for Ib in range(4):
                    V.tensor_copy(ptsb[:, Ib * 128:(Ib + 1) * 128],
                                  pb[Ib][:, 0:128])
                step(V.drain())
                begin(T)
                T.matmul(pb[4][0:DV, 0:512], vtm[:, Jc * VS:Jc * VS + 64],
                         ptsb[:, 0:512], start=(Jc == 0), stop=(Jc == 3))
                if Jc < 3:
                    for Ib in range(4):
                        T.transpose(pb[Ib][:, 0:128],
                                    psb[:, Ib, (Jc + 1) * 128:(Jc + 2) * 128],
                                    ident[:, :])
                step(T.drain())
            begin(V)
            V.tensor_copy(attd[0:64, 0:512], pb[4][0:DV, 0:512])
            for clo in range(2):
                sl = slice(64 * clo, 64 * clo + 64)
                V.tensor_scalar_min(denr[sl, 0:256], qsg_[sl, :], 0.0)
            step(V.drain())
            if mode == 'pt':
                return
            begin(A)
            for clo in range(2):
                sl = slice(64 * clo, 64 * clo + 64)
                A.activation(denr[sl, 0:256], denr[sl, 0:256], AF.Exp)
            step(A.drain())
            begin(V)
            for clo in range(2):
                sl = slice(64 * clo, 64 * clo + 64)
                V.scalar_tensor_tensor(sq_[sl, :], qsg_[sl, :], 0.0,
                                       denr[sl, 0:256], ALU.max, ALU.add)
            step(V.drain())
            if mode == 'elu':
                return
            begin(T)
            T.matmul(pb[5][0:DK + 1, 0:256], mst[0:64, h * MS:h * MS + 65],
                     sq_[0:64, :], start=True, stop=True)
            T.drain()
            T.matmul(pb[5][0:DK + 1, 256:512], mst[64:128, h * MS:h * MS + 65],
                     sq_[64:128, :], start=True, stop=True)
            step(T.drain())
            begin(V)
            V.reciprocal(denr[64:65, 0:512], pb[5][64:65, 0:512])
            V.tensor_copy(nums[0:64, 0:512], pb[5][0:64, 0:512])
            step(V.drain())
            begin(T)
            step(T.matmul(pb[6][0:64, 0:512], ones[64:65, 0:64],
                          denr[64:65, 0:512], start=True, stop=True))
            if mode == 'mem':
                return
            begin(V)
            dsS.wait(0, eng=V)
            V.tensor_mul(nums[0:64, 0:512], nums[0:64, 0:512], pb[6][0:64, 0:512])
            V.drain()
            V.tensor_sub(nums[0:64, 0:512], nums[0:64, 0:512], attd[0:64, 0:512])
            V.drain()
            c0 = qtr * 128 + hq * 32
            V.scalar_tensor_tensor(
                attfm[0:64, :, c0:c0 + 32],
                nums[0:64, 0:256].rearrange("p (a b) -> p a b", b=32),
                gates[0:64, h:h + 1],
                attd[0:64, 0:256].rearrange("p (a b) -> p a b", b=32),
                ALU.mult, ALU.add)
            V.scalar_tensor_tensor(
                stg[0:64, 0:256], nums[0:64, 256:512], gates[0:64, h:h + 1],
                attd[0:64, 256:512], ALU.mult, ALU.add)
            step(V.drain())
            begin(Sy)
            dsS.issue(0, [(attfm[64:128, :, c0:c0 + 32],
                           stg[0:64, 0:256].rearrange("p (a b) -> p a b", b=32))])
            step(Sy.nop())
            dm_update(h)

        def emit_ln(t0, tn):
            for half in range(2):
                ca = 4 * half
                begin(V)
                for c in range(ca, ca + 4):
                    V.tensor_mul(sqb[:, (c - ca) * 512:(c - ca) * 512 + tn],
                                 xs[:, c, t0:t0 + tn], xs[:, c, t0:t0 + tn])
                step(V.drain())
                begin(T)
                for c in range(ca, ca + 4):
                    T.matmul(pb[6][0:1, 0:tn], ones[:, 0:1],
                             xsf[:, c * TT + t0: c * TT + t0 + tn],
                             start=(c == 0), stop=(c == 7))
                for c in range(ca, ca + 4):
                    T.matmul(pb[7][0:1, 0:tn], ones[:, 0:1],
                             sqb[:, (c - ca) * 512:(c - ca) * 512 + tn],
                             start=(c == 0), stop=(c == 7))
                step(T.drain())
            murow = lnrows[0:1, 0:512]
            varrow = lnrows[0:1, 512:1024]
            mrsrow = lnrows[0:1, 1024:1536]
            begin(V)
            V.tensor_scalar_mul(murow[:, 0:tn], pb[6][0:1, 0:tn], 1.0 / D)
            V.drain()
            V.tensor_mul(varrow[:, 0:tn], murow[:, 0:tn], murow[:, 0:tn])
            V.drain()
            V.scalar_tensor_tensor(varrow[:, 0:tn], pb[7][0:1, 0:tn], 1.0 / D,
                                   varrow[:, 0:tn], ALU.mult, ALU.subtract)
            step(V.drain())
            begin(A)
            A.activation(varrow[:, 0:tn], varrow[:, 0:tn], AF.Ln,
                         bias=stats[0:1, 13:14])
            A.drain()
            A.activation(varrow[:, 0:tn], varrow[:, 0:tn], AF.Exp, scale=-0.5)
            step(A.drain())
            begin(V)
            V.tensor_mul(mrsrow[:, 0:tn], murow[:, 0:tn], varrow[:, 0:tn])
            step(V.drain())
            begin(T)
            T.matmul(pb[4][:, 0:tn], ones[0:1, 0:128], varrow[:, 0:tn],
                     start=True, stop=True)
            T.matmul(pb[5][:, 0:tn], ones[0:1, 0:128], mrsrow[:, 0:tn],
                     start=True, stop=True)
            step(T.drain())
            begin(V)
            for c in range(NCH):
                V.tensor_mul(xs[:, c, t0:t0 + tn], xs[:, c, t0:t0 + tn],
                             pb[4][:, 0:tn])
            V.drain()
            for c in range(NCH):
                V.tensor_sub(xs[:, c, t0:t0 + tn], xs[:, c, t0:t0 + tn],
                             pb[5][:, 0:tn])
            step(V.drain())

        heads3 = ([(seg, h) for seg in range(NSEG) for h in range(H)]
                  if _PHASES >= 3 else [])
        if heads3:
            begin(Sy)
            head_kv_issue(0, kdr, vdr, qdr, 1 + heads3[0][0] * LSEG, True)
            step(Sy.nop())
            begin(Sy)
            dsH.wait(0)
            head_kv_mirror(0)
            step(Sy.nop())
        for i, (seg, h) in enumerate(heads3):
            qtr, hq = h // 4, h % 4
            b = i % 2
            begin(Sy)
            dsH.wait(b)
            if i + 1 < len(heads3):
                s2, h2 = heads3[i + 1]
                head_kv_issue(1 - b, kdr, vdr, qdr,
                              1 + s2 * LSEG + h2 * 32, True)
            step(Sy.nop())
            attn_head(SGS[b], qtr, hq, h)
            if i + 1 < len(heads3):
                begin(Sy)
                dsH.wait(1 - b)
                head_kv_mirror(1 - b)
                step(Sy.nop())
            if h != H - 1:
                continue
            segc = 1 + seg * LSEG
            begin(Sy)
            dsS.wait(0)
            w_issue(0, wi["L0_attn_Wo"], 0)
            step(Sy.nop())
            for m in range(NCH):
                begin(Sy)
                dsW.wait(m % 2)
                if m + 1 < NCH:
                    w_issue((m + 1) % 2, wi["L0_attn_Wo"], m + 1)
                step(Sy.nop())
                wcur = [wb, wb2][m % 2]
                begin(T)
                for k in range(NCH):
                    T.matmul(pb[0][:, 0:512], wcur[:, k * 128:(k + 1) * 128],
                             attfm_f[:, k * 512:(k + 1) * 512],
                             start=(k == 0), stop=(k == NCH - 1))
                step(T.drain())
                begin(V)
                step(V.tensor_add(xs[:, m, segc:segc + 512],
                                  xs[:, m, segc:segc + 512], pb[0][:, 0:512]))
            emit_ln(segc, 512)

        if _GHOSTS and _PHASES >= 3:
            ghost_R()
            for m in range(NCH):
                begin(Sy)
                for k in range(NCH):
                    dma(wb[:, k * 128:(k + 1) * 128],
                        wi["L0_attn_Wo"][k * 128:(k + 1) * 128,
                                         m * 128:(m + 1) * 128])
                sy_step()
                begin(T)
                for k in range(NCH):
                    T.matmul(pb[0][:, 0:2], wb[:, k * 128:(k + 1) * 128],
                             ghatt_f[:, k * 2:(k + 1) * 2],
                             start=(k == 0), stop=(k == NCH - 1))
                step(T.drain())
                begin(V)
                V.scalar_tensor_tensor(xs[:, m, 0:1], pb[0][:, 0:1],
                                       stats[:, 5:6], xs[:, m, 0:1],
                                       ALU.mult, ALU.add)
                V.drain()
                V.scalar_tensor_tensor(xs[:, m, TT - 1:TT], pb[0][:, 1:2],
                                       stats[:, 6:7], xs[:, m, TT - 1:TT],
                                       ALU.mult, ALU.add)
                step(V.drain())
            emit_ln(0, 1)
            emit_ln(TT - 1, 1)

        if "l0attn" in taps:
            begin(Sy)
            dma(dbg["l0attn"], xs[:, :, :])
            sy_step()

        C["mark_P3"] = (C["ch"], C["sd"])
        wpA = arv(0, [512])
        wpB = arv(512, [512])
        wpC = arv(1024, [512])
        wpD = arv(1536, [512])
        wpS = arv(2048, [512])
        wpM = arv(2560, [128])

        MID = [0.25, 0.75, 1.25, 1.75, 2.5, 3.5, 5.0]

        def fp4_stair(asrc, chain, chain2, nb16, wide):
            begin(V)
            V.tensor_scalar(chain, asrc, MID[0], None, ALU.is_gt)
            V.drain()
            for j in (1, 2, 3):
                V.scalar_tensor_tensor(chain, asrc, MID[j], chain,
                                       ALU.is_gt, ALU.add)
                V.drain()
            V.tensor_scalar(chain2, asrc, MID[4], None, ALU.is_gt)
            V.drain()
            V.scalar_tensor_tensor(chain2, asrc, MID[5], chain2,
                                   ALU.is_gt, ALU.add)
            V.drain()
            V.tensor_scalar(asrc, asrc, MID[6], 4.0, ALU.is_gt, ALU.mult)
            V.drain()
            V.scalar_tensor_tensor(chain2, chain2, 2.0, asrc, ALU.mult, ALU.add)
            V.drain()
            V.tensor_add(chain, chain, chain2)
            step(V.drain())

        def fp4_quant_nat(w_ap, nb16):
            n = nb16 * 16
            bm = wpM[:, 0:nb16]
            sh = wpM[:, 32:32 + nb16]
            rr = wpM[:, 64:64 + nb16]
            begin(V)
            V.tensor_reduce(bm[:, :, None], w_ap.rearrange("p (a b) -> p a b", b=16),
                            axis=AX.X, op=ALU.max, apply_absolute_value=True)
            V.drain()
            V.tensor_scalar(sh, bm, 1.0 / 6.0, 1e-12, ALU.mult, ALU.max)
            V.drain()
            V.reciprocal(rr, sh)
            V.drain()
            V.tensor_scalar_mul(sh, sh, 0.5)
            step(V.drain())
            rb = AP(rr.tensor, rr.offset, [[AR, 128], [1, nb16], [0, 16]])
            sb_ = AP(sh.tensor, sh.offset, [[AR, 128], [1, nb16], [0, 16]])
            begin(V)
            V.tensor_tensor(wpA[:, 0:n].rearrange("p (a b) -> p a b", b=16),
                            w_ap.rearrange("p (a b) -> p a b", b=16), rb, ALU.mult)
            step(V.drain())
            begin(A)
            A.activation(wpB[:, 0:n], wpA[:, 0:n], AF.Sign)
            A.drain()
            A.activation(wpA[:, 0:n], wpA[:, 0:n], AF.Abs)
            step(A.drain())
            fp4_stair(wpA[:, 0:n], wpC[:, 0:n], wpD[:, 0:n], nb16, n)
            begin(V)
            V.tensor_tensor(wpC[:, 0:n].rearrange("p (a b) -> p a b", b=16),
                            wpC[:, 0:n].rearrange("p (a b) -> p a b", b=16),
                            sb_, ALU.mult)
            V.drain()
            V.tensor_mul(wpC[:, 0:n], wpC[:, 0:n], wpB[:, 0:n])
            step(V.drain())

        def prep_store_T(src_ap, dstT, n0, kglob0, nk):
            for g0 in range(0, nk, 4):
                gn = min(4, nk - g0)
                begin(T)
                for i in range(gn):
                    T.transpose(pb[i][:, 0:128],
                                src_ap[:, (g0 + i) * 128:(g0 + i + 1) * 128],
                                ident[:, :])
                step(T.drain())
                begin(V)
                for i in range(gn):
                    V.tensor_copy(wpS[:, i * 128:(i + 1) * 128], pb[i][:, 0:128])
                step(V.drain())
                begin(Sy)
                dma(dstT[:, kglob0 + g0:kglob0 + g0 + gn, n0:n0 + 128],
                    wpS[:, 0:gn * 128].rearrange("p (a b) -> p a b", b=128))
                sy_step()

        WNB = None

        def prep_tiles(NW, KW):
            return [(n0, k0, min(512, KW - k0))
                    for n0 in range(0, NW, 128) for k0 in range(0, KW, 512)]

        def prep_loop(whbm, tiles, body):
            bufs = [wnat, wb3[:, 0:512]]
            begin(Sy)
            n0, k0, kn = tiles[0]
            dsX.issue(0, [(bufs[0][:, 0:kn], whbm[n0:n0 + 128, k0:k0 + kn])])
            step(Sy.nop())
            for i, (n0, k0, kn) in enumerate(tiles):
                b = i % 2
                begin(Sy)
                dsX.wait(b)
                if i + 1 < len(tiles):
                    n2, k2, kn2 = tiles[i + 1]
                    dsX.issue(1 - b, [(bufs[1 - b][:, 0:kn2],
                                       whbm[n2:n2 + 128, k2:k2 + kn2])])
                step(Sy.nop())
                body(bufs[b], n0, k0, kn)

        def prep_fp4_weight(whbm, dstT, NW, KW):
            def body(buf, n0, k0, kn):
                fp4_quant_nat(buf[:, 0:kn], kn // 16)
                prep_store_T(wpC, dstT, n0, k0 // 128, kn // 128)
            prep_loop(whbm, prep_tiles(NW, KW), body)

        def prep_plain_weight(whbm, dstT, NW, KW):
            def body(buf, n0, k0, kn):
                begin(V)
                V.tensor_copy(wpC[:, 0:kn], buf[:, 0:kn])
                step(V.drain())
                prep_store_T(wpC, dstT, n0, k0 // 128, kn // 128)
            prep_loop(whbm, prep_tiles(NW, KW), body)

        def prep_bit_weight(whbm, dstT, NW, KW, scol):
            acc = stats[:, 40:41]
            tmp = stats[:, 41:42]
            begin(V)
            V.memset(acc, 0.0)
            step(V.drain())
            def body1(buf, n0, k0, kn):
                begin(V)
                V.tensor_reduce(tmp, buf[:, 0:kn], axis=AX.X, op=ALU.add,
                                apply_absolute_value=True)
                V.drain()
                V.tensor_add(acc, acc, tmp)
                step(V.drain())
            prep_loop(whbm, prep_tiles(NW, KW), body1)
            begin(T)
            step(T.matmul(pb[0][0:1, 0:1], ones[:, 0:1], acc, start=True, stop=True))
            begin(V)
            V.tensor_scalar(stats[0:1, 43:44], pb[0][0:1, 0:1], 1.0 / (NW * KW),
                            1e-5, ALU.mult, ALU.max)
            V.drain()
            V.reciprocal(stats[0:1, 44:45], stats[0:1, 43:44])
            V.drain()
            V.tensor_scalar_mul(stats[0:1, 45:46], stats[0:1, 43:44], -1.0)
            step(V.drain())
            begin(T)
            T.matmul(pb[0][:, 0:1], ones[0:1, 0:128], stats[0:1, 44:45],
                     start=True, stop=True)
            T.matmul(pb[1][:, 0:1], ones[0:1, 0:128], stats[0:1, 45:46],
                     start=True, stop=True)
            step(T.drain())
            begin(V)
            V.tensor_copy(stats[:, scol:scol + 1], pb[0][:, 0:1])
            V.tensor_copy(stats[:, scol + 1:scol + 2], pb[1][:, 0:1])
            step(V.drain())
            def body2(buf, n0, k0, kn):
                begin(V)
                V.tensor_scalar(wpA[:, 0:kn], buf[:, 0:kn],
                                stats[:, scol:scol + 1], None, ALU.mult)
                V.drain()
                V.tensor_scalar(wpB[:, 0:kn], wpA[:, 0:kn], 0.5, None,
                                ALU.is_gt)
                V.drain()
                V.scalar_tensor_tensor(wpC[:, 0:kn], wpA[:, 0:kn], -0.5,
                                       wpB[:, 0:kn], ALU.is_lt, ALU.subtract)
                V.drain()
                V.tensor_scalar(wpC[:, 0:kn], wpC[:, 0:kn],
                                stats[:, scol + 1:scol + 2], None, ALU.mult)
                step(V.drain())
                prep_store_T(wpC, dstT, n0, k0 // 128, kn // 128)
            prep_loop(whbm, prep_tiles(NW, KW), body2)

        if _PHASES >= 4:
            if _P4MODE in ('fp4g', 'fp4', 'full'):
                prep_fp4_weight(wi["L0_ffn_w_gate"], wgT, DFF0, D)
            if _P4MODE in ('fp4', 'full'):
                prep_fp4_weight(wi["L0_ffn_w_up"], wuT, DFF0, D)
                prep_fp4_weight(wi["L0_ffn_w_down"], wdT, D, DFF0)
            if _P4MODE in ('plain', 'full'):
                prep_plain_weight(wi["L1_attn_Wq"], wqT1, D, D)
                prep_plain_weight(wi["L1_attn_Wk"], wkT1, D, D)
                prep_plain_weight(wi["L1_attn_Wv"], wvT1, D, D)
            if _P4MODE in ('bit', 'full'):
                prep_bit_weight(wi["L1_ffn_w1"], w1T, DFF1, D, 46)
                prep_bit_weight(wi["L1_ffn_w2"], w2T, D, DFF1, 49)
            for h in (range(H) if _P4MODE in ('phi', 'full') else []):
                begin(Sy)
                dma(wnat[0:64, 0:64],
                    AP(wi["L1_attn_weights"].tensor, h * DK * DK,
                       [[DK, 64], [1, DK]]))
                sy_step()
                begin(T)
                step(T.transpose(pb[0][0:64, 0:64],
                                 wnat[0:64, 0:64], ident[0:64, 0:64]))
                if h % 2 == 0:
                    begin(V)
                    V.tensor_copy(phiw[0:64, (h // 2) * 64:(h // 2) * 64 + 64],
                                  pb[0][0:64, 0:64])
                    step(V.drain())
                else:
                    begin(V)
                    V.tensor_copy(wpS[0:64, 0:64], pb[0][0:64, 0:64])
                    step(V.drain())
                    begin(Sy)
                    dma(phiw[64:128, (h // 2) * 64:(h // 2) * 64 + 64],
                        wpS[0:64, 0:64])
                    sy_step()

        C["mark_P4"] = (C["ch"], C["sd"])
        xqF = arv(0, [NCH, 384])
        XT = arv(3072, [512])
        FB1 = arv(3584, [512])
        FB2 = arv(4096, [512])
        FB3 = arv(4608, [512])
        FB4 = arv(5120, [512])
        hqb = arv(5648, [NCH0, 384])
        xqF_f = xqF.rearrange("p a b -> p (a b)")
        hqb_f = hqb.rearrange("p a b -> p (a b)")

        qm5 = arv(14720, [128])
        mst0 = mst[:, 0:1024]

        def quant_fp4_tm(buf, nsub, w, nb16_tot):
            n = nsub * 128
            bm = qm5[:, 0:nb16_tot]
            sh = qm5[:, 32:32 + nb16_tot]
            rr = qm5[:, 64:64 + nb16_tot]
            begin(V)
            V.tensor_reduce(bm[:, :, None],
                            buf[:, 0:n].rearrange("p (a b) -> p a b", b=16),
                            axis=AX.X, op=ALU.max, apply_absolute_value=True)
            V.drain()
            V.tensor_scalar(sh, bm, 1.0 / 6.0, 1e-12, ALU.mult, ALU.max)
            V.drain()
            V.reciprocal(rr, sh)
            V.drain()
            V.tensor_scalar_mul(sh, sh, 0.5)
            step(V.drain())
            rb = AP(rr.tensor, rr.offset, [[AR, 128], [1, nb16_tot], [0, 16]])
            sb_ = AP(sh.tensor, sh.offset, [[AR, 128], [1, nb16_tot], [0, 16]])
            begin(V)
            V.tensor_tensor(buf[:, 0:n].rearrange("p (a b) -> p a b", b=16),
                            buf[:, 0:n].rearrange("p (a b) -> p a b", b=16),
                            rb, ALU.mult)
            step(V.drain())
            begin(A)
            A.activation(FB2[:, 0:n], buf[:, 0:n], AF.Sign)
            A.drain()
            A.activation(buf[:, 0:n], buf[:, 0:n], AF.Abs)
            step(A.drain())
            fp4_stair(buf[:, 0:n], FB3[:, 0:n], FB4[:, 0:n], nb16_tot, n)
            begin(V)
            V.tensor_tensor(FB3[:, 0:n].rearrange("p (a b) -> p a b", b=16),
                            FB3[:, 0:n].rearrange("p (a b) -> p a b", b=16),
                            sb_, ALU.mult)
            V.drain()
            V.tensor_mul(buf[:, 0:n], FB3[:, 0:n], FB2[:, 0:n])
            step(V.drain())

        def evac_tp(dst128, bank, w):
            if w < 128:
                V.memset(dst128, 0.0)
                V.drain()
            V.tensor_copy(dst128[0:w, 0:128] if w < 128 else dst128,
                          pb[bank][0:w, 0:128])

        def xq_fp4(t0, tn, dst, dst_f):
            for s0 in range(0, tn, 128):
                w = min(128, tn - s0)
                for fh in range(2):
                    begin(T)
                    for i in range(4):
                        c = fh * 4 + i
                        T.transpose(pb[i][0:w, 0:128],
                                    xsf[:, c * TT + t0 + s0: c * TT + t0 + s0 + w],
                                    ident[:, :])
                    step(T.drain())
                    begin(V)
                    for i in range(4):
                        evac_tp(XT[:, i * 128:(i + 1) * 128], i, w)
                    step(V.drain())
                    quant_fp4_tm(XT, 4, w, 32)
                    begin(T)
                    for i in range(4):
                        c = fh * 4 + i
                        T.transpose(pb[i][:, 0:128],
                                    XT[:, i * 128:(i + 1) * 128], ident[:, :])
                    step(T.drain())
                    begin(V)
                    for i in range(4):
                        c = fh * 4 + i
                        V.tensor_copy(dst_f[:, c * 384 + s0: c * 384 + s0 + w],
                                      pb[i][0:128, 0:w])
                    step(V.drain())

        if _PHASES >= 5:
            for (t0, tn) in _ttiles(TT, 384):
                xq_fp4(t0, tn, xqF, xqF_f)
                if t0 == 0 and 'p5xq' in _DUMPS:
                    begin(Sy)
                    dma(dmp['p5xq'], ar[:, :])
                    sy_step()
                begin(Sy)
                dsW.issue(0, [(wb[:, :].rearrange("p (a b) -> p a b", b=128),
                               wgT[:, :, 0:128])])
                dsW2.issue(0, [(wb2[:, :].rearrange("p (a b) -> p a b", b=128),
                                wuT[:, :, 0:128])])
                step(Sy.nop())
                for nch in range(NCH0):
                    nb = nch % 2
                    gbuf = [wb, mst0][nb]
                    ubuf = [wb2, mst0][0]
                    begin(Sy)
                    dsW.wait(nb)
                    dsW2.wait(nch % 2)
                    if nch + 1 < NCH0:
                        nbuf = [wb, mst0][1 - nb]
                        dsW.issue(1 - nb,
                                  [(nbuf[:, :].rearrange("p (a b) -> p a b", b=128),
                                    wgT[:, :, (nch + 1) * 128:(nch + 2) * 128])])
                    step(Sy.nop())
                    begin(T)
                    for k in range(NCH):
                        T.matmul(pb[4][:, 0:tn], gbuf[:, k * 128:(k + 1) * 128],
                                 xqF_f[:, k * 384:k * 384 + tn],
                                 start=(k == 0), stop=(k == NCH - 1))
                    step(T.drain())
                    begin(T)
                    for k in range(NCH):
                        T.matmul(pb[5][:, 0:tn], ubuf[:, k * 128:(k + 1) * 128],
                                 xqF_f[:, k * 384:k * 384 + tn],
                                 start=(k == 0), stop=(k == NCH - 1))
                    step(T.drain())
                    begin(Sy)
                    if nch + 1 < NCH0:
                        dsW2.issue((nch + 1) % 2,
                                   [(wb2[:, :].rearrange("p (a b) -> p a b", b=128),
                                     wuT[:, :, (nch + 1) * 128:(nch + 2) * 128])])
                    step(Sy.nop())
                    begin(A)
                    A.activation(XT[:, 0:tn], pb[4][:, 0:tn], AF.Silu)
                    step(A.drain())
                    begin(V)
                    V.tensor_mul(XT[:, 0:tn], XT[:, 0:tn], pb[5][:, 0:tn])
                    step(V.drain())
                    nsub = (tn + 127) // 128
                    begin(T)
                    for s in range(nsub):
                        w = min(128, tn - s * 128)
                        T.transpose(pb[[6, 7, 3][s % 3]][0:w, 0:128],
                                    XT[:, s * 128:s * 128 + w], ident[:, :])
                    step(T.drain())
                    begin(V)
                    for s in range(nsub):
                        w = min(128, tn - s * 128)
                        evac_tp(FB1[:, s * 128:(s + 1) * 128], [6, 7, 3][s % 3], w)
                    step(V.drain())
                    if t0 == 0 and nch == 0 and 'p5h0' in _DUMPS:
                        begin(Sy)
                        dma(dmp['p5h0'], ar[:, :])
                        sy_step()
                    quant_fp4_tm(FB1, nsub, 128, nsub * 8)
                    if t0 == 0 and nch == 0 and 'p5hq0' in _DUMPS:
                        begin(Sy)
                        dma(dmp['p5hq0'], ar[:, :])
                        sy_step()
                    begin(T)
                    for s in range(nsub):
                        T.transpose(pb[[6, 7, 3][s % 3]][:, 0:128],
                                    FB1[:, s * 128:(s + 1) * 128], ident[:, :])
                    step(T.drain())
                    begin(V)
                    for s in range(nsub):
                        w = min(128, tn - s * 128)
                        V.tensor_copy(hqb_f[:, nch * 384 + s * 128:
                                            nch * 384 + s * 128 + w],
                                      pb[[6, 7, 3][s % 3]][0:128, 0:w])
                    step(V.drain())
                if t0 == 0 and 'p5hqb' in _DUMPS:
                    begin(Sy)
                    dma(dmp['p5hqb'], ar[:, :])
                    sy_step()
                dgroups = [(m, kg, min(8, NCH0 - kg))
                           for m in range(NCH) for kg in range(0, NCH0, 8)]
                begin(Sy)
                m0_, kg0, gn0 = dgroups[0]
                dsW.issue(0, [(wb[:, 0:gn0 * 128].rearrange("p (a b) -> p a b", b=128),
                               wdT[:, kg0:kg0 + gn0, m0_ * 128:(m0_ + 1) * 128])])
                step(Sy.nop())
                for gi, (m, kg, gn) in enumerate(dgroups):
                    gb = gi % 2
                    begin(Sy)
                    dsW.wait(gb)
                    if gi + 1 < len(dgroups):
                        m2, kg2, gn2 = dgroups[gi + 1]
                        nbuf = [wb, mst0][1 - gb]
                        dsW.issue(1 - gb,
                                  [(nbuf[:, 0:gn2 * 128].rearrange("p (a b) -> p a b", b=128),
                                    wdT[:, kg2:kg2 + gn2, m2 * 128:(m2 + 1) * 128])])
                    step(Sy.nop())
                    cbuf = [wb, mst0][gb]
                    begin(T)
                    for i in range(gn):
                        T.matmul(pb[m][:, 0:tn],
                                 cbuf[:, i * 128:(i + 1) * 128],
                                 hqb_f[:, (kg + i) * 384:(kg + i) * 384 + tn],
                                 start=(kg + i == 0),
                                 stop=(kg + i == NCH0 - 1))
                    step(T.drain())
                for m in range(NCH):
                    begin(V)
                    step(V.tensor_add(xs[:, m, t0:t0 + tn], xs[:, m, t0:t0 + tn],
                                      pb[m][:, 0:tn]))
                emit_ln(t0, tn)

        if "ffn0" in taps:
            begin(Sy)
            dma(dbg["ffn0"], xs[:, :, :])
            sy_step()

        C["mark_P5"] = (C["ch"], C["sd"])
        qf1 = arv(5648, [NCH, 256])
        kf1 = arv(7696, [NCH, 256])
        vf1 = arv(9744, [NCH, 256])
        phq = arv(11792, [256])
        phk = arv(12048, [256])
        prod = arv(12304, [256])
        srow = arv(12560, [256])
        rsrow = arv(12816, [256])
        tv1 = arv(13072, [256])
        dum6 = arv(13328, [256])
        phiwQ = arv(13584, [512])
        phiwK = arv(14096, [512])
        qf1_f = qf1.rearrange("p a b -> p (a b)")
        kf1_f = kf1.rearrange("p a b -> p (a b)")
        vf1_f = vf1.rearrange("p a b -> p (a b)")

        def l1_proj(dstf, wT, bias_sb, t0, tn):
            begin(Sy)
            dsW.issue(0, [(wb[:, :].rearrange("p (a b) -> p a b", b=128),
                           wT[:, :, 0:128])])
            step(Sy.nop())
            for m in range(NCH):
                mb = m % 2
                begin(Sy)
                dsW.wait(mb)
                if m + 1 < NCH:
                    nbuf = [wb, wb2][1 - mb]
                    dsW.issue(1 - mb,
                              [(nbuf[:, :].rearrange("p (a b) -> p a b", b=128),
                                wT[:, :, (m + 1) * 128:(m + 2) * 128])])
                step(Sy.nop())
                wcur = [wb, wb2][mb]
                begin(T)
                for k in range(NCH):
                    T.matmul(pb[0][:, 0:tn], wcur[:, k * 128:(k + 1) * 128],
                             xsf[:, k * TT + t0: k * TT + t0 + tn],
                             start=(k == 0), stop=(k == NCH - 1))
                step(T.drain())
                begin(A)
                A.activation(dstf[:, m * 256:m * 256 + tn], pb[0][:, 0:tn],
                             AF.Identity, bias=bias_sb[:, m:m + 1])
                step(A.drain())

        if _PHASES >= 6:
            begin(V)
            V.memset(stats[:, 33:34], 0.0)
            V.memset(stats[:, 34:35], 0.0)
            step(V.drain())
            for (wTl, acc_col, b_sb) in [(wqT1, 33, biasq), (wkT1, 34, biask)]:
                begin(Sy)
                dsW.issue(0, [(wb[:, :].rearrange("p (a b) -> p a b", b=128),
                               wTl[:, :, 0:128])])
                step(Sy.nop())
                for m in range(NCH):
                    mb = m % 2
                    begin(Sy)
                    dsW.wait(mb)
                    if m + 1 < NCH:
                        nbuf = [wb, wb2][1 - mb]
                        dsW.issue(1 - mb,
                                  [(nbuf[:, :].rearrange("p (a b) -> p a b", b=128),
                                    wTl[:, :, (m + 1) * 128:(m + 2) * 128])])
                    step(Sy.nop())
                    wpa_cur = [wb, wb2][mb]
                    for (t0, tn) in _ttiles(TC, 512, start=1):
                        tn = min(tn, 4097 - t0)
                        begin(T)
                        for k in range(NCH):
                            T.matmul(pb[0][:, 0:tn],
                                     wpa_cur[:, k * 128:(k + 1) * 128],
                                     xsf[:, k * TT + t0: k * TT + t0 + tn],
                                     start=(k == 0), stop=(k == NCH - 1))
                        step(T.drain())
                        begin(A)
                        A.activation(XT[:, 0:tn], pb[0][:, 0:tn], AF.Square,
                                     bias=b_sb[:, m:m + 1],
                                     accum_out=stats[:, 35:36])
                        step(A.drain())
                        begin(V)
                        V.tensor_add(stats[:, acc_col:acc_col + 1],
                                     stats[:, acc_col:acc_col + 1],
                                     stats[:, 35:36])
                        step(V.drain())
            LN8 = 2.0794415416798357
            for (acc_col, rq_col, ls_col) in [(33, 36, 37), (34, 38, 39)]:
                begin(T)
                step(T.matmul(pb[0][0:1, 0:1], ones[:, 0:1],
                              stats[:, acc_col:acc_col + 1], start=True, stop=True))
                begin(V)
                V.tensor_copy(stats[0:1, 60:61], pb[0][0:1, 0:1])
                step(V.drain())
                begin(A)
                A.activation(stats[0:1, 61:62], stats[0:1, 60:61], AF.Ln)
                A.drain()
                A.activation(stats[0:1, 61:62], stats[0:1, 61:62], AF.Exp,
                             scale=-0.5)
                step(A.drain())
                begin(V)
                V.tensor_mul(stats[0:1, 62:63], stats[0:1, 61:62],
                             stats[0:1, 61:62])
                V.drain()
                V.tensor_mul(stats[0:1, 62:63], stats[0:1, 62:63],
                             stats[0:1, 60:61])
                V.drain()
                V.tensor_scalar(stats[0:1, 62:63], stats[0:1, 62:63], -0.5,
                                -LN8, ALU.mult, ALU.add)
                step(V.drain())
                begin(T)
                T.matmul(pb[0][:, 0:1], ones[0:1, 0:128], stats[0:1, 61:62],
                         start=True, stop=True)
                T.matmul(pb[1][:, 0:1], ones[0:1, 0:128], stats[0:1, 62:63],
                         start=True, stop=True)
                step(T.drain())
                begin(V)
                V.tensor_copy(stats[:, rq_col:rq_col + 1], pb[0][:, 0:1])
                V.tensor_copy(stats[:, ls_col:ls_col + 1], pb[1][:, 0:1])
                step(V.drain())
            begin(V)
            V.tensor_scalar(phiwQ[:, :], phiw[:, :], stats[:, 36:37], None,
                            ALU.mult)
            V.tensor_scalar(phiwK[:, :], phiw[:, :], stats[:, 38:39], None,
                            ALU.mult)
            step(V.drain())
            edge_tiles = ([(0, 1, 5), (TT - 1, 1, 6)] if _GHOSTS else [])
            for (t0, tn, fcol) in [(1 + i * 256, 256, None) for i in range(16)] \
                    + edge_tiles:
                l1_proj(qf1_f, wqT1, biasq, t0, tn)
                l1_proj(kf1_f, wkT1, biask, t0, tn)
                l1_proj(vf1_f, wvT1, biasv, t0, tn)
                for h in range(H):
                    hb = 64 * (h % 2)
                    hsl = slice(hb, hb + 64)
                    hcol = (h // 2) * 256
                    begin(T)
                    T.matmul(pb[0][0:64, 0:tn],
                             phiwQ[hsl, (h // 2) * 64:(h // 2) * 64 + 64],
                             qf1_f[hsl, hcol:hcol + tn], start=True, stop=True)
                    T.drain()
                    T.matmul(pb[1][0:64, 0:tn],
                             phiwK[hsl, (h // 2) * 64:(h // 2) * 64 + 64],
                             kf1_f[hsl, hcol:hcol + tn], start=True, stop=True)
                    step(T.drain())
                    begin(A)
                    A.activation(phq[0:64, 0:tn], pb[0][0:64, 0:tn], AF.Exp,
                                 bias=stats[0:64, 37:38])
                    A.drain()
                    A.activation(phk[0:64, 0:tn], pb[1][0:64, 0:tn], AF.Exp,
                                 bias=stats[0:64, 39:40])
                    step(A.drain())
                    begin(V)
                    V.tensor_mul(prod[0:64, 0:tn], phq[0:64, 0:tn],
                                 phk[0:64, 0:tn])
                    step(V.drain())
                    begin(T)
                    step(T.matmul(pb[2][0:1, 0:tn], ones[0:64, 0:1],
                                  prod[0:64, 0:tn], start=True, stop=True))
                    begin(V)
                    V.tensor_copy(srow[0:1, 0:tn], pb[2][0:1, 0:tn])
                    V.drain()
                    V.reciprocal(rsrow[0:1, 0:tn], srow[0:1, 0:tn])
                    step(V.drain())
                    begin(T)
                    T.matmul(pb[3][hb:hb + 64, 0:tn], ones[0:1, 0:64],
                             srow[0:1, 0:tn], start=True, stop=True)
                    T.drain()
                    T.matmul(pb[4][hb:hb + 64, 0:tn], ones[0:1, 0:64],
                             rsrow[0:1, 0:tn], start=True, stop=True)
                    step(T.drain())
                    begin(V)
                    V.tensor_mul(tv1[hsl, 0:tn],
                                 vf1_f[hsl, hcol:hcol + tn],
                                 pb[3][hsl, 0:tn])
                    V.drain()
                    V.tensor_mul(tv1[hsl, 0:tn], tv1[hsl, 0:tn],
                                 pb[4][hsl, 0:tn])
                    V.drain()
                    if fcol is None:
                        V.tensor_add(xs[hsl, h // 2, t0:t0 + tn],
                                     xs[hsl, h // 2, t0:t0 + tn], tv1[hsl, 0:tn])
                    else:
                        V.scalar_tensor_tensor(xs[hsl, h // 2, t0:t0 + tn],
                                               tv1[hsl, 0:tn],
                                               stats[hsl, fcol:fcol + 1],
                                               xs[hsl, h // 2, t0:t0 + tn],
                                               ALU.mult, ALU.add)
                    step(V.drain())
                emit_ln(t0, tn)

        if "attn1" in taps:
            begin(Sy)
            dma(dbg["attn1"], xs[:, :, :])
            sy_step()

        C["mark_P6"] = (C["ch"], C["sd"])
        ari = sbt("ari", [128, 512], I32)
        hb7 = arv(5648, [NCH1, 322])
        xq7 = arv(0, [NCH, 322])
        hb7_f = hb7.rearrange("p a b -> p (a b)")
        xq7_f = xq7.rearrange("p a b -> p (a b)")

        def act_quant_tm(buf, n, maxcol, rscol):
            begin(V)
            V.tensor_scalar(FB2[:, 0:n], buf[:, 0:n],
                            stats[:, maxcol + 1:maxcol + 2], 0.5,
                            ALU.mult, ALU.add)
            V.drain()
            V.tensor_copy(ari[:, 0:n], FB2[:, 0:n])
            V.drain()
            V.tensor_copy(FB3[:, 0:n], ari[:, 0:n])
            V.drain()
            V.tensor_tensor(FB4[:, 0:n], FB3[:, 0:n], FB2[:, 0:n], ALU.is_gt)
            V.drain()
            V.tensor_sub(FB3[:, 0:n], FB3[:, 0:n], FB4[:, 0:n])
            V.drain()
            V.tensor_scalar(FB3[:, 0:n], FB3[:, 0:n], -128.0, 127.0,
                            ALU.max, ALU.min)
            V.drain()
            V.tensor_scalar(buf[:, 0:n], FB3[:, 0:n],
                            stats[:, rscol:rscol + 1], None, ALU.mult)
            step(V.drain())

        def xq_l1(t0c, tnc, dst_f, W_chunks, maxcol, rscol):
            for s0 in range(0, tnc, 128):
                w = min(128, tnc - s0)
                begin(V)
                V.memset(stats[:, maxcol:maxcol + 1], 0.0)
                step(V.drain())
                for fh in range(W_chunks // 4):
                    begin(T)
                    for i in range(4):
                        c = fh * 4 + i
                        T.transpose(pb[i][0:w, 0:128],
                                    xsf[:, c * TT + t0c + s0: c * TT + t0c + s0 + w],
                                    ident[:, :])
                    step(T.drain())
                    begin(V)
                    for i in range(4):
                        evac_tp(XT[:, i * 128:(i + 1) * 128], i, w)
                    step(V.drain())
                    begin(V)
                    V.tensor_reduce(stats[:, maxcol + 2:maxcol + 3],
                                    XT[:, 0:512], axis=AX.X, op=ALU.max,
                                    apply_absolute_value=True)
                    V.drain()
                    V.tensor_max(stats[:, maxcol:maxcol + 1],
                                 stats[:, maxcol:maxcol + 1],
                                 stats[:, maxcol + 2:maxcol + 3])
                    step(V.drain())
                begin(V)
                V.tensor_scalar(stats[:, maxcol:maxcol + 1],
                                stats[:, maxcol:maxcol + 1], 1e-5, None, ALU.max)
                V.drain()
                V.reciprocal(stats[:, maxcol + 1:maxcol + 2],
                             stats[:, maxcol:maxcol + 1])
                V.drain()
                V.tensor_scalar_mul(stats[:, maxcol + 1:maxcol + 2],
                                    stats[:, maxcol + 1:maxcol + 2], 127.0)
                V.drain()
                V.tensor_scalar_mul(stats[:, rscol:rscol + 1],
                                    stats[:, maxcol:maxcol + 1], 1.0 / 127.0)
                step(V.drain())
                for fh in range(W_chunks // 4):
                    begin(T)
                    for i in range(4):
                        c = fh * 4 + i
                        T.transpose(pb[i][0:w, 0:128],
                                    xsf[:, c * TT + t0c + s0: c * TT + t0c + s0 + w],
                                    ident[:, :])
                    step(T.drain())
                    begin(V)
                    for i in range(4):
                        evac_tp(XT[:, i * 128:(i + 1) * 128], i, w)
                    step(V.drain())
                    act_quant_tm(XT, 512, maxcol, rscol)
                    begin(T)
                    for i in range(4):
                        T.transpose(pb[i][:, 0:128],
                                    XT[:, i * 128:(i + 1) * 128], ident[:, :])
                    step(T.drain())
                    begin(V)
                    for i in range(4):
                        c = fh * 4 + i
                        V.tensor_copy(dst_f[:, c * 322 + s0: c * 322 + s0 + w],
                                      pb[i][0:128, 0:w])
                    step(V.drain())

        if _PHASES >= 7:
            SN7 = 320
            tiles7 = []
            t = 1
            while t < 1 + TC:
                n = min(SN7, 1 + TC - t)
                tiles7.append((t, n))
                t += n
            for (t0, tn) in tiles7:
                hn = tn + 2
                xq_l1(t0 - 1, hn, xq7_f, NCH, 50, 52)
                begin(Sy)
                dsW.issue(0, [(wb[:, :].rearrange("p (a b) -> p a b", b=128),
                               w1T[:, :, 0:128])])
                step(Sy.nop())
                for nch in range(NCH1):
                    nb = nch % 2
                    begin(Sy)
                    dsW.wait(nb)
                    if nch + 1 < NCH1:
                        nbuf = [wb, mst0][1 - nb]
                        dsW.issue(1 - nb,
                                  [(nbuf[:, :].rearrange("p (a b) -> p a b", b=128),
                                    w1T[:, :, (nch + 1) * 128:(nch + 2) * 128])])
                    step(Sy.nop())
                    wcur = [wb, mst0][nb]
                    begin(T)
                    for k in range(NCH):
                        T.matmul(pb[4][:, 0:hn], wcur[:, k * 128:(k + 1) * 128],
                                 xq7_f[:, k * 322:k * 322 + hn],
                                 start=(k == 0), stop=(k == NCH - 1))
                    step(T.drain())
                    begin(V)
                    V.tensor_scalar(FB2[:, 0:tn], pb[4][:, 0:tn],
                                    convw[:, nch, 0:1], None, ALU.mult)
                    V.drain()
                    V.scalar_tensor_tensor(FB2[:, 0:tn], pb[4][:, 1:1 + tn],
                                           convw[:, nch, 1:2], FB2[:, 0:tn],
                                           ALU.mult, ALU.add)
                    V.drain()
                    V.scalar_tensor_tensor(FB2[:, 0:tn], pb[4][:, 2:2 + tn],
                                           convw[:, nch, 2:3], FB2[:, 0:tn],
                                           ALU.mult, ALU.add)
                    step(V.drain())
                    begin(A)
                    A.activation(hb7_f[:, nch * 322:nch * 322 + tn],
                                 FB2[:, 0:tn], AF.Silu,
                                 bias=convb[:, nch:nch + 1])
                    step(A.drain())
                for s0 in range(0, tn, 128):
                    w = min(128, tn - s0)
                    begin(V)
                    V.memset(stats[:, 54:55], 0.0)
                    step(V.drain())
                    for fh in range(NCH1 // 4):
                        begin(T)
                        for i in range(4):
                            c = fh * 4 + i
                            T.transpose(pb[i][0:w, 0:128],
                                        hb7_f[:, c * 322 + s0: c * 322 + s0 + w],
                                        ident[:, :])
                        step(T.drain())
                        begin(V)
                        for i in range(4):
                            evac_tp(XT[:, i * 128:(i + 1) * 128], i, w)
                        step(V.drain())
                        begin(V)
                        V.tensor_reduce(stats[:, 56:57], XT[:, 0:512],
                                        axis=AX.X, op=ALU.max,
                                        apply_absolute_value=True)
                        V.drain()
                        V.tensor_max(stats[:, 54:55], stats[:, 54:55],
                                     stats[:, 56:57])
                        step(V.drain())
                    begin(V)
                    V.tensor_scalar(stats[:, 54:55], stats[:, 54:55], 1e-5,
                                    None, ALU.max)
                    V.drain()
                    V.reciprocal(stats[:, 55:56], stats[:, 54:55])
                    V.drain()
                    V.tensor_scalar_mul(stats[:, 55:56], stats[:, 55:56], 127.0)
                    V.drain()
                    V.tensor_scalar_mul(stats[:, 57:58], stats[:, 54:55],
                                        1.0 / 127.0)
                    step(V.drain())
                    for fh in range(NCH1 // 4):
                        begin(T)
                        for i in range(4):
                            c = fh * 4 + i
                            T.transpose(pb[i][0:w, 0:128],
                                        hb7_f[:, c * 322 + s0: c * 322 + s0 + w],
                                        ident[:, :])
                        step(T.drain())
                        begin(V)
                        for i in range(4):
                            evac_tp(XT[:, i * 128:(i + 1) * 128], i, w)
                        step(V.drain())
                        act_quant_tm(XT, 512, 54, 57)
                        begin(T)
                        for i in range(4):
                            T.transpose(pb[i][:, 0:128],
                                        XT[:, i * 128:(i + 1) * 128], ident[:, :])
                        step(T.drain())
                        begin(V)
                        for i in range(4):
                            c = fh * 4 + i
                            V.tensor_copy(hb7_f[:, c * 322 + s0: c * 322 + s0 + w],
                                          pb[i][0:128, 0:w])
                        step(V.drain())
                w2groups = [(m, kg) for m in range(NCH) for kg in range(0, NCH1, 8)]
                begin(Sy)
                m0_, kg0 = w2groups[0]
                dsW.issue(0, [(wb[:, :].rearrange("p (a b) -> p a b", b=128),
                               w2T[:, kg0:kg0 + 8, m0_ * 128:(m0_ + 1) * 128])])
                step(Sy.nop())
                for gi, (m, kg) in enumerate(w2groups):
                    gb = gi % 2
                    begin(Sy)
                    dsW.wait(gb)
                    if gi + 1 < len(w2groups):
                        m2, kg2 = w2groups[gi + 1]
                        nbuf = [wb, mst0][1 - gb]
                        dsW.issue(1 - gb,
                                  [(nbuf[:, :].rearrange("p (a b) -> p a b", b=128),
                                    w2T[:, kg2:kg2 + 8, m2 * 128:(m2 + 1) * 128])])
                    step(Sy.nop())
                    cbuf = [wb, mst0][gb]
                    begin(T)
                    for i in range(8):
                        T.matmul(pb[m][:, 0:tn],
                                 cbuf[:, i * 128:(i + 1) * 128],
                                 hb7_f[:, (kg + i) * 322:(kg + i) * 322 + tn],
                                 start=(kg + i == 0),
                                 stop=(kg + i == NCH1 - 1))
                    step(T.drain())
                for m in range(NCH):
                    begin(V)
                    step(V.tensor_add(xs[:, m, t0:t0 + tn],
                                      xs[:, m, t0:t0 + tn], pb[m][:, 0:tn]))
                emit_ln(t0, tn)

        begin(Sy)
        dma(y, xs[:, :, 1:1 + TC])
        sy_step()
        begin(Sy)
        step(Sy.nop())
        build_nc.counters = dict(C)
    return nc


def kernel(**inputs):
    cores = []
    x = np.asarray(inputs["L0_attn_x"])
    for c in range(8):
        b, half = c // 2, c % 2
        t0 = half * TC
        xs_ = np.zeros((D, TT), dtype=np.float32)
        xs_[:, 1:1 + TC] = x[b, t0:t0 + TC, :].T
        if half == 0:
            xs_[:, TT - 1] = x[b, t0 + TC, :]
        else:
            xs_[:, 0] = x[b, t0 - 1, :]
        xpre_ = np.ascontiguousarray(x[b, 0:TC, :].T)
        fM = 1.0 if half == 1 else 0.0
        fL = 1.0 if half == 1 else 0.0
        fR = 1.0 if half == 0 else 0.0
        im = {
            "xf": xs_.reshape(NCH, 128, TT).transpose(1, 0, 2).copy(),
            "xpre": xpre_.reshape(NCH, 128, TC).transpose(1, 0, 2).copy(),
            "flags": np.array([[fM, fL, fR, 0.0]], dtype=np.float32),
        }
        for nm in ["L0_attn_Wq", "L0_attn_Wk", "L0_attn_Wv", "L0_attn_Wo",
                   "L0_ffn_w_gate", "L0_ffn_w_up", "L0_ffn_w_down",
                   "L1_attn_Wq", "L1_attn_Wk", "L1_attn_Wv",
                   "L1_ffn_w1", "L1_ffn_w2"]:
            im[nm] = np.ascontiguousarray(np.asarray(inputs[nm], dtype=np.float32))
        im["L0_attn_betas"] = np.asarray(inputs["L0_attn_betas"], np.float32).reshape(H, DV)
        im["L0_attn_memory0"] = np.asarray(inputs["L0_attn_memory0"], np.float32).reshape(H, DK, DV)
        im["L0_attn_z0"] = np.asarray(inputs["L0_attn_z0"], np.float32).reshape(H, DK)
        im["L1_attn_bq"] = np.asarray(inputs["L1_attn_bq"], np.float32).reshape(1, D)
        im["L1_attn_bk"] = np.asarray(inputs["L1_attn_bk"], np.float32).reshape(1, D)
        im["L1_attn_bv"] = np.asarray(inputs["L1_attn_bv"], np.float32).reshape(1, D)
        im["L1_attn_weights"] = np.asarray(inputs["L1_attn_weights"], np.float32)
        im["L1_ffn_conv_w"] = np.asarray(inputs["L1_ffn_conv_w"], np.float32).reshape(DFF1, 3)
        im["L1_ffn_conv_b"] = np.asarray(inputs["L1_ffn_conv_b"], np.float32).reshape(1, DFF1)
        cores.append(im)
    nc = build_nc(taps=_DEBUG_TAPS)
    res = run_bass_kernel_spmd(nc, cores, core_ids=list(range(8)))
    kernel._last = res
    out = np.empty((B, S, D), dtype=np.float32)
    for c in range(8):
        b, half = c // 2, c % 2
        yc = np.asarray(res.results[c]["y"])
        out[b, half * TC:(half + 1) * TC, :] = \
            yc.transpose(1, 0, 2).reshape(D, TC).T
    return out


# version 11
# speedup vs baseline: 1.0566x; 1.0208x over previous
import numpy as np
from contextlib import ExitStack

import concourse.bass as bass
import concourse.mybir as mybir
from concourse.bass_types import AP
from concourse.bass_utils import run_bass_kernel_spmd

F32 = mybir.dt.float32
I32 = mybir.dt.int32
BF16 = mybir.dt.bfloat16
AF = mybir.ActivationFunctionType
ALU = mybir.AluOpType
AX = mybir.AxisListType

B, S, D = 4, 8192, 1024
H, DK, DV, LSEG = 16, 64, 64, 512
TC = 4096
TT = TC + 2
NSEG = TC // LSEG
NCH = D // 128
DFF0, NCH0 = 2816, 22
DFF1, NCH1 = 4096, 32
SC0 = 1.0 / 8.0
NEG = -1.0e30

_DEBUG_TAPS = ()
_PHASES = 99
_P2N = 128
_P2MODE = 'full'
_P3N = 512
_P3MODE = 'full'
_P3TAIL = True
_P1M = 8
_P4MODE = 'full'
_DUMPS = ()
_GHOSTS = True


def _ttiles(n, step, start=0):
    out = []
    t = start
    while t < n:
        out.append((t, min(step, n - t)))
        t += step
    return out


def build_nc(taps=()):
    nc = bass.Bass("TRN2", target_bir_lowering=False)
    T, V, A, G, Sy = nc.tensor, nc.vector, nc.scalar, nc.gpsimd, nc.sync

    xf = nc.dram_tensor("xf", [128, NCH, TT], F32, kind="ExternalInput").ap()
    xpre = nc.dram_tensor("xpre", [128, NCH, TC], F32, kind="ExternalInput").ap()
    flags = nc.dram_tensor("flags", [1, 4], F32, kind="ExternalInput").ap()
    wi = {}
    for nm, shp in [
        ("L0_attn_Wq", [D, D]), ("L0_attn_Wk", [D, D]), ("L0_attn_Wv", [D, D]),
        ("L0_attn_Wo", [D, D]), ("L0_attn_betas", [H, DV]),
        ("L0_attn_memory0", [H, DK, DV]), ("L0_attn_z0", [H, DK]),
        ("L0_ffn_w_gate", [DFF0, D]), ("L0_ffn_w_up", [DFF0, D]),
        ("L0_ffn_w_down", [D, DFF0]),
        ("L1_attn_Wq", [D, D]), ("L1_attn_bq", [1, D]),
        ("L1_attn_Wk", [D, D]), ("L1_attn_bk", [1, D]),
        ("L1_attn_Wv", [D, D]), ("L1_attn_bv", [1, D]),
        ("L1_attn_weights", [H, DK, DK]),
        ("L1_ffn_w1", [DFF1, D]), ("L1_ffn_conv_w", [DFF1, 3]),
        ("L1_ffn_conv_b", [1, DFF1]), ("L1_ffn_w2", [D, DFF1]),
    ]:
        wi[nm] = nc.dram_tensor(nm, shp, F32, kind="ExternalInput").ap()
    y = nc.dram_tensor("y", [128, NCH, TC], F32, kind="ExternalOutput").ap()
    dbg = {}
    for t_ in taps:
        dbg[t_] = nc.dram_tensor("dbg_" + t_, [128, NCH, TT], F32,
                                 kind="ExternalOutput").ap()
    dmp = {}
    for t_ in _DUMPS:
        dmp[t_] = nc.dram_tensor("dmp_" + t_, [128, 16256], F32,
                                 kind="ExternalOutput").ap()

    qdr = nc.dram_tensor("qdr", [128, NCH, TT], F32).ap()
    kdr = nc.dram_tensor("kdr", [128, NCH, TT], F32).ap()
    vdr = nc.dram_tensor("vdr", [128, NCH, TT], F32).ap()
    kpre = nc.dram_tensor("kpre", [128, NCH, TC], F32).ap()
    vpre = nc.dram_tensor("vpre", [128, NCH, TC], F32).ap()
    wgT = nc.dram_tensor("wgT", [128, NCH, DFF0], F32).ap()
    wuT = nc.dram_tensor("wuT", [128, NCH, DFF0], F32).ap()
    wdT = nc.dram_tensor("wdT", [128, NCH0, D], F32).ap()
    w1T = nc.dram_tensor("w1T", [128, NCH, DFF1], F32).ap()
    w2T = nc.dram_tensor("w2T", [128, NCH1, D], F32).ap()
    wqT1 = nc.dram_tensor("wqT1", [128, NCH, D], F32).ap()
    wkT1 = nc.dram_tensor("wkT1", [128, NCH, D], F32).ap()
    wvT1 = nc.dram_tensor("wvT1", [128, NCH, D], F32).ap()

    with ExitStack() as st:
        def sbt(name, shape, dt=F32):
            return st.enter_context(nc.sbuf_tensor(name, shape, dt))

        xs = sbt("xs", [128, NCH, TT])
        AR = 16256
        ar = sbt("arena", [128, AR])
        wb = sbt("wb", [128, NCH * 128])
        wnat = sbt("wnat", [128, 512])
        ident = sbt("ident", [128, 128])
        ones = sbt("ones", [128, 128])
        biasq = sbt("biasq", [128, NCH])
        biask = sbt("biask", [128, NCH])
        biasv = sbt("biasv", [128, NCH])
        convw = sbt("convw", [128, NCH1, 3])
        convb = sbt("convb", [128, NCH1])
        gates = sbt("gates", [128, H])
        phiw = sbt("phiw", [128, NCH * DK])
        stats = sbt("stats", [128, 64])
        MS = 68
        mst = sbt("mst", [128, H * MS])
        pb = [st.enter_context(nc.psum_tensor("pb%d" % i, [128, 512], F32))
              for i in range(8)]

        xsf = xs[:, :, :].rearrange("p a b -> p (a b)")
        iicol = wnat

        CH = st.enter_context(nc.semaphore("ch"))
        SD = st.enter_context(nc.semaphore("sd"))
        C = {"ch": 0, "sd": 0}

        def dma(out, in_):
            Sy.dma_start(out=out, in_=in_).then_inc(SD, 16)
            C["sd"] += 16

        def dmas(out, in_):
            Sy.dma_start(out=out, in_=in_,
                         allow_slow_non_contiguous=True).then_inc(SD, 16)
            C["sd"] += 16

        def step(instr):
            instr.then_inc(CH, 1)
            C["ch"] += 1

        def begin(eng):
            eng.wait_ge(CH, C["ch"])

        def sy_step():
            Sy.wait_ge(SD, C["sd"])
            step(Sy.nop())

        class DStream:
            def __init__(self, name):
                self.name = name
                self.sems = [st.enter_context(nc.semaphore(name + "0")),
                             st.enter_context(nc.semaphore(name + "1"))]
                self.cnt = [0, 0]
                C.setdefault("streams", {})[name] = self

            def issue(self, b, pairs, slow=False):
                for (o, i_) in pairs:
                    Sy.dma_start(out=o, in_=i_,
                                 allow_slow_non_contiguous=slow).then_inc(
                        self.sems[b], 16)
                    self.cnt[b] += 16

            def wait(self, b, eng=None):
                (eng or Sy).wait_ge(self.sems[b], self.cnt[b])

        dsW = DStream("dsw")
        dsW2 = DStream("dsw2")
        dsH = DStream("dsh")
        dsX = DStream("dsx")
        dsP = DStream("dsp")
        dsS = DStream("dss")

        def arv(off, shape):
            n = int(np.prod(shape))
            assert off + n <= AR, (off, shape)
            a = ar[:, off:off + n]
            if len(shape) == 1:
                return a
            if len(shape) == 2:
                return a.rearrange("p (a b) -> p a b", b=shape[1])
            return a.rearrange("p (a b c) -> p a b c", b=shape[1], c=shape[2])

        qsg = arv(0, [256])
        ksg = arv(256, [256])
        kmsg = arv(512, [256])
        vsg = arv(768, [256])
        ktm = arv(1024, [256])
        VS = 68
        vtm = arv(1280, [4 * 68])
        sktm = arv(1552, [256])
        sq_ = arv(1808, [256])
        psb = arv(2064, [4, 512])
        ptsb = arv(4112, [512])
        attd = arv(4624, [512])
        nums = arv(5136, [512])
        stg = arv(5648, [256])
        denr = arv(5904, [512])
        ghatt = arv(6416, [NCH, 2])
        attfm = arv(6432, [NCH, 512])
        mask = arv(10528, [4, 512])
        wb2 = arv(14848, [NCH * 128])
        wb3 = arv(12576, [NCH * 128])
        wb4 = arv(13600, [NCH * 128])
        sgB = arv(12576, [1024])
        qsg2 = sgB[:, 0:256]
        ksg2 = sgB[:, 256:512]
        kmsg2 = sgB[:, 512:768]
        vsg2 = sgB[:, 768:1024]
        pstg = arv(0, [512])
        pstg2 = arv(8704, [512])
        xpre_sb = arv(512, [NCH, 512])
        xpre_sb2 = arv(4608, [NCH, 512])
        m0tmp = arv(0, [H, 68])
        attfm_f = attfm.rearrange("p a b -> p (a b)")
        lnrows = arv(4112, [1536])
        sqb = psb.rearrange("p a b -> p (a b)")

        begin(Sy)
        dma(xs[:, :, :], xf)
        dmas(gates[0:64, 0:H], AP(wi["L0_attn_betas"].tensor, 0, [[1, DV], [DV, H]]))
        dmas(biasq[:, 0:NCH], AP(wi["L1_attn_bq"].tensor, 0, [[1, 128], [128, NCH]]))
        dmas(biask[:, 0:NCH], AP(wi["L1_attn_bk"].tensor, 0, [[1, 128], [128, NCH]]))
        dmas(biasv[:, 0:NCH], AP(wi["L1_attn_bv"].tensor, 0, [[1, 128], [128, NCH]]))
        dmas(convw[:, :, :], AP(wi["L1_ffn_conv_w"].tensor, 0,
                                [[3, 128], [3 * 128, NCH1], [1, 3]]))
        dmas(convb[:, 0:NCH1], AP(wi["L1_ffn_conv_b"].tensor, 0,
                                  [[1, 128], [128, NCH1]]))
        dma(stats[0:1, 0:4], flags)
        sy_step()
        begin(V)
        V.memset(stats[:, 13:14], 1e-5)
        step(V.drain())
        begin(A)
        step(A.activation(gates[0:64, 0:H], gates[0:64, 0:H], AF.Sigmoid))

        begin(G)
        G.iota(iicol[:, 0:128], pattern=[[1, 128]], base=0, channel_multiplier=-1,
               allow_small_or_imprecise_dtypes=True)
        step(G.drain())
        begin(V)
        V.tensor_single_scalar(ident[:, :], iicol[:, 0:128], 0.0, ALU.is_equal)
        V.memset(ones[:, :], 1.0)
        step(V.drain())
        begin(T)
        step(T.matmul(pb[0][:, 0:4], ones[0:1, 0:128], stats[0:1, 0:4],
                      start=True, stop=True))
        begin(V)
        V.tensor_copy(stats[:, 4:8], pb[0][:, 0:4])
        step(V.drain())

        begin(G)
        G.iota(iicol[:, 0:128].rearrange("p (a b) -> p a b", b=32),
               pattern=[[2, 4], [16, 32]], base=0, channel_multiplier=0,
               allow_small_or_imprecise_dtypes=True)
        step(G.drain())
        begin(T)
        step(T.transpose(pb[0][:, 0:128], iicol[:, 0:128], ident[:, :]))
        begin(V)
        V.tensor_copy(iicol[:, 0:1], pb[0][:, 0:1])
        step(V.drain())
        begin(G)
        G.iota(denr[:, :].rearrange("p (a b c) -> p a b c", b=8, c=32),
               pattern=[[1, 2], [2, 8], [16, 32]], base=0, channel_multiplier=0,
               allow_small_or_imprecise_dtypes=True)
        step(G.drain())
        begin(V)
        for blk in range(4):
            clo_i, ihalf = blk // 2, blk % 2
            off = float(8 * ihalf + clo_i)
            V.tensor_scalar(stats[:, 8 + blk:9 + blk], iicol[:, 0:1], off, None,
                            ALU.add)
        V.drain()
        for blk in range(4):
            V.scalar_tensor_tensor(mask[:, blk, :], denr[:, :],
                                   stats[:, 8 + blk:9 + blk], denr[:, :],
                                   ALU.subtract, ALU.bypass)
        V.drain()
        for blk in range(4):
            V.tensor_scalar(mask[:, blk, :], mask[:, blk, :], 0.0, NEG,
                            ALU.is_gt, ALU.mult)
        step(V.drain())

        C["mark_P0"] = (C["ch"], C["sd"])
        main_tiles = _ttiles(TT, 512)
        pre_tiles = _ttiles(TC, 512)
        xpre_f = xpre_sb.rearrange("p a b -> p (a b)")

        def w_issue(b, whbm, m, nk=NCH, stream=None):
            s = stream if stream is not None else dsW
            buf = ([wb, wb2] if s is dsW else [wb3, wb4])[b]
            s.issue(b, [(buf[:, k * 128:(k + 1) * 128],
                         whbm[k * 128:(k + 1) * 128, m * 128:(m + 1) * 128])
                        for k in range(nk)])
            return buf

        def proj_pass(whbm, dst, src_is_stream, tiles):
            xpre_bufs = [xpre_sb, xpre_sb2]
            xpre_fs = [xpre_f, xpre_sb2.rearrange("p a b -> p (a b)")]
            begin(Sy)
            w_issue(0, whbm, 0)
            if not src_is_stream:
                dsX.issue(0, [(xpre_bufs[0][:, :, 0:tiles[0][1]],
                               xpre[:, :, tiles[0][0]:tiles[0][0] + tiles[0][1]])])
            step(Sy.nop())
            nt = len(tiles)
            for m in range(NCH):
                mb = m % 2
                begin(Sy)
                dsW.wait(mb)
                if m + 1 < NCH:
                    w_issue((m + 1) % 2, whbm, m + 1)
                step(Sy.nop())
                wcur = [wb, wb2][mb]
                for ti, (t0, tn) in enumerate(tiles):
                    gi = m * nt + ti
                    if not src_is_stream:
                        begin(Sy)
                        dsX.wait(gi % 2)
                        if gi + 1 < NCH * nt:
                            t0n, tnn = tiles[(ti + 1) % nt]
                            dsX.issue((gi + 1) % 2,
                                      [(xpre_bufs[(gi + 1) % 2][:, :, 0:tnn],
                                        xpre[:, :, t0n:t0n + tnn])])
                        step(Sy.nop())
                    begin(T)
                    for k in range(NCH):
                        rhs = (xsf[:, k * TT + t0: k * TT + t0 + tn]
                               if src_is_stream
                               else xpre_fs[gi % 2][:, k * 512: k * 512 + tn])
                        T.matmul(pb[0][:, 0:tn], wcur[:, k * 128:(k + 1) * 128],
                                 rhs, start=(k == 0), stop=(k == NCH - 1))
                    step(T.drain())
                    begin(V)
                    dsP.wait(gi % 2, eng=V)
                    V.tensor_copy([pstg, pstg2][gi % 2][:, 0:tn],
                                  pb[0][:, 0:tn])
                    step(V.drain())
                    begin(Sy)
                    dsP.issue(gi % 2,
                              [(dst[:, m, t0:t0 + tn],
                                [pstg, pstg2][gi % 2][:, 0:tn])])
                    step(Sy.nop())

        if _PHASES >= 1:
            proj_pass(wi["L0_attn_Wq"], qdr, True, main_tiles)
            proj_pass(wi["L0_attn_Wk"], kdr, True, main_tiles)
            proj_pass(wi["L0_attn_Wv"], vdr, True, main_tiles)
            proj_pass(wi["L0_attn_Wk"], kpre, False, pre_tiles)
            proj_pass(wi["L0_attn_Wv"], vpre, False, pre_tiles)

        C["mark_P1"] = (C["ch"], C["sd"])
        begin(V)
        V.memset(mst[:, :], 0.0)
        step(V.drain())

        def elu1(dst, src, tmp):
            begin(V)
            V.tensor_scalar_min(tmp, src, 0.0)
            step(V.drain())
            begin(A)
            A.activation(tmp, tmp, AF.Exp)
            step(A.drain())
            begin(V)
            V.scalar_tensor_tensor(dst, src, 0.0, tmp, ALU.max, ALU.add)
            step(V.drain())

        SGS = [(qsg, ksg, kmsg, vsg), (qsg2, ksg2, kmsg2, vsg2)]

        def head_kv_issue(b, ksrc, vsrc, qsrc, col0, with_q):
            qs, ks, km, vs = SGS[b]
            pairs = [(ks[:, :].rearrange("p (a b) -> p a b", b=32),
                      ksrc[:, :, col0:col0 + 32]),
                     (vs[:, :].rearrange("p (a b) -> p a b", b=32),
                      vsrc[:, :, col0:col0 + 32])]
            if with_q:
                pairs.append((qs[:, :].rearrange("p (a b) -> p a b", b=32),
                              qsrc[:, :, col0:col0 + 32]))
            dsH.issue(b, pairs)

        def head_kv_mirror(b):
            qs, ks, km, vs = SGS[b]
            dsH.issue(b, [(km[0:64, :], ks[64:128, :]),
                          (km[64:128, :], ks[0:64, :])])

        def head_tm(sg, with_sk=True, parts='all', pre_T=None):
            for rnd, (src, dstt, stridew) in enumerate(
                    [(sg[1], ktm, 64), (sg[3], vtm, VS)]):
                begin(T)
                if rnd == 0 and pre_T is not None:
                    pre_T()
                for L in range(4):
                    clo, g = L // 2, L % 2
                    idn = ident[64 * clo:64 * clo + 64, 64 * clo:64 * clo + 64]
                    T.transpose(pb[4 + L][:, 0:64],
                                src[64 * clo:64 * clo + 64,
                                    g * 128:(g + 1) * 128], idn)
                step(T.drain())
                begin(V)
                for L in range(4):
                    V.tensor_copy(dstt[:, L * stridew:L * stridew + 64],
                                  pb[4 + L][:, 0:64])
                if rnd == 1 and parts != 'tp':
                    V.memset(vtm.rearrange("p (a b) -> p a b", b=VS)[:, :, 64:65],
                             1.0)
                    if parts != 'ones' and with_sk:
                        V.tensor_scalar_min(denr[:, 0:256], ktm[:, :], 0.0)
                step(V.drain())
            if parts in ('tp', 'ones'):
                return
            if with_sk:
                begin(A)
                A.activation(denr[:, 0:256], denr[:, 0:256], AF.Exp)
                step(A.drain())
                begin(V)
                V.scalar_tensor_tensor(sktm[:, :], ktm[:, :], 0.0,
                                       denr[:, 0:256], ALU.max, ALU.add)
                step(V.drain())

        def dm_update(h, mfac=None):
            begin(T)
            for base in range(2):
                for L in range(4):
                    T.matmul(pb[7][64 * base:64 * base + 64, 0:DV + 1],
                             sktm[:, L * 64:(L + 1) * 64],
                             vtm[:, L * VS:L * VS + 65],
                             start=(L == 0), stop=(L == 3))
            step(T.drain())
            begin(V)
            for base in range(2):
                sl = slice(64 * base, 64 * base + 64)
                if mfac is None:
                    V.tensor_add(mst[sl, h * MS:h * MS + 65],
                                 mst[sl, h * MS:h * MS + 65],
                                 pb[7][sl, 0:DV + 1])
                else:
                    V.scalar_tensor_tensor(mst[sl, h * MS:h * MS + 65],
                                           pb[7][sl, 0:DV + 1], mfac[sl, :],
                                           mst[sl, h * MS:h * MS + 65],
                                           ALU.mult, ALU.add)
            step(V.drain())

        fM = stats[:, 4:5]
        begin(Sy)
        dsP.wait(0)
        dsP.wait(1)
        m0 = wi["L0_attn_memory0"]
        z0 = wi["L0_attn_z0"]
        dmas(m0tmp[0:64, :, 0:DK], AP(m0.tensor, 0, [[DV, 64], [DK * DV, H], [1, DV]]))
        dmas(m0tmp[64:128, :, 0:DK], AP(m0.tensor, 0, [[DV, 64], [DK * DV, H], [1, DV]]))
        dmas(m0tmp[0:64, :, DK:DK + 1], AP(z0.tensor, 0, [[1, 64], [DK, H]]))
        dmas(m0tmp[64:128, :, DK:DK + 1], AP(z0.tensor, 0, [[1, 64], [DK, H]]))
        sy_step()
        begin(V)
        V.tensor_add(mst[:, :], mst[:, :],
                     m0tmp[:, :, :].rearrange("p a b -> p (a b)"))
        step(V.drain())

        gq = arv(2064, [8])
        gk = arv(2072, [8])
        gv = arv(2080, [8])
        gkm = arv(2088, [8])
        gsq = arv(2096, [8])
        gnum = arv(2112, [16])
        gattd = arv(2128, [16])
        gden = arv(2144, [16])
        gstg = arv(2160, [64])
        gvt = arv(2224, [68])
        gptm = arv(2292, [64])
        gsc = arv(2356, [1024])
        gmk = arv(3380, [1024])
        ghatt_f = ghatt.rearrange("p a b -> p (a b)")

        def ghost_softmax(ncols):
            v2 = gsc[0:8, 0:2 * ncols].rearrange("p (a b) -> p a b", b=ncols)
            st2 = stats[0:8, 56:58]
            begin(V)
            V.tensor_reduce(st2[:, :, None], v2, axis=AX.X, op=ALU.max,
                            negate=True)
            step(V.drain())
            bca = AP(st2.tensor, st2.offset, [[64, 8], [1, 2], [0, ncols]])
            begin(V)
            V.tensor_tensor(v2, v2, bca, ALU.add)
            step(V.drain())
            begin(A)
            A.activation(gsc[0:8, 0:2 * ncols], gsc[0:8, 0:2 * ncols], AF.Exp,
                         scale=SC0)
            step(A.drain())
            begin(V)
            V.tensor_reduce(st2[:, :, None], v2, axis=AX.X, op=ALU.add)
            V.drain()
            V.reciprocal(st2, st2)
            step(V.drain())
            begin(V)
            V.tensor_tensor(v2, v2, bca, ALU.mult)
            step(V.drain())

        def ghost_mem_combine(h):
            begin(T)
            T.matmul(pb[7][0:DK + 1, 0:8], mst[0:64, h * MS:h * MS + 65],
                     gsq[0:64, 0:8], start=True, stop=True)
            T.drain()
            T.matmul(pb[7][0:DK + 1, 8:16], mst[64:128, h * MS:h * MS + 65],
                     gsq[64:128, 0:8], start=True, stop=True)
            step(T.drain())
            begin(V)
            V.reciprocal(gden[64:65, 0:16], pb[7][64:65, 0:16])
            step(V.drain())
            begin(T)
            step(T.matmul(pb[2][0:64, 0:16], ones[64:65, 0:64],
                          gden[64:65, 0:16], start=True, stop=True))
            begin(V)
            V.tensor_copy(gnum[0:64, 0:16], pb[7][0:64, 0:16])
            V.drain()
            V.tensor_mul(gnum[0:64, 0:16], gnum[0:64, 0:16], pb[2][0:64, 0:16])
            V.drain()
            V.tensor_sub(gnum[0:64, 0:16], gnum[0:64, 0:16], gattd[0:64, 0:16])
            V.drain()
            V.scalar_tensor_tensor(gnum[0:64, 0:16], gnum[0:64, 0:16],
                                   gates[0:64, h:h + 1], gattd[0:64, 0:16],
                                   ALU.mult, ALU.add)
            step(V.drain())

        def ghost_store(colidx):
            begin(V)
            V.tensor_copy(ghatt[0:64, :, colidx:colidx + 1],
                          gnum[0:64, 0:8][:, :, None])
            V.tensor_copy(gstg[0:64, 0:8], gnum[0:64, 8:16])
            step(V.drain())
            begin(Sy)
            dmas(ghatt[64:128, :, colidx:colidx + 1], gstg[0:64, 0:8][:, :, None])
            sy_step()

        def ghost_L(sg):
            begin(Sy)
            dmas(gq[:, 0:8], qdr[:, :, 0:1])
            sy_step()
            begin(T)
            for ci in range(2):
                lhsT = gq[64 * ci:64 * ci + 64, 0:8]
                for cj in range(2):
                    srct = sg[1] if cj == ci else sg[2]
                    T.matmul(pb[ci][0:8, cj * 256:cj * 256 + 256], lhsT,
                             srct[64 * ci:64 * ci + 64, 0:256],
                             start=(cj == 0), stop=(cj == 1))
            step(T.drain())
            begin(G)
            G.iota(gmk[:, :].rearrange("p (a b c d) -> p a b c d", b=2, c=8, d=32),
                   pattern=[[-1, 2], [1, 2], [2, 8], [16, 32]], base=-496,
                   channel_multiplier=-2, allow_small_or_imprecise_dtypes=True)
            step(G.drain())
            begin(V)
            V.tensor_scalar(gmk[:, :], gmk[:, :], 0.0, NEG, ALU.is_gt, ALU.mult)
            V.drain()
            for ci in range(2):
                V.tensor_add(gsc[0:8, ci * 512:(ci + 1) * 512],
                             pb[ci][0:8, 0:512], gmk[0:8, ci * 512:(ci + 1) * 512])
            step(V.drain())
            ghost_softmax(512)
            for ci in range(2):
                begin(T)
                for Jc in range(4):
                    T.transpose(pb[2 + Jc][0:128, 0:8],
                                gsc[0:8, ci * 512 + Jc * 128: ci * 512 + (Jc + 1) * 128],
                                ident[0:8, 0:8])
                step(T.drain())
                begin(V)
                for Jc in range(4):
                    V.tensor_copy(gptm[:, Jc * 16 + ci * 8: Jc * 16 + ci * 8 + 8],
                                  pb[2 + Jc][0:128, 0:8])
                step(V.drain())
            begin(T)
            for Jc in range(4):
                T.matmul(pb[6][0:DV, 0:16], vtm[:, Jc * VS:Jc * VS + 64],
                         gptm[:, Jc * 16:(Jc + 1) * 16],
                         start=(Jc == 0), stop=(Jc == 3))
            step(T.drain())
            begin(V)
            V.tensor_copy(gattd[0:64, 0:16], pb[6][0:DV, 0:16])
            step(V.drain())
            for clo in range(2):
                sl = slice(64 * clo, 64 * clo + 64)
                elu1(gsq[sl, 0:8], gq[sl, 0:8], denr[sl, 0:8])
            ghost_mem_combine(15)
            ghost_store(0)

        def ghost_R():
            begin(Sy)
            dmas(gq[:, 0:8], qdr[:, :, TT - 1:TT])
            dmas(gk[:, 0:8], kdr[:, :, TT - 1:TT])
            dmas(gv[:, 0:8], vdr[:, :, TT - 1:TT])
            sy_step()
            begin(Sy)
            dma(gkm[0:64, 0:8], gk[64:128, 0:8])
            dma(gkm[64:128, 0:8], gk[0:64, 0:8])
            sy_step()
            begin(T)
            for ci in range(2):
                lhsT = gq[64 * ci:64 * ci + 64, 0:8]
                for cj in range(2):
                    srct = gk if cj == ci else gkm
                    T.matmul(pb[ci][0:8, cj * 8:cj * 8 + 8], lhsT,
                             srct[64 * ci:64 * ci + 64, 0:8],
                             start=(cj == 0), stop=(cj == 1))
            step(T.drain())
            begin(G)
            G.iota(gmk[:, 0:32].rearrange("p (a b c) -> p a b c", b=2, c=8),
                   pattern=[[-1, 2], [1, 2], [2, 8]], base=0,
                   channel_multiplier=-2, allow_small_or_imprecise_dtypes=True)
            step(G.drain())
            begin(V)
            V.tensor_scalar(gmk[:, 0:32], gmk[:, 0:32], 0.0, NEG,
                            ALU.is_gt, ALU.mult)
            V.drain()
            for ci in range(2):
                V.tensor_add(gsc[0:8, ci * 16:(ci + 1) * 16],
                             pb[ci][0:8, 0:16], gmk[0:8, ci * 16:(ci + 1) * 16])
            step(V.drain())
            ghost_softmax(16)
            begin(T)
            T.transpose(pb[2][0:8, 0:64], gv[0:64, 0:8], ident[0:64, 0:64])
            T.drain()
            T.transpose(pb[3][0:8, 0:64], gv[64:128, 0:8], ident[64:128, 64:128])
            step(T.drain())
            begin(V)
            V.tensor_copy(gvt[0:8, 0:64], pb[2][0:8, 0:64])
            V.tensor_copy(gstg[0:8, 0:64], pb[3][0:8, 0:64])
            step(V.drain())
            begin(Sy)
            dma(gvt[8:16, 0:64], gstg[0:8, 0:64])
            sy_step()
            begin(T)
            T.transpose(pb[4][0:16, 0:8], gsc[0:8, 0:16], ident[0:8, 0:8])
            T.drain()
            T.transpose(pb[5][0:16, 0:8], gsc[0:8, 16:32], ident[0:8, 0:8])
            step(T.drain())
            begin(V)
            V.tensor_copy(gptm[0:16, 0:8], pb[4][0:16, 0:8])
            V.tensor_copy(gptm[0:16, 8:16], pb[5][0:16, 0:8])
            step(V.drain())
            begin(T)
            step(T.matmul(pb[6][0:DV, 0:16], gvt[0:16, 0:64],
                          gptm[0:16, 0:16], start=True, stop=True))
            begin(V)
            V.tensor_copy(gattd[0:64, 0:16], pb[6][0:DV, 0:16])
            step(V.drain())
            for clo in range(2):
                sl = slice(64 * clo, 64 * clo + 64)
                elu1(gsq[sl, 0:8], gq[sl, 0:8], denr[sl, 0:8])
            ghost_mem_combine(0)
            ghost_store(1)

        if _PHASES >= 2:
            heads2 = [(seg, h) for seg in range(NSEG) for h in range(H)]
            begin(Sy)
            head_kv_issue(0, kpre, vpre, None, heads2[0][0] * LSEG, False)
            step(Sy.nop())
            begin(Sy)
            dsH.wait(0)
            head_kv_mirror(0)
            step(Sy.nop())
            for i, (seg, h) in enumerate(heads2):
                b = i % 2
                begin(Sy)
                dsH.wait(b)
                if i + 1 < len(heads2):
                    s2, h2 = heads2[i + 1]
                    head_kv_issue(1 - b, kpre, vpre, None,
                                  s2 * LSEG + h2 * 32, False)
                step(Sy.nop())
                head_tm(SGS[b], with_sk=True)
                if _GHOSTS and seg == 7 and h == 15:
                    ghost_L(SGS[b])
                dm_update(h, mfac=fM)
                if i + 1 < len(heads2):
                    begin(Sy)
                    dsH.wait(1 - b)
                    head_kv_mirror(1 - b)
                    step(Sy.nop())

        C["mark_P2"] = (C["ch"], C["sd"])
        def softmax_block(blk):
            bank = pb[blk]
            begin(V)
            V.tensor_add(bank[:, :], bank[:, :], mask[:, blk, :])
            V.drain()
            V.reduce_max(stats[:, 16:17], bank[:, :], axis=AX.X, negate=True)
            V.drain()
            step(V.tensor_scalar_mul(stats[:, 17:18], stats[:, 16:17], SC0))
            begin(A)
            A.activation(psb[:, blk, :], bank[:, :], AF.Exp, scale=SC0,
                         bias=stats[:, 17:18], accum_out=stats[:, 18:19])
            A.drain()
            A.activation(stats[:, 19:20], stats[:, 18:19], AF.Ln)
            step(A.drain())
            begin(V)
            step(V.tensor_sub(stats[:, 20:21], stats[:, 17:18], stats[:, 19:20]))
            begin(A)
            A.activation(psb[:, blk, :], bank[:, :], AF.Exp, scale=SC0,
                         bias=stats[:, 20:21])
            step(A.drain())

        def softmax_all():
            begin(V)
            for blk in range(4):
                V.tensor_add(pb[blk][:, :], pb[blk][:, :], mask[:, blk, :])
            V.drain()
            for blk in range(4):
                V.reduce_max(stats[:, 16 + blk:17 + blk], pb[blk][:, :],
                             axis=AX.X, negate=True)
            V.drain()
            for blk in range(4):
                V.tensor_scalar_mul(stats[:, 20 + blk:21 + blk],
                                    stats[:, 16 + blk:17 + blk], SC0)
            step(V.drain())
            begin(A)
            for blk in range(4):
                A.activation(psb[:, blk, :], pb[blk][:, :], AF.Exp, scale=SC0,
                             bias=stats[:, 20 + blk:21 + blk],
                             accum_out=stats[:, 24 + blk:25 + blk])
                A.drain()
            for blk in range(4):
                A.activation(stats[:, 28 + blk:29 + blk],
                             stats[:, 24 + blk:25 + blk], AF.Ln)
            step(A.drain())
            begin(V)
            for blk in range(4):
                V.tensor_sub(stats[:, 32 + blk:33 + blk],
                             stats[:, 20 + blk:21 + blk],
                             stats[:, 28 + blk:29 + blk])
            step(V.drain())
            begin(A)
            for blk in range(4):
                A.activation(psb[:, blk, :], pb[blk][:, :], AF.Exp, scale=SC0,
                             bias=stats[:, 32 + blk:33 + blk])
            step(A.drain())

        def attn_head(sg, qtr, hq, h):
            qsg_, ksg_, kmsg_, vsg_ = sg
            mode = _P3MODE

            def scores_T():
                for blk in range(4):
                    clo_i, ihalf = blk // 2, blk % 2
                    lhsT = qsg_[64 * clo_i:64 * clo_i + 64,
                                ihalf * 128:(ihalf + 1) * 128]
                    for clo_j in range(2):
                        srct = ksg_ if clo_j == clo_i else kmsg_
                        rhs = srct[64 * clo_i:64 * clo_i + 64, 0:256]
                        T.matmul(pb[blk][:, clo_j * 256:clo_j * 256 + 256],
                                 lhsT, rhs, start=(clo_j == 0),
                                 stop=(clo_j == 1))

            head_tm(sg, with_sk=True, pre_T=scores_T)
            if mode == 'tm':
                return
            softmax_all()
            if mode == 'softmax':
                return
            begin(T)
            for Ib in range(4):
                T.transpose(pb[Ib][:, 0:128],
                            psb[:, Ib, 0:128], ident[:, :])
            step(T.drain())
            for Jc in range(4):
                begin(V)
                for Ib in range(4):
                    V.tensor_copy(ptsb[:, Ib * 128:(Ib + 1) * 128],
                                  pb[Ib][:, 0:128])
                step(V.drain())
                begin(T)
                T.matmul(pb[4][0:DV, 0:512], vtm[:, Jc * VS:Jc * VS + 64],
                         ptsb[:, 0:512], start=(Jc == 0), stop=(Jc == 3))
                if Jc < 3:
                    for Ib in range(4):
                        T.transpose(pb[Ib][:, 0:128],
                                    psb[:, Ib, (Jc + 1) * 128:(Jc + 2) * 128],
                                    ident[:, :])
                step(T.drain())
            begin(V)
            V.tensor_copy(attd[0:64, 0:512], pb[4][0:DV, 0:512])
            for clo in range(2):
                sl = slice(64 * clo, 64 * clo + 64)
                V.tensor_scalar_min(denr[sl, 0:256], qsg_[sl, :], 0.0)
            step(V.drain())
            if mode == 'pt':
                return
            begin(A)
            for clo in range(2):
                sl = slice(64 * clo, 64 * clo + 64)
                A.activation(denr[sl, 0:256], denr[sl, 0:256], AF.Exp)
            step(A.drain())
            begin(V)
            for clo in range(2):
                sl = slice(64 * clo, 64 * clo + 64)
                V.scalar_tensor_tensor(sq_[sl, :], qsg_[sl, :], 0.0,
                                       denr[sl, 0:256], ALU.max, ALU.add)
            step(V.drain())
            if mode == 'elu':
                return
            begin(T)
            T.matmul(pb[5][0:DK + 1, 0:256], mst[0:64, h * MS:h * MS + 65],
                     sq_[0:64, :], start=True, stop=True)
            T.drain()
            T.matmul(pb[5][0:DK + 1, 256:512], mst[64:128, h * MS:h * MS + 65],
                     sq_[64:128, :], start=True, stop=True)
            step(T.drain())
            begin(V)
            V.reciprocal(denr[64:65, 0:512], pb[5][64:65, 0:512])
            V.tensor_copy(nums[0:64, 0:512], pb[5][0:64, 0:512])
            step(V.drain())
            begin(T)
            step(T.matmul(pb[6][0:64, 0:512], ones[64:65, 0:64],
                          denr[64:65, 0:512], start=True, stop=True))
            if mode == 'mem':
                return
            begin(V)
            dsS.wait(0, eng=V)
            V.tensor_mul(nums[0:64, 0:512], nums[0:64, 0:512], pb[6][0:64, 0:512])
            V.drain()
            V.tensor_sub(nums[0:64, 0:512], nums[0:64, 0:512], attd[0:64, 0:512])
            V.drain()
            c0 = qtr * 128 + hq * 32
            V.scalar_tensor_tensor(
                attfm[0:64, :, c0:c0 + 32],
                nums[0:64, 0:256].rearrange("p (a b) -> p a b", b=32),
                gates[0:64, h:h + 1],
                attd[0:64, 0:256].rearrange("p (a b) -> p a b", b=32),
                ALU.mult, ALU.add)
            V.scalar_tensor_tensor(
                stg[0:64, 0:256], nums[0:64, 256:512], gates[0:64, h:h + 1],
                attd[0:64, 256:512], ALU.mult, ALU.add)
            step(V.drain())
            begin(Sy)
            dsS.issue(0, [(attfm[64:128, :, c0:c0 + 32],
                           stg[0:64, 0:256].rearrange("p (a b) -> p a b", b=32))])
            step(Sy.nop())
            dm_update(h)

        def emit_ln(t0, tn):
            for half in range(2):
                ca = 4 * half
                begin(V)
                for c in range(ca, ca + 4):
                    V.tensor_mul(sqb[:, (c - ca) * 512:(c - ca) * 512 + tn],
                                 xs[:, c, t0:t0 + tn], xs[:, c, t0:t0 + tn])
                step(V.drain())
                begin(T)
                for c in range(ca, ca + 4):
                    T.matmul(pb[6][0:1, 0:tn], ones[:, 0:1],
                             xsf[:, c * TT + t0: c * TT + t0 + tn],
                             start=(c == 0), stop=(c == 7))
                for c in range(ca, ca + 4):
                    T.matmul(pb[7][0:1, 0:tn], ones[:, 0:1],
                             sqb[:, (c - ca) * 512:(c - ca) * 512 + tn],
                             start=(c == 0), stop=(c == 7))
                step(T.drain())
            murow = lnrows[0:1, 0:512]
            varrow = lnrows[0:1, 512:1024]
            mrsrow = lnrows[0:1, 1024:1536]
            begin(V)
            V.tensor_scalar_mul(murow[:, 0:tn], pb[6][0:1, 0:tn], 1.0 / D)
            V.drain()
            V.tensor_mul(varrow[:, 0:tn], murow[:, 0:tn], murow[:, 0:tn])
            V.drain()
            V.scalar_tensor_tensor(varrow[:, 0:tn], pb[7][0:1, 0:tn], 1.0 / D,
                                   varrow[:, 0:tn], ALU.mult, ALU.subtract)
            step(V.drain())
            begin(A)
            A.activation(varrow[:, 0:tn], varrow[:, 0:tn], AF.Ln,
                         bias=stats[0:1, 13:14])
            A.drain()
            A.activation(varrow[:, 0:tn], varrow[:, 0:tn], AF.Exp, scale=-0.5)
            step(A.drain())
            begin(V)
            V.tensor_mul(mrsrow[:, 0:tn], murow[:, 0:tn], varrow[:, 0:tn])
            step(V.drain())
            begin(T)
            T.matmul(pb[4][:, 0:tn], ones[0:1, 0:128], varrow[:, 0:tn],
                     start=True, stop=True)
            T.matmul(pb[5][:, 0:tn], ones[0:1, 0:128], mrsrow[:, 0:tn],
                     start=True, stop=True)
            step(T.drain())
            begin(V)
            for c in range(NCH):
                V.tensor_mul(xs[:, c, t0:t0 + tn], xs[:, c, t0:t0 + tn],
                             pb[4][:, 0:tn])
            V.drain()
            for c in range(NCH):
                V.tensor_sub(xs[:, c, t0:t0 + tn], xs[:, c, t0:t0 + tn],
                             pb[5][:, 0:tn])
            step(V.drain())

        heads3 = ([(seg, h) for seg in range(NSEG) for h in range(H)]
                  if _PHASES >= 3 else [])
        if heads3:
            begin(Sy)
            head_kv_issue(0, kdr, vdr, qdr, 1 + heads3[0][0] * LSEG, True)
            step(Sy.nop())
            begin(Sy)
            dsH.wait(0)
            head_kv_mirror(0)
            step(Sy.nop())
        for i, (seg, h) in enumerate(heads3):
            qtr, hq = h // 4, h % 4
            b = i % 2
            begin(Sy)
            dsH.wait(b)
            if i + 1 < len(heads3):
                s2, h2 = heads3[i + 1]
                head_kv_issue(1 - b, kdr, vdr, qdr,
                              1 + s2 * LSEG + h2 * 32, True)
            step(Sy.nop())
            attn_head(SGS[b], qtr, hq, h)
            if i + 1 < len(heads3):
                begin(Sy)
                dsH.wait(1 - b)
                head_kv_mirror(1 - b)
                step(Sy.nop())
            if h != H - 1:
                continue
            segc = 1 + seg * LSEG
            begin(Sy)
            dsS.wait(0)
            w_issue(0, wi["L0_attn_Wo"], 0)
            step(Sy.nop())
            for m in range(NCH):
                begin(Sy)
                dsW.wait(m % 2)
                if m + 1 < NCH:
                    w_issue((m + 1) % 2, wi["L0_attn_Wo"], m + 1)
                step(Sy.nop())
                wcur = [wb, wb2][m % 2]
                begin(T)
                for k in range(NCH):
                    T.matmul(pb[0][:, 0:512], wcur[:, k * 128:(k + 1) * 128],
                             attfm_f[:, k * 512:(k + 1) * 512],
                             start=(k == 0), stop=(k == NCH - 1))
                step(T.drain())
                begin(V)
                step(V.tensor_add(xs[:, m, segc:segc + 512],
                                  xs[:, m, segc:segc + 512], pb[0][:, 0:512]))
            emit_ln(segc, 512)

        if _GHOSTS and _PHASES >= 3:
            ghost_R()
            for m in range(NCH):
                begin(Sy)
                for k in range(NCH):
                    dma(wb[:, k * 128:(k + 1) * 128],
                        wi["L0_attn_Wo"][k * 128:(k + 1) * 128,
                                         m * 128:(m + 1) * 128])
                sy_step()
                begin(T)
                for k in range(NCH):
                    T.matmul(pb[0][:, 0:2], wb[:, k * 128:(k + 1) * 128],
                             ghatt_f[:, k * 2:(k + 1) * 2],
                             start=(k == 0), stop=(k == NCH - 1))
                step(T.drain())
                begin(V)
                V.scalar_tensor_tensor(xs[:, m, 0:1], pb[0][:, 0:1],
                                       stats[:, 5:6], xs[:, m, 0:1],
                                       ALU.mult, ALU.add)
                V.drain()
                V.scalar_tensor_tensor(xs[:, m, TT - 1:TT], pb[0][:, 1:2],
                                       stats[:, 6:7], xs[:, m, TT - 1:TT],
                                       ALU.mult, ALU.add)
                step(V.drain())
            emit_ln(0, 1)
            emit_ln(TT - 1, 1)

        if "l0attn" in taps:
            begin(Sy)
            dma(dbg["l0attn"], xs[:, :, :])
            sy_step()

        C["mark_P3"] = (C["ch"], C["sd"])
        wpA = arv(0, [512])
        wpB = arv(512, [512])
        wpC = arv(1024, [512])
        wpD = arv(1536, [512])
        wpS = arv(2048, [512])
        wpM = arv(2560, [128])

        MID = [0.25, 0.75, 1.25, 1.75, 2.5, 3.5, 5.0]

        def fp4_stair(asrc, chain, chain2, nb16, wide):
            begin(V)
            V.tensor_scalar(chain, asrc, MID[0], None, ALU.is_gt)
            V.drain()
            for j in (1, 2, 3):
                V.scalar_tensor_tensor(chain, asrc, MID[j], chain,
                                       ALU.is_gt, ALU.add)
                V.drain()
            V.tensor_scalar(chain2, asrc, MID[4], None, ALU.is_gt)
            V.drain()
            V.scalar_tensor_tensor(chain2, asrc, MID[5], chain2,
                                   ALU.is_gt, ALU.add)
            V.drain()
            V.tensor_scalar(asrc, asrc, MID[6], 4.0, ALU.is_gt, ALU.mult)
            V.drain()
            V.scalar_tensor_tensor(chain2, chain2, 2.0, asrc, ALU.mult, ALU.add)
            V.drain()
            V.tensor_add(chain, chain, chain2)
            step(V.drain())

        def fp4_quant_nat(w_ap, nb16):
            n = nb16 * 16
            bm = wpM[:, 0:nb16]
            sh = wpM[:, 32:32 + nb16]
            rr = wpM[:, 64:64 + nb16]
            begin(V)
            V.tensor_reduce(bm[:, :, None], w_ap.rearrange("p (a b) -> p a b", b=16),
                            axis=AX.X, op=ALU.max, apply_absolute_value=True)
            V.drain()
            V.tensor_scalar(sh, bm, 1.0 / 6.0, 1e-12, ALU.mult, ALU.max)
            V.drain()
            V.reciprocal(rr, sh)
            V.drain()
            V.tensor_scalar_mul(sh, sh, 0.5)
            step(V.drain())
            rb = AP(rr.tensor, rr.offset, [[AR, 128], [1, nb16], [0, 16]])
            sb_ = AP(sh.tensor, sh.offset, [[AR, 128], [1, nb16], [0, 16]])
            begin(V)
            V.tensor_tensor(wpA[:, 0:n].rearrange("p (a b) -> p a b", b=16),
                            w_ap.rearrange("p (a b) -> p a b", b=16), rb, ALU.mult)
            step(V.drain())
            begin(A)
            A.activation(wpB[:, 0:n], wpA[:, 0:n], AF.Sign)
            A.drain()
            A.activation(wpA[:, 0:n], wpA[:, 0:n], AF.Abs)
            step(A.drain())
            fp4_stair(wpA[:, 0:n], wpC[:, 0:n], wpD[:, 0:n], nb16, n)
            begin(V)
            V.tensor_tensor(wpC[:, 0:n].rearrange("p (a b) -> p a b", b=16),
                            wpC[:, 0:n].rearrange("p (a b) -> p a b", b=16),
                            sb_, ALU.mult)
            V.drain()
            V.tensor_mul(wpC[:, 0:n], wpC[:, 0:n], wpB[:, 0:n])
            step(V.drain())

        wpS2 = arv(4608, [512])
        psT = [0]

        def prep_store_T(src_ap, dstT, n0, kglob0, nk):
            for g0 in range(0, nk, 4):
                gn = min(4, nk - g0)
                gi = psT[0]
                psT[0] += 1
                sb = [wpS, wpS2][gi % 2]
                begin(T)
                for i in range(gn):
                    T.transpose(pb[i][:, 0:128],
                                src_ap[:, (g0 + i) * 128:(g0 + i + 1) * 128],
                                ident[:, :])
                step(T.drain())
                begin(V)
                dsP.wait(gi % 2, eng=V)
                for i in range(gn):
                    V.tensor_copy(sb[:, i * 128:(i + 1) * 128], pb[i][:, 0:128])
                step(V.drain())
                begin(Sy)
                dsP.issue(gi % 2,
                          [(dstT[:, kglob0 + g0:kglob0 + g0 + gn, n0:n0 + 128],
                            sb[:, 0:gn * 128].rearrange("p (a b) -> p a b",
                                                        b=128))])
                step(Sy.nop())

        WNB = None

        def prep_tiles(NW, KW):
            return [(n0, k0, min(512, KW - k0))
                    for n0 in range(0, NW, 128) for k0 in range(0, KW, 512)]

        def prep_loop(whbm, tiles, body):
            bufs = [wnat, wb3[:, 0:512]]
            begin(Sy)
            n0, k0, kn = tiles[0]
            dsX.issue(0, [(bufs[0][:, 0:kn], whbm[n0:n0 + 128, k0:k0 + kn])])
            step(Sy.nop())
            for i, (n0, k0, kn) in enumerate(tiles):
                b = i % 2
                begin(Sy)
                dsX.wait(b)
                if i + 1 < len(tiles):
                    n2, k2, kn2 = tiles[i + 1]
                    dsX.issue(1 - b, [(bufs[1 - b][:, 0:kn2],
                                       whbm[n2:n2 + 128, k2:k2 + kn2])])
                step(Sy.nop())
                body(bufs[b], n0, k0, kn)

        def prep_fp4_weight(whbm, dstT, NW, KW):
            def body(buf, n0, k0, kn):
                fp4_quant_nat(buf[:, 0:kn], kn // 16)
                prep_store_T(wpC, dstT, n0, k0 // 128, kn // 128)
            prep_loop(whbm, prep_tiles(NW, KW), body)

        def prep_plain_weight(whbm, dstT, NW, KW):
            def body(buf, n0, k0, kn):
                begin(V)
                V.tensor_copy(wpC[:, 0:kn], buf[:, 0:kn])
                step(V.drain())
                prep_store_T(wpC, dstT, n0, k0 // 128, kn // 128)
            prep_loop(whbm, prep_tiles(NW, KW), body)

        def prep_bit_weight(whbm, dstT, NW, KW, scol):
            acc = stats[:, 40:41]
            tmp = stats[:, 41:42]
            begin(V)
            V.memset(acc, 0.0)
            step(V.drain())
            def body1(buf, n0, k0, kn):
                begin(V)
                V.tensor_reduce(tmp, buf[:, 0:kn], axis=AX.X, op=ALU.add,
                                apply_absolute_value=True)
                V.drain()
                V.tensor_add(acc, acc, tmp)
                step(V.drain())
            prep_loop(whbm, prep_tiles(NW, KW), body1)
            begin(T)
            step(T.matmul(pb[0][0:1, 0:1], ones[:, 0:1], acc, start=True, stop=True))
            begin(V)
            V.tensor_scalar(stats[0:1, 43:44], pb[0][0:1, 0:1], 1.0 / (NW * KW),
                            1e-5, ALU.mult, ALU.max)
            V.drain()
            V.reciprocal(stats[0:1, 44:45], stats[0:1, 43:44])
            V.drain()
            V.tensor_scalar_mul(stats[0:1, 45:46], stats[0:1, 43:44], -1.0)
            step(V.drain())
            begin(T)
            T.matmul(pb[0][:, 0:1], ones[0:1, 0:128], stats[0:1, 44:45],
                     start=True, stop=True)
            T.matmul(pb[1][:, 0:1], ones[0:1, 0:128], stats[0:1, 45:46],
                     start=True, stop=True)
            step(T.drain())
            begin(V)
            V.tensor_copy(stats[:, scol:scol + 1], pb[0][:, 0:1])
            V.tensor_copy(stats[:, scol + 1:scol + 2], pb[1][:, 0:1])
            step(V.drain())
            def body2(buf, n0, k0, kn):
                begin(V)
                V.tensor_scalar(wpA[:, 0:kn], buf[:, 0:kn],
                                stats[:, scol:scol + 1], None, ALU.mult)
                V.drain()
                V.tensor_scalar(wpB[:, 0:kn], wpA[:, 0:kn], 0.5, None,
                                ALU.is_gt)
                V.drain()
                V.scalar_tensor_tensor(wpC[:, 0:kn], wpA[:, 0:kn], -0.5,
                                       wpB[:, 0:kn], ALU.is_lt, ALU.subtract)
                V.drain()
                V.tensor_scalar(wpC[:, 0:kn], wpC[:, 0:kn],
                                stats[:, scol + 1:scol + 2], None, ALU.mult)
                step(V.drain())
                prep_store_T(wpC, dstT, n0, k0 // 128, kn // 128)
            prep_loop(whbm, prep_tiles(NW, KW), body2)

        if _PHASES >= 4:
            if _P4MODE in ('fp4g', 'fp4', 'full'):
                prep_fp4_weight(wi["L0_ffn_w_gate"], wgT, DFF0, D)
            if _P4MODE in ('fp4', 'full'):
                prep_fp4_weight(wi["L0_ffn_w_up"], wuT, DFF0, D)
                prep_fp4_weight(wi["L0_ffn_w_down"], wdT, D, DFF0)
            if _P4MODE in ('plain', 'full'):
                prep_plain_weight(wi["L1_attn_Wq"], wqT1, D, D)
                prep_plain_weight(wi["L1_attn_Wk"], wkT1, D, D)
                prep_plain_weight(wi["L1_attn_Wv"], wvT1, D, D)
            if _P4MODE in ('bit', 'full'):
                prep_bit_weight(wi["L1_ffn_w1"], w1T, DFF1, D, 46)
                prep_bit_weight(wi["L1_ffn_w2"], w2T, D, DFF1, 49)
            for h in (range(H) if _P4MODE in ('phi', 'full') else []):
                begin(Sy)
                dma(wnat[0:64, 0:64],
                    AP(wi["L1_attn_weights"].tensor, h * DK * DK,
                       [[DK, 64], [1, DK]]))
                sy_step()
                begin(T)
                step(T.transpose(pb[0][0:64, 0:64],
                                 wnat[0:64, 0:64], ident[0:64, 0:64]))
                if h % 2 == 0:
                    begin(V)
                    V.tensor_copy(phiw[0:64, (h // 2) * 64:(h // 2) * 64 + 64],
                                  pb[0][0:64, 0:64])
                    step(V.drain())
                else:
                    begin(V)
                    dsP.wait(0, eng=V)
                    V.tensor_copy(wpS[0:64, 0:64], pb[0][0:64, 0:64])
                    step(V.drain())
                    begin(Sy)
                    dma(phiw[64:128, (h // 2) * 64:(h // 2) * 64 + 64],
                        wpS[0:64, 0:64])
                    sy_step()

        begin(Sy)
        dsP.wait(0)
        dsP.wait(1)
        step(Sy.nop())
        C["mark_P4"] = (C["ch"], C["sd"])
        xqF = arv(0, [NCH, 384])
        XT = arv(3072, [512])
        FB1 = arv(3584, [512])
        FB2 = arv(4096, [512])
        FB3 = arv(4608, [512])
        FB4 = arv(5120, [512])
        hqb = arv(5648, [NCH0, 384])
        xqF_f = xqF.rearrange("p a b -> p (a b)")
        hqb_f = hqb.rearrange("p a b -> p (a b)")

        qm5 = arv(14720, [128])
        mst0 = mst[:, 0:1024]

        def quant_fp4_tm(buf, nsub, w, nb16_tot):
            n = nsub * 128
            bm = qm5[:, 0:nb16_tot]
            sh = qm5[:, 32:32 + nb16_tot]
            rr = qm5[:, 64:64 + nb16_tot]
            begin(V)
            V.tensor_reduce(bm[:, :, None],
                            buf[:, 0:n].rearrange("p (a b) -> p a b", b=16),
                            axis=AX.X, op=ALU.max, apply_absolute_value=True)
            V.drain()
            V.tensor_scalar(sh, bm, 1.0 / 6.0, 1e-12, ALU.mult, ALU.max)
            V.drain()
            V.reciprocal(rr, sh)
            V.drain()
            V.tensor_scalar_mul(sh, sh, 0.5)
            step(V.drain())
            rb = AP(rr.tensor, rr.offset, [[AR, 128], [1, nb16_tot], [0, 16]])
            sb_ = AP(sh.tensor, sh.offset, [[AR, 128], [1, nb16_tot], [0, 16]])
            begin(V)
            V.tensor_tensor(buf[:, 0:n].rearrange("p (a b) -> p a b", b=16),
                            buf[:, 0:n].rearrange("p (a b) -> p a b", b=16),
                            rb, ALU.mult)
            step(V.drain())
            begin(A)
            A.activation(FB2[:, 0:n], buf[:, 0:n], AF.Sign)
            A.drain()
            A.activation(buf[:, 0:n], buf[:, 0:n], AF.Abs)
            step(A.drain())
            fp4_stair(buf[:, 0:n], FB3[:, 0:n], FB4[:, 0:n], nb16_tot, n)
            begin(V)
            V.tensor_tensor(FB3[:, 0:n].rearrange("p (a b) -> p a b", b=16),
                            FB3[:, 0:n].rearrange("p (a b) -> p a b", b=16),
                            sb_, ALU.mult)
            V.drain()
            V.tensor_mul(buf[:, 0:n], FB3[:, 0:n], FB2[:, 0:n])
            step(V.drain())

        def evac_tp(dst128, bank, w):
            if w < 128:
                V.memset(dst128, 0.0)
                V.drain()
            V.tensor_copy(dst128[0:w, 0:128] if w < 128 else dst128,
                          pb[bank][0:w, 0:128])

        def xq_fp4(t0, tn, dst, dst_f):
            for s0 in range(0, tn, 128):
                w = min(128, tn - s0)
                for fh in range(2):
                    begin(T)
                    for i in range(4):
                        c = fh * 4 + i
                        T.transpose(pb[i][0:w, 0:128],
                                    xsf[:, c * TT + t0 + s0: c * TT + t0 + s0 + w],
                                    ident[:, :])
                    step(T.drain())
                    begin(V)
                    for i in range(4):
                        evac_tp(XT[:, i * 128:(i + 1) * 128], i, w)
                    step(V.drain())
                    quant_fp4_tm(XT, 4, w, 32)
                    begin(T)
                    for i in range(4):
                        c = fh * 4 + i
                        T.transpose(pb[i][:, 0:128],
                                    XT[:, i * 128:(i + 1) * 128], ident[:, :])
                    step(T.drain())
                    begin(V)
                    for i in range(4):
                        c = fh * 4 + i
                        V.tensor_copy(dst_f[:, c * 384 + s0: c * 384 + s0 + w],
                                      pb[i][0:128, 0:w])
                    step(V.drain())

        if _PHASES >= 5:
            for (t0, tn) in _ttiles(TT, 384):
                xq_fp4(t0, tn, xqF, xqF_f)
                if t0 == 0 and 'p5xq' in _DUMPS:
                    begin(Sy)
                    dma(dmp['p5xq'], ar[:, :])
                    sy_step()
                begin(Sy)
                dsW.issue(0, [(wb[:, :].rearrange("p (a b) -> p a b", b=128),
                               wgT[:, :, 0:128])])
                dsW2.issue(0, [(wb2[:, :].rearrange("p (a b) -> p a b", b=128),
                                wuT[:, :, 0:128])])
                step(Sy.nop())
                for nch in range(NCH0):
                    nb = nch % 2
                    gbuf = [wb, mst0][nb]
                    ubuf = [wb2, mst0][0]
                    begin(Sy)
                    dsW.wait(nb)
                    dsW2.wait(nch % 2)
                    if nch + 1 < NCH0:
                        nbuf = [wb, mst0][1 - nb]
                        dsW.issue(1 - nb,
                                  [(nbuf[:, :].rearrange("p (a b) -> p a b", b=128),
                                    wgT[:, :, (nch + 1) * 128:(nch + 2) * 128])])
                    step(Sy.nop())
                    begin(T)
                    for k in range(NCH):
                        T.matmul(pb[4][:, 0:tn], gbuf[:, k * 128:(k + 1) * 128],
                                 xqF_f[:, k * 384:k * 384 + tn],
                                 start=(k == 0), stop=(k == NCH - 1))
                    step(T.drain())
                    begin(T)
                    for k in range(NCH):
                        T.matmul(pb[5][:, 0:tn], ubuf[:, k * 128:(k + 1) * 128],
                                 xqF_f[:, k * 384:k * 384 + tn],
                                 start=(k == 0), stop=(k == NCH - 1))
                    step(T.drain())
                    begin(Sy)
                    if nch + 1 < NCH0:
                        dsW2.issue((nch + 1) % 2,
                                   [(wb2[:, :].rearrange("p (a b) -> p a b", b=128),
                                     wuT[:, :, (nch + 1) * 128:(nch + 2) * 128])])
                    step(Sy.nop())
                    begin(A)
                    A.activation(XT[:, 0:tn], pb[4][:, 0:tn], AF.Silu)
                    step(A.drain())
                    begin(V)
                    V.tensor_mul(XT[:, 0:tn], XT[:, 0:tn], pb[5][:, 0:tn])
                    step(V.drain())
                    nsub = (tn + 127) // 128
                    begin(T)
                    for s in range(nsub):
                        w = min(128, tn - s * 128)
                        T.transpose(pb[[6, 7, 3][s % 3]][0:w, 0:128],
                                    XT[:, s * 128:s * 128 + w], ident[:, :])
                    step(T.drain())
                    begin(V)
                    for s in range(nsub):
                        w = min(128, tn - s * 128)
                        evac_tp(FB1[:, s * 128:(s + 1) * 128], [6, 7, 3][s % 3], w)
                    step(V.drain())
                    if t0 == 0 and nch == 0 and 'p5h0' in _DUMPS:
                        begin(Sy)
                        dma(dmp['p5h0'], ar[:, :])
                        sy_step()
                    quant_fp4_tm(FB1, nsub, 128, nsub * 8)
                    if t0 == 0 and nch == 0 and 'p5hq0' in _DUMPS:
                        begin(Sy)
                        dma(dmp['p5hq0'], ar[:, :])
                        sy_step()
                    begin(T)
                    for s in range(nsub):
                        T.transpose(pb[[6, 7, 3][s % 3]][:, 0:128],
                                    FB1[:, s * 128:(s + 1) * 128], ident[:, :])
                    step(T.drain())
                    begin(V)
                    for s in range(nsub):
                        w = min(128, tn - s * 128)
                        V.tensor_copy(hqb_f[:, nch * 384 + s * 128:
                                            nch * 384 + s * 128 + w],
                                      pb[[6, 7, 3][s % 3]][0:128, 0:w])
                    step(V.drain())
                if t0 == 0 and 'p5hqb' in _DUMPS:
                    begin(Sy)
                    dma(dmp['p5hqb'], ar[:, :])
                    sy_step()
                dgroups = [(m, kg, min(8, NCH0 - kg))
                           for m in range(NCH) for kg in range(0, NCH0, 8)]
                begin(Sy)
                m0_, kg0, gn0 = dgroups[0]
                dsW.issue(0, [(wb[:, 0:gn0 * 128].rearrange("p (a b) -> p a b", b=128),
                               wdT[:, kg0:kg0 + gn0, m0_ * 128:(m0_ + 1) * 128])])
                step(Sy.nop())
                for gi, (m, kg, gn) in enumerate(dgroups):
                    gb = gi % 2
                    begin(Sy)
                    dsW.wait(gb)
                    if gi + 1 < len(dgroups):
                        m2, kg2, gn2 = dgroups[gi + 1]
                        nbuf = [wb, mst0][1 - gb]
                        dsW.issue(1 - gb,
                                  [(nbuf[:, 0:gn2 * 128].rearrange("p (a b) -> p a b", b=128),
                                    wdT[:, kg2:kg2 + gn2, m2 * 128:(m2 + 1) * 128])])
                    step(Sy.nop())
                    cbuf = [wb, mst0][gb]
                    begin(T)
                    for i in range(gn):
                        T.matmul(pb[m][:, 0:tn],
                                 cbuf[:, i * 128:(i + 1) * 128],
                                 hqb_f[:, (kg + i) * 384:(kg + i) * 384 + tn],
                                 start=(kg + i == 0),
                                 stop=(kg + i == NCH0 - 1))
                    step(T.drain())
                for m in range(NCH):
                    begin(V)
                    step(V.tensor_add(xs[:, m, t0:t0 + tn], xs[:, m, t0:t0 + tn],
                                      pb[m][:, 0:tn]))
                emit_ln(t0, tn)

        if "ffn0" in taps:
            begin(Sy)
            dma(dbg["ffn0"], xs[:, :, :])
            sy_step()

        C["mark_P5"] = (C["ch"], C["sd"])
        qf1 = arv(5648, [NCH, 256])
        kf1 = arv(7696, [NCH, 256])
        vf1 = arv(9744, [NCH, 256])
        phq = arv(11792, [256])
        phk = arv(12048, [256])
        prod = arv(12304, [256])
        srow = arv(12560, [256])
        rsrow = arv(12816, [256])
        tv1 = arv(13072, [256])
        dum6 = arv(13328, [256])
        phiwQ = arv(13584, [512])
        phiwK = arv(14096, [512])
        qf1_f = qf1.rearrange("p a b -> p (a b)")
        kf1_f = kf1.rearrange("p a b -> p (a b)")
        vf1_f = vf1.rearrange("p a b -> p (a b)")

        def l1_proj(dstf, wT, bias_sb, t0, tn):
            begin(Sy)
            dsW.issue(0, [(wb[:, :].rearrange("p (a b) -> p a b", b=128),
                           wT[:, :, 0:128])])
            step(Sy.nop())
            for m in range(NCH):
                mb = m % 2
                begin(Sy)
                dsW.wait(mb)
                if m + 1 < NCH:
                    nbuf = [wb, wb2][1 - mb]
                    dsW.issue(1 - mb,
                              [(nbuf[:, :].rearrange("p (a b) -> p a b", b=128),
                                wT[:, :, (m + 1) * 128:(m + 2) * 128])])
                step(Sy.nop())
                wcur = [wb, wb2][mb]
                begin(T)
                for k in range(NCH):
                    T.matmul(pb[0][:, 0:tn], wcur[:, k * 128:(k + 1) * 128],
                             xsf[:, k * TT + t0: k * TT + t0 + tn],
                             start=(k == 0), stop=(k == NCH - 1))
                step(T.drain())
                begin(A)
                A.activation(dstf[:, m * 256:m * 256 + tn], pb[0][:, 0:tn],
                             AF.Identity, bias=bias_sb[:, m:m + 1])
                step(A.drain())

        if _PHASES >= 6:
            begin(V)
            V.memset(stats[:, 33:34], 0.0)
            V.memset(stats[:, 34:35], 0.0)
            step(V.drain())
            for (wTl, acc_col, b_sb) in [(wqT1, 33, biasq), (wkT1, 34, biask)]:
                begin(Sy)
                dsW.issue(0, [(wb[:, :].rearrange("p (a b) -> p a b", b=128),
                               wTl[:, :, 0:128])])
                step(Sy.nop())
                for m in range(NCH):
                    mb = m % 2
                    begin(Sy)
                    dsW.wait(mb)
                    if m + 1 < NCH:
                        nbuf = [wb, wb2][1 - mb]
                        dsW.issue(1 - mb,
                                  [(nbuf[:, :].rearrange("p (a b) -> p a b", b=128),
                                    wTl[:, :, (m + 1) * 128:(m + 2) * 128])])
                    step(Sy.nop())
                    wpa_cur = [wb, wb2][mb]
                    for (t0, tn) in _ttiles(TC, 512, start=1):
                        tn = min(tn, 4097 - t0)
                        begin(T)
                        for k in range(NCH):
                            T.matmul(pb[0][:, 0:tn],
                                     wpa_cur[:, k * 128:(k + 1) * 128],
                                     xsf[:, k * TT + t0: k * TT + t0 + tn],
                                     start=(k == 0), stop=(k == NCH - 1))
                        step(T.drain())
                        begin(A)
                        A.activation(XT[:, 0:tn], pb[0][:, 0:tn], AF.Square,
                                     bias=b_sb[:, m:m + 1],
                                     accum_out=stats[:, 35:36])
                        step(A.drain())
                        begin(V)
                        V.tensor_add(stats[:, acc_col:acc_col + 1],
                                     stats[:, acc_col:acc_col + 1],
                                     stats[:, 35:36])
                        step(V.drain())
            LN8 = 2.0794415416798357
            for (acc_col, rq_col, ls_col) in [(33, 36, 37), (34, 38, 39)]:
                begin(T)
                step(T.matmul(pb[0][0:1, 0:1], ones[:, 0:1],
                              stats[:, acc_col:acc_col + 1], start=True, stop=True))
                begin(V)
                V.tensor_copy(stats[0:1, 60:61], pb[0][0:1, 0:1])
                step(V.drain())
                begin(A)
                A.activation(stats[0:1, 61:62], stats[0:1, 60:61], AF.Ln)
                A.drain()
                A.activation(stats[0:1, 61:62], stats[0:1, 61:62], AF.Exp,
                             scale=-0.5)
                step(A.drain())
                begin(V)
                V.tensor_mul(stats[0:1, 62:63], stats[0:1, 61:62],
                             stats[0:1, 61:62])
                V.drain()
                V.tensor_mul(stats[0:1, 62:63], stats[0:1, 62:63],
                             stats[0:1, 60:61])
                V.drain()
                V.tensor_scalar(stats[0:1, 62:63], stats[0:1, 62:63], -0.5,
                                -LN8, ALU.mult, ALU.add)
                step(V.drain())
                begin(T)
                T.matmul(pb[0][:, 0:1], ones[0:1, 0:128], stats[0:1, 61:62],
                         start=True, stop=True)
                T.matmul(pb[1][:, 0:1], ones[0:1, 0:128], stats[0:1, 62:63],
                         start=True, stop=True)
                step(T.drain())
                begin(V)
                V.tensor_copy(stats[:, rq_col:rq_col + 1], pb[0][:, 0:1])
                V.tensor_copy(stats[:, ls_col:ls_col + 1], pb[1][:, 0:1])
                step(V.drain())
            begin(V)
            V.tensor_scalar(phiwQ[:, :], phiw[:, :], stats[:, 36:37], None,
                            ALU.mult)
            V.tensor_scalar(phiwK[:, :], phiw[:, :], stats[:, 38:39], None,
                            ALU.mult)
            step(V.drain())
            edge_tiles = ([(0, 1, 5), (TT - 1, 1, 6)] if _GHOSTS else [])
            for (t0, tn, fcol) in [(1 + i * 256, 256, None) for i in range(16)] \
                    + edge_tiles:
                l1_proj(qf1_f, wqT1, biasq, t0, tn)
                l1_proj(kf1_f, wkT1, biask, t0, tn)
                l1_proj(vf1_f, wvT1, biasv, t0, tn)
                for h in range(H):
                    hb = 64 * (h % 2)
                    hsl = slice(hb, hb + 64)
                    hcol = (h // 2) * 256
                    begin(T)
                    T.matmul(pb[0][0:64, 0:tn],
                             phiwQ[hsl, (h // 2) * 64:(h // 2) * 64 + 64],
                             qf1_f[hsl, hcol:hcol + tn], start=True, stop=True)
                    T.drain()
                    T.matmul(pb[1][0:64, 0:tn],
                             phiwK[hsl, (h // 2) * 64:(h // 2) * 64 + 64],
                             kf1_f[hsl, hcol:hcol + tn], start=True, stop=True)
                    step(T.drain())
                    begin(A)
                    A.activation(phq[0:64, 0:tn], pb[0][0:64, 0:tn], AF.Exp,
                                 bias=stats[0:64, 37:38])
                    A.drain()
                    A.activation(phk[0:64, 0:tn], pb[1][0:64, 0:tn], AF.Exp,
                                 bias=stats[0:64, 39:40])
                    step(A.drain())
                    begin(V)
                    V.tensor_mul(prod[0:64, 0:tn], phq[0:64, 0:tn],
                                 phk[0:64, 0:tn])
                    step(V.drain())
                    begin(T)
                    step(T.matmul(pb[2][0:1, 0:tn], ones[0:64, 0:1],
                                  prod[0:64, 0:tn], start=True, stop=True))
                    begin(V)
                    V.tensor_copy(srow[0:1, 0:tn], pb[2][0:1, 0:tn])
                    V.drain()
                    V.reciprocal(rsrow[0:1, 0:tn], srow[0:1, 0:tn])
                    step(V.drain())
                    begin(T)
                    T.matmul(pb[3][hb:hb + 64, 0:tn], ones[0:1, 0:64],
                             srow[0:1, 0:tn], start=True, stop=True)
                    T.drain()
                    T.matmul(pb[4][hb:hb + 64, 0:tn], ones[0:1, 0:64],
                             rsrow[0:1, 0:tn], start=True, stop=True)
                    step(T.drain())
                    begin(V)
                    V.tensor_mul(tv1[hsl, 0:tn],
                                 vf1_f[hsl, hcol:hcol + tn],
                                 pb[3][hsl, 0:tn])
                    V.drain()
                    V.tensor_mul(tv1[hsl, 0:tn], tv1[hsl, 0:tn],
                                 pb[4][hsl, 0:tn])
                    V.drain()
                    if fcol is None:
                        V.tensor_add(xs[hsl, h // 2, t0:t0 + tn],
                                     xs[hsl, h // 2, t0:t0 + tn], tv1[hsl, 0:tn])
                    else:
                        V.scalar_tensor_tensor(xs[hsl, h // 2, t0:t0 + tn],
                                               tv1[hsl, 0:tn],
                                               stats[hsl, fcol:fcol + 1],
                                               xs[hsl, h // 2, t0:t0 + tn],
                                               ALU.mult, ALU.add)
                    step(V.drain())
                emit_ln(t0, tn)

        if "attn1" in taps:
            begin(Sy)
            dma(dbg["attn1"], xs[:, :, :])
            sy_step()

        C["mark_P6"] = (C["ch"], C["sd"])
        ari = sbt("ari", [128, 512], I32)
        hb7 = arv(5648, [NCH1, 322])
        xq7 = arv(0, [NCH, 322])
        hb7_f = hb7.rearrange("p a b -> p (a b)")
        xq7_f = xq7.rearrange("p a b -> p (a b)")

        def act_quant_tm(buf, n, maxcol, rscol):
            begin(V)
            V.tensor_scalar(FB2[:, 0:n], buf[:, 0:n],
                            stats[:, maxcol + 1:maxcol + 2], 0.5,
                            ALU.mult, ALU.add)
            V.drain()
            V.tensor_copy(ari[:, 0:n], FB2[:, 0:n])
            V.drain()
            V.tensor_copy(FB3[:, 0:n], ari[:, 0:n])
            V.drain()
            V.tensor_tensor(FB4[:, 0:n], FB3[:, 0:n], FB2[:, 0:n], ALU.is_gt)
            V.drain()
            V.tensor_sub(FB3[:, 0:n], FB3[:, 0:n], FB4[:, 0:n])
            V.drain()
            V.tensor_scalar(FB3[:, 0:n], FB3[:, 0:n], -128.0, 127.0,
                            ALU.max, ALU.min)
            V.drain()
            V.tensor_scalar(buf[:, 0:n], FB3[:, 0:n],
                            stats[:, rscol:rscol + 1], None, ALU.mult)
            step(V.drain())

        def xq_l1(t0c, tnc, dst_f, W_chunks, maxcol, rscol):
            for s0 in range(0, tnc, 128):
                w = min(128, tnc - s0)
                begin(V)
                V.memset(stats[:, maxcol:maxcol + 1], 0.0)
                step(V.drain())
                for fh in range(W_chunks // 4):
                    begin(T)
                    for i in range(4):
                        c = fh * 4 + i
                        T.transpose(pb[i][0:w, 0:128],
                                    xsf[:, c * TT + t0c + s0: c * TT + t0c + s0 + w],
                                    ident[:, :])
                    step(T.drain())
                    begin(V)
                    for i in range(4):
                        evac_tp(XT[:, i * 128:(i + 1) * 128], i, w)
                    step(V.drain())
                    begin(V)
                    V.tensor_reduce(stats[:, maxcol + 2:maxcol + 3],
                                    XT[:, 0:512], axis=AX.X, op=ALU.max,
                                    apply_absolute_value=True)
                    V.drain()
                    V.tensor_max(stats[:, maxcol:maxcol + 1],
                                 stats[:, maxcol:maxcol + 1],
                                 stats[:, maxcol + 2:maxcol + 3])
                    step(V.drain())
                begin(V)
                V.tensor_scalar(stats[:, maxcol:maxcol + 1],
                                stats[:, maxcol:maxcol + 1], 1e-5, None, ALU.max)
                V.drain()
                V.reciprocal(stats[:, maxcol + 1:maxcol + 2],
                             stats[:, maxcol:maxcol + 1])
                V.drain()
                V.tensor_scalar_mul(stats[:, maxcol + 1:maxcol + 2],
                                    stats[:, maxcol + 1:maxcol + 2], 127.0)
                V.drain()
                V.tensor_scalar_mul(stats[:, rscol:rscol + 1],
                                    stats[:, maxcol:maxcol + 1], 1.0 / 127.0)
                step(V.drain())
                for fh in range(W_chunks // 4):
                    begin(T)
                    for i in range(4):
                        c = fh * 4 + i
                        T.transpose(pb[i][0:w, 0:128],
                                    xsf[:, c * TT + t0c + s0: c * TT + t0c + s0 + w],
                                    ident[:, :])
                    step(T.drain())
                    begin(V)
                    for i in range(4):
                        evac_tp(XT[:, i * 128:(i + 1) * 128], i, w)
                    step(V.drain())
                    act_quant_tm(XT, 512, maxcol, rscol)
                    begin(T)
                    for i in range(4):
                        T.transpose(pb[i][:, 0:128],
                                    XT[:, i * 128:(i + 1) * 128], ident[:, :])
                    step(T.drain())
                    begin(V)
                    for i in range(4):
                        c = fh * 4 + i
                        V.tensor_copy(dst_f[:, c * 322 + s0: c * 322 + s0 + w],
                                      pb[i][0:128, 0:w])
                    step(V.drain())

        if _PHASES >= 7:
            SN7 = 320
            tiles7 = []
            t = 1
            while t < 1 + TC:
                n = min(SN7, 1 + TC - t)
                tiles7.append((t, n))
                t += n
            for (t0, tn) in tiles7:
                hn = tn + 2
                xq_l1(t0 - 1, hn, xq7_f, NCH, 50, 52)
                begin(Sy)
                dsW.issue(0, [(wb[:, :].rearrange("p (a b) -> p a b", b=128),
                               w1T[:, :, 0:128])])
                step(Sy.nop())
                for nch in range(NCH1):
                    nb = nch % 2
                    begin(Sy)
                    dsW.wait(nb)
                    if nch + 1 < NCH1:
                        nbuf = [wb, mst0][1 - nb]
                        dsW.issue(1 - nb,
                                  [(nbuf[:, :].rearrange("p (a b) -> p a b", b=128),
                                    w1T[:, :, (nch + 1) * 128:(nch + 2) * 128])])
                    step(Sy.nop())
                    wcur = [wb, mst0][nb]
                    begin(T)
                    for k in range(NCH):
                        T.matmul(pb[4][:, 0:hn], wcur[:, k * 128:(k + 1) * 128],
                                 xq7_f[:, k * 322:k * 322 + hn],
                                 start=(k == 0), stop=(k == NCH - 1))
                    step(T.drain())
                    begin(V)
                    V.tensor_scalar(FB2[:, 0:tn], pb[4][:, 0:tn],
                                    convw[:, nch, 0:1], None, ALU.mult)
                    V.drain()
                    V.scalar_tensor_tensor(FB2[:, 0:tn], pb[4][:, 1:1 + tn],
                                           convw[:, nch, 1:2], FB2[:, 0:tn],
                                           ALU.mult, ALU.add)
                    V.drain()
                    V.scalar_tensor_tensor(FB2[:, 0:tn], pb[4][:, 2:2 + tn],
                                           convw[:, nch, 2:3], FB2[:, 0:tn],
                                           ALU.mult, ALU.add)
                    step(V.drain())
                    begin(A)
                    A.activation(hb7_f[:, nch * 322:nch * 322 + tn],
                                 FB2[:, 0:tn], AF.Silu,
                                 bias=convb[:, nch:nch + 1])
                    step(A.drain())
                for s0 in range(0, tn, 128):
                    w = min(128, tn - s0)
                    begin(V)
                    V.memset(stats[:, 54:55], 0.0)
                    step(V.drain())
                    for fh in range(NCH1 // 4):
                        begin(T)
                        for i in range(4):
                            c = fh * 4 + i
                            T.transpose(pb[i][0:w, 0:128],
                                        hb7_f[:, c * 322 + s0: c * 322 + s0 + w],
                                        ident[:, :])
                        step(T.drain())
                        begin(V)
                        for i in range(4):
                            evac_tp(XT[:, i * 128:(i + 1) * 128], i, w)
                        step(V.drain())
                        begin(V)
                        V.tensor_reduce(stats[:, 56:57], XT[:, 0:512],
                                        axis=AX.X, op=ALU.max,
                                        apply_absolute_value=True)
                        V.drain()
                        V.tensor_max(stats[:, 54:55], stats[:, 54:55],
                                     stats[:, 56:57])
                        step(V.drain())
                    begin(V)
                    V.tensor_scalar(stats[:, 54:55], stats[:, 54:55], 1e-5,
                                    None, ALU.max)
                    V.drain()
                    V.reciprocal(stats[:, 55:56], stats[:, 54:55])
                    V.drain()
                    V.tensor_scalar_mul(stats[:, 55:56], stats[:, 55:56], 127.0)
                    V.drain()
                    V.tensor_scalar_mul(stats[:, 57:58], stats[:, 54:55],
                                        1.0 / 127.0)
                    step(V.drain())
                    for fh in range(NCH1 // 4):
                        begin(T)
                        for i in range(4):
                            c = fh * 4 + i
                            T.transpose(pb[i][0:w, 0:128],
                                        hb7_f[:, c * 322 + s0: c * 322 + s0 + w],
                                        ident[:, :])
                        step(T.drain())
                        begin(V)
                        for i in range(4):
                            evac_tp(XT[:, i * 128:(i + 1) * 128], i, w)
                        step(V.drain())
                        act_quant_tm(XT, 512, 54, 57)
                        begin(T)
                        for i in range(4):
                            T.transpose(pb[i][:, 0:128],
                                        XT[:, i * 128:(i + 1) * 128], ident[:, :])
                        step(T.drain())
                        begin(V)
                        for i in range(4):
                            c = fh * 4 + i
                            V.tensor_copy(hb7_f[:, c * 322 + s0: c * 322 + s0 + w],
                                          pb[i][0:128, 0:w])
                        step(V.drain())
                w2groups = [(m, kg) for m in range(NCH) for kg in range(0, NCH1, 8)]
                begin(Sy)
                m0_, kg0 = w2groups[0]
                dsW.issue(0, [(wb[:, :].rearrange("p (a b) -> p a b", b=128),
                               w2T[:, kg0:kg0 + 8, m0_ * 128:(m0_ + 1) * 128])])
                step(Sy.nop())
                for gi, (m, kg) in enumerate(w2groups):
                    gb = gi % 2
                    begin(Sy)
                    dsW.wait(gb)
                    if gi + 1 < len(w2groups):
                        m2, kg2 = w2groups[gi + 1]
                        nbuf = [wb, mst0][1 - gb]
                        dsW.issue(1 - gb,
                                  [(nbuf[:, :].rearrange("p (a b) -> p a b", b=128),
                                    w2T[:, kg2:kg2 + 8, m2 * 128:(m2 + 1) * 128])])
                    step(Sy.nop())
                    cbuf = [wb, mst0][gb]
                    begin(T)
                    for i in range(8):
                        T.matmul(pb[m][:, 0:tn],
                                 cbuf[:, i * 128:(i + 1) * 128],
                                 hb7_f[:, (kg + i) * 322:(kg + i) * 322 + tn],
                                 start=(kg + i == 0),
                                 stop=(kg + i == NCH1 - 1))
                    step(T.drain())
                for m in range(NCH):
                    begin(V)
                    step(V.tensor_add(xs[:, m, t0:t0 + tn],
                                      xs[:, m, t0:t0 + tn], pb[m][:, 0:tn]))
                emit_ln(t0, tn)

        begin(Sy)
        dma(y, xs[:, :, 1:1 + TC])
        sy_step()
        begin(Sy)
        step(Sy.nop())
        build_nc.counters = dict(C)
    return nc


def kernel(**inputs):
    cores = []
    x = np.asarray(inputs["L0_attn_x"])
    for c in range(8):
        b, half = c // 2, c % 2
        t0 = half * TC
        xs_ = np.zeros((D, TT), dtype=np.float32)
        xs_[:, 1:1 + TC] = x[b, t0:t0 + TC, :].T
        if half == 0:
            xs_[:, TT - 1] = x[b, t0 + TC, :]
        else:
            xs_[:, 0] = x[b, t0 - 1, :]
        xpre_ = np.ascontiguousarray(x[b, 0:TC, :].T)
        fM = 1.0 if half == 1 else 0.0
        fL = 1.0 if half == 1 else 0.0
        fR = 1.0 if half == 0 else 0.0
        im = {
            "xf": xs_.reshape(NCH, 128, TT).transpose(1, 0, 2).copy(),
            "xpre": xpre_.reshape(NCH, 128, TC).transpose(1, 0, 2).copy(),
            "flags": np.array([[fM, fL, fR, 0.0]], dtype=np.float32),
        }
        for nm in ["L0_attn_Wq", "L0_attn_Wk", "L0_attn_Wv", "L0_attn_Wo",
                   "L0_ffn_w_gate", "L0_ffn_w_up", "L0_ffn_w_down",
                   "L1_attn_Wq", "L1_attn_Wk", "L1_attn_Wv",
                   "L1_ffn_w1", "L1_ffn_w2"]:
            im[nm] = np.ascontiguousarray(np.asarray(inputs[nm], dtype=np.float32))
        im["L0_attn_betas"] = np.asarray(inputs["L0_attn_betas"], np.float32).reshape(H, DV)
        im["L0_attn_memory0"] = np.asarray(inputs["L0_attn_memory0"], np.float32).reshape(H, DK, DV)
        im["L0_attn_z0"] = np.asarray(inputs["L0_attn_z0"], np.float32).reshape(H, DK)
        im["L1_attn_bq"] = np.asarray(inputs["L1_attn_bq"], np.float32).reshape(1, D)
        im["L1_attn_bk"] = np.asarray(inputs["L1_attn_bk"], np.float32).reshape(1, D)
        im["L1_attn_bv"] = np.asarray(inputs["L1_attn_bv"], np.float32).reshape(1, D)
        im["L1_attn_weights"] = np.asarray(inputs["L1_attn_weights"], np.float32)
        im["L1_ffn_conv_w"] = np.asarray(inputs["L1_ffn_conv_w"], np.float32).reshape(DFF1, 3)
        im["L1_ffn_conv_b"] = np.asarray(inputs["L1_ffn_conv_b"], np.float32).reshape(1, DFF1)
        cores.append(im)
    nc = build_nc(taps=_DEBUG_TAPS)
    res = run_bass_kernel_spmd(nc, cores, core_ids=list(range(8)))
    kernel._last = res
    out = np.empty((B, S, D), dtype=np.float32)
    for c in range(8):
        b, half = c // 2, c % 2
        yc = np.asarray(res.results[c]["y"])
        out[b, half * TC:(half + 1) * TC, :] = \
            yc.transpose(1, 0, 2).reshape(D, TC).T
    return out
